# version 33
# baseline (speedup 1.0000x reference)
import contextlib
import numpy as np
import ml_dtypes
import concourse.bass as bass
import concourse.mybir as mybir
from concourse.bass_utils import run_bass_kernel_spmd

F32 = mybir.dt.float32
BF16 = mybir.dt.bfloat16
AF = mybir.ActivationFunctionType
ALU = mybir.AluOpType
AX = mybir.AxisListType

MAGIC = 12582912.0
NEGBIG = -1.0e30
LN_EPS = 1e-5
Q_EPS = 1e-5
SCALE = 8.0

FULL_CFG = dict(T=2048, D=1024, H=16, F=4096, PRELEN=64)



class Op:
    __slots__ = ("eng", "fn", "deps", "dma", "sem_val", "needed", "idx")

    def __init__(self, eng, fn, deps, dma):
        self.eng = eng
        self.fn = fn
        self.deps = deps
        self.dma = dma
        self.sem_val = None
        self.needed = False
        self.idx = None


def _flat(deps):
    out = []

    def rec(d):
        if d is None:
            return
        if isinstance(d, (list, tuple)):
            for e in d:
                rec(e)
        else:
            out.append(d)
    rec(deps)
    return out


class Prog:
    CE = ("tensor", "vector", "scalar", "gpsimd")

    def __init__(self, n_g=8, n_y=8, n_s=16):
        self.ops = []
        self.n_dma = {"gpsimd": n_g, "sync": n_y, "scalar": n_s}
        self.extra = []

    def add(self, eng, fn, deps=()):
        op = Op(eng, fn, _flat(deps) + self.extra, dma=False)
        self.ops.append(op)
        return op

    def dma(self, eng, fn, deps=()):
        op = Op(eng, fn, _flat(deps) + self.extra, dma=True)
        self.ops.append(op)
        return op

    def fence(self):
        last = {}
        dix = {e: 0 for e in self.n_dma}
        for op in self.ops:
            if op.dma:
                last[(op.eng, dix[op.eng] % self.n_dma[op.eng])] = op
                dix[op.eng] += 1
            else:
                last[op.eng] = op
        self.extra = list(last.values())

    def emit(self, nc, block, stack):
        for op in self.ops:
            for d in op.deps:
                d.needed = True
        counters = {}
        dma_idx = {"gpsimd": 0, "sync": 0, "scalar": 0}
        for op in self.ops:
            if op.dma:
                i = dma_idx[op.eng]
                dma_idx[op.eng] = i + 1
                op.idx = i
                op.sem_val = 16 * (i // self.n_dma[op.eng] + 1)
            elif op.needed:
                c = counters.get(op.eng, 0) + 1
                counters[op.eng] = c
                op.sem_val = c

        sems = {e: stack.enter_context(nc.semaphore("sem_" + e)) for e in self.CE}
        dsems = {e: [stack.enter_context(nc.semaphore(f"sem_{e}{i}")) for i in range(n)]
                 for e, n in self.n_dma.items()}
        ops = self.ops
        n_dma = self.n_dma

        def runner(engname):
            def body(eng):
                hw = {}

                def wait(key, handle, val):
                    if hw.get(key, 0) < val:
                        eng.wait_ge(handle, val)
                        hw[key] = val

                for op in ops:
                    if op.eng != engname:
                        continue
                    for d in op.deps:
                        if d.dma:
                            slot = d.idx % n_dma[d.eng]
                            wait((d.eng, slot), dsems[d.eng][slot], d.sem_val)
                        else:
                            if d.eng == engname and engname == "tensor":
                                continue
                            if d.sem_val is None:
                                continue
                            wait((d.eng,), sems[d.eng], d.sem_val)
                    if op.dma:
                        slot = op.idx % n_dma[op.eng]
                        if op.sem_val > 16:
                            wait((op.eng, slot), dsems[op.eng][slot], op.sem_val - 16)
                        op.fn(eng).then_inc(dsems[op.eng][slot], 16)
                    elif op.fn is not None:
                        inst = op.fn(eng)
                        if op.needed:
                            inst.then_inc(sems[op.eng], 1)
            return body

        block.tensor(runner("tensor"))
        block.vector(runner("vector"))
        block.scalar(runner("scalar"))
        block.gpsimd(runner("gpsimd"))
        block.sync(runner("sync"))



def make_consts(cfg):
    P = 128
    pre = cfg["PRELEN"]
    t = np.arange(P)[:, None]
    kk = np.arange(P)[None, :]
    maskA = np.where(kk <= t, 0.0, NEGBIG)
    maskC = np.where((kk <= t) | ((t < pre) & (kk < pre)), 0.0, NEGBIG)
    ind = np.zeros((2, P), np.float32)
    ind[0, 0:64] = 1.0
    ind[1, 64:128] = 1.0
    dind = np.zeros((P, 4), np.float32)
    dind[:, 0] = 1.0
    dind[:, 3] = 1.0
    return {
        "c_maskA": maskA.astype(ml_dtypes.bfloat16),
        "c_maskC": maskC.astype(ml_dtypes.bfloat16),
        "c_maskAT": maskA.T.copy().astype(ml_dtypes.bfloat16),
        "c_maskCT": maskC.T.copy().astype(ml_dtypes.bfloat16),
        "c_dind": dind.astype(ml_dtypes.bfloat16),
        "c_nind": dind[0:2, :].copy().astype(ml_dtypes.bfloat16),
        "c_ident": np.eye(P).astype(ml_dtypes.bfloat16),
        "c_identf": np.eye(P).astype(np.float32),
        "c_ones_row_b": np.ones((1, P), ml_dtypes.bfloat16),
        "c_ones_row_f": np.ones((1, P), np.float32),
        "c_ones_col_b": np.ones((P, 1), ml_dtypes.bfloat16),
        "c_ones_col_f": np.ones((P, 1), np.float32),
        "c_indic": ind,
    }



def build_program(cfg):
    T, D, H, F = cfg["T"], cfg["D"], cfg["H"], cfg["F"]
    NT, ND, NF = T // 128, D // 128, F // 128
    assert H * 64 == D and NT % 2 == 0
    PW = min(F, 1024)
    NW2S = PW // 128
    NQ2 = F // PW
    W2ST = ND * NQ2
    NH = (D + 511) // 512
    BQ = 4 if NT % 4 == 0 else 2
    BW = BQ * 128
    AOFF = []
    _aw = 0
    for ib in range(BQ):
        AOFF.append(_aw)
        _aw += (NT - BQ + ib + 1) * 128 + 4
    AW = _aw

    nc = bass.Bass("TRN2", target_bir_lowering=False)

    dx = nc.dram_tensor("x", [T, D], F32, kind="ExternalInput").ap()
    dgamma = nc.dram_tensor("gamma", [D], F32, kind="ExternalInput").ap()
    dWq = nc.dram_tensor("Wq", [D, D], F32, kind="ExternalInput").ap()
    dWkv = nc.dram_tensor("Wkv", [D, 128], F32, kind="ExternalInput").ap()
    dqs = nc.dram_tensor("q_scale", [64], F32, kind="ExternalInput").ap()
    dks = nc.dram_tensor("k_scale", [64], F32, kind="ExternalInput").ap()
    dnkv = nc.dram_tensor("null_kv", [2, 2, 64], F32, kind="ExternalInput").ap()
    dWout = nc.dram_tensor("Wout", [D, D], F32, kind="ExternalInput").ap()
    dW1 = nc.dram_tensor("W1", [F, D], F32, kind="ExternalInput").ap()
    db1 = nc.dram_tensor("b1", [F], F32, kind="ExternalInput").ap()
    dW2 = nc.dram_tensor("W2", [D, F], F32, kind="ExternalInput").ap()
    db2 = nc.dram_tensor("b2", [D], F32, kind="ExternalInput").ap()
    dY = nc.dram_tensor("y", [T, D], F32, kind="ExternalOutput").ap()
    dw2qt = nc.dram_tensor("w2qt", [F, D], BF16).ap()
    dw1qt = nc.dram_tensor("w1qt", [D, F], BF16).ap()
    dxnt = nc.dram_tensor("xnt", [ND * 128, T], BF16).ap()
    dinv = nc.dram_tensor("dinvt", [2, ND, T], F32).ap()
    dcon = {}
    for n, a in make_consts(cfg).items():
        dt = BF16 if a.dtype == ml_dtypes.bfloat16 else F32
        dcon[n] = nc.dram_tensor(n, list(a.shape), dt, kind="ExternalInput").ap()

    xr = dx.rearrange("(i p) d -> i p d", p=128)
    yr = dY.rearrange("(i p) d -> i p d", p=128)
    Wqr = dWq.rearrange("(c p) e -> p c e", p=128)
    Wkvr = dWkv.rearrange("(c p) e -> p c e", p=128)
    Woutr = dWout.rearrange("(c p) e -> p c e", p=128)
    W1r = dW1.rearrange("(c p) d -> c p d", p=128)
    W2r = dW2.rearrange("(c p) f -> c p f", p=128)
    w2qt_rd = dw2qt.rearrange("(c p) d -> p c d", p=128)
    w2qt_wr = dw2qt.rearrange("(q s p) d -> p q s d", p=128, s=NW2S)
    w1qt_rw = dw1qt.rearrange("(c p) f -> p c f", p=128)
    xnt_rw = dxnt.rearrange("(c p) t -> p c t", p=128)
    gam_r = dgamma.rearrange("(c p) -> p c", p=128)
    b1_r = db1.rearrange("(c p) -> p c", p=128)

    prog = Prog()
    stack = contextlib.ExitStack()
    S = stack.enter_context

    def sb(name, shape, dtype):
        return S(nc.sbuf_tensor(name, shape, dtype))

    psim = S(nc.psum_tensor("psim", [128, 2048], F32))
    ptrb = S(nc.psum_tensor("ptrb", [128, 1024], BF16))
    pav = S(nc.psum_tensor("pav", [128, 512], F32))
    pwork = S(nc.psum_tensor("pwork", [128, 1024], F32))

    outunT = sb("outunT", [128, ND, T], BF16)
    qtmp = sb("qtmp", [128, max(D, PW)], F32)
    maskAT = sb("maskAT", [128, 128], BF16)
    maskCT = sb("maskCT", [128, 128], BF16)
    dind = sb("dind", [128, 4], BF16)
    nind = sb("nind", [2, 4], BF16)
    ident = sb("ident", [128, 128], BF16)
    identf = sb("identf", [128, 128], F32)
    ones_row_b = sb("ones_row_b", [1, 128], BF16)
    ones_row_f = sb("ones_row_f", [1, 128], F32)
    ones_col_b = sb("ones_col_b", [128, 1], BF16)
    ones_col_f = sb("ones_col_f", [128, 1], F32)
    indic = sb("indic", [2, 128], F32)
    b1_sb = sb("b1_sb", [128, NF], F32)
    b2_sb = sb("b2_sb", [1, D], BF16)
    gam_sb = sb("gam_sb", [128, ND], F32)
    s8col = sb("s8col", [128, 1], F32)
    qs_st = sb("qs_st", [128, 1], F32)
    ks_st = sb("ks_st", [128, 1], F32)
    mu2b = sb("mu2b", [128, 2], BF16)
    mu2row = sb("mu2row", [1, 2, 128], BF16)
    negc = sb("negc", [1, D], BF16)
    stat = sb("stat", [128, 2, 8], F32)
    pabs = sb("pabs", [128, NF + W2ST], F32)
    gsc = sb("gsc", [1, 8], F32)
    ginv = sb("ginv", [128, 2], F32)
    gcol = sb("gcol", [128, 2], F32)
    epsc = sb("epsc", [128, 1], F32)
    magc = sb("magc", [128, 1], F32)

    ctxA = contextlib.ExitStack()
    SA = ctxA.enter_context

    def sba(name, shape, dtype):
        return SA(nc.sbuf_tensor(name, shape, dtype))

    xbufA = sba("xbufA", [128, 2, D], F32)
    xnT = sba("xnT", [128, 2, ND, 128], BF16)
    xnbf = sba("xnbf", [128, D], BF16)
    qraw = sba("qraw", [128, D], BF16)
    qss = sba("qss", [128, H], F32)
    qlog = sba("qlog", [128, H], F32)
    invq = sba("invq", [128, H], F32)
    qhat = sba("qhat", [128, D], BF16)
    qhatT = sba("qhatT", [128, 3, ND, BW], BF16)
    ATS = 3
    attnT = sba("attnT", [128, ATS, NT, BW], BF16)
    attnTn = sba("attnTn", [2, ATS, BW], BF16)
    negC = sba("negC", [128, 1], F32)
    pairden = sba("pairden", [2, 2, BW], F32)
    Wq_sb = sba("Wq_sb", [128, ND, D], BF16)
    Wkv_sb = sba("Wkv_sb", [128, ND, 128], BF16)
    khsT = sba("khsT", [128, T], BF16)
    nkhsT = sba("nkhsT", [128, 2], BF16)
    v_sb = sba("v_sb", [128, NT, 64], BF16)
    vnull = sba("vnull", [2, 64], BF16)
    kvT_sb = sba("kvT_sb", [128, 128], BF16)
    ktok = sba("ktok", [128, 128], BF16)
    ktr2 = sba("ktr2", [128, 128], BF16)
    kss = sba("kss", [128, 2], F32)
    krs = sba("krs", [128, 1], F32)
    nst = sba("nst", [2, 64], F32)
    nss = sba("nss", [2, 2], F32)
    nkh = sba("nkh", [2, 64], BF16)
    wstage = sba("wstage", [128, 2, max(D, PW)], F32)
    qsc = sba("qsc", [128, max(D, PW)], F32)
    wqbf = sba("wqbf", [128, 1, max(D, PW)], BF16)
    wtout = sba("wtout", [128, 8, 128], BF16)

    for name, ap_sb in [("c_maskAT", maskAT), ("c_maskCT", maskCT),
                        ("c_dind", dind), ("c_nind", nind), ("c_ident", ident),
                        ("c_identf", identf), ("c_ones_row_b", ones_row_b),
                        ("c_ones_row_f", ones_row_f), ("c_ones_col_b", ones_col_b),
                        ("c_ones_col_f", ones_col_f), ("c_indic", indic)]:
        prog.dma("gpsimd", lambda g, a=ap_sb, d=dcon[name]: g.dma_start(out=a[:], in_=d))
    t_consts = prog.ops[-1]
    t_b1 = prog.dma("gpsimd", lambda g: g.dma_start(
        out=b1_sb[:], in_=b1_r, allow_slow_non_contiguous=True))
    t_b2 = prog.dma("gpsimd", lambda g: g.dma_start(
        out=b2_sb[:], in_=db2.rearrange("(o d) -> o d", o=1)))
    t_gam = prog.dma("gpsimd", lambda g: g.dma_start(
        out=gam_sb[:], in_=gam_r, allow_slow_non_contiguous=True))
    t_qs = [prog.dma("gpsimd", lambda g, o=o: g.dma_start(
        out=qs_st[o * 64:(o + 1) * 64, :], in_=dqs.rearrange("(k o) -> k o", o=1)))
        for o in range(2)]
    t_ks = [prog.dma("gpsimd", lambda g, o=o: g.dma_start(
        out=ks_st[o * 64:(o + 1) * 64, :], in_=dks.rearrange("(k o) -> k o", o=1)))
        for o in range(2)]
    t_s8 = prog.add("vector", lambda v: v.scalar_tensor_tensor(
        s8col[:], qs_st[:], SCALE, ks_st[:], ALU.mult, ALU.mult), [t_qs, t_ks])
    t_eps = prog.add("vector", lambda v: v.memset(epsc[:], LN_EPS))
    t_mag = prog.add("vector", lambda v: v.memset(magc[:], -MAGIC))

    t_wst = [None, None]
    for c in range(ND):
        tl = prog.dma("scalar", lambda g, c=c: g.dma_start(
            out=wstage[:, c % 2, 0:128], in_=Wkvr[:, c, :]), [t_wst[c % 2]])
        t_wst[c % 2] = prog.add("vector", lambda v, c=c: v.tensor_copy(
            Wkv_sb[:, c, :], wstage[:, c % 2, 0:128]), [tl])
    t_wkv = [t_wst[0], t_wst[1]]
    t_wq_chunks = []
    for c in range(ND):
        tl = prog.dma("scalar", lambda g, c=c: g.dma_start(
            out=wstage[:, c % 2, 0:D], in_=Wqr[:, c, :]), [t_wst[c % 2]])
        t_wst[c % 2] = prog.add("vector", lambda v, c=c: v.tensor_scalar(
            Wq_sb[:, c, :], wstage[:, c % 2, 0:D], gam_sb[:, c:c + 1], None,
            ALU.mult), [tl, t_gam])
        t_wq_chunks.append(t_wst[c % 2])
    t_wq = list(t_wq_chunks)

    t_ptr_free = t_consts
    t_pav_free = None
    t_pwork_free = None
    t_qtmp_free = None

    t_negc_parts = []
    for hh in range(NH):
        w = min(512, D - hh * 512)
        t_mm = None
        for c in range(ND):
            t_mm = prog.add("tensor", lambda t, c=c, hh=hh, w=w: t.matmul(
                pwork[0:1, hh * 512:hh * 512 + w], ones_col_b[:],
                Wq_sb[:, c, hh * 512:hh * 512 + w],
                start=(c == 0), stop=(c == ND - 1)),
                [t_wq, t_mm, t_consts])
        t_negc_parts.append(t_mm)
    t_negc = prog.add("vector", lambda v: v.tensor_scalar(
        negc[:], pwork[0:1, 0:D], -1.0 / D, None, ALU.mult), t_negc_parts)

    t_nk_l = prog.dma("gpsimd", lambda g: g.dma_start(out=nst[:], in_=dnkv[0]))
    t_nv_l = prog.dma("gpsimd", lambda g: g.dma_start(out=qtmp[0:2, 64:128], in_=dnkv[1]))
    t_vnull = prog.add("vector", lambda v: v.tensor_copy(vnull[:], qtmp[0:2, 64:128]),
                       [t_nv_l])
    t_qtmp_free = [t_qtmp_free, t_vnull]
    t_nss = prog.add("vector", lambda v: v.scalar_tensor_tensor(
        qtmp[0:2, 0:64], nst[:], 1.0, nst[:], ALU.mult, ALU.mult,
        accum_out=nss[:, 0:1]), [t_nk_l, t_qtmp_free])
    t_qtmp_free = t_nss
    t_nlog = prog.add("scalar", lambda s: s.activation(nss[:, 1:2], nss[:, 0:1],
                                                       AF.Ln), [t_nss])
    t_nrs = prog.add("scalar", lambda s: s.activation(nss[:, 0:1], nss[:, 1:2],
                                                      AF.Exp, scale=-0.5), [t_nlog])
    t_nkh = prog.add("vector", lambda v: v.tensor_scalar(
        nkh[:], nst[:], nss[:, 0:1], None, ALU.mult), [t_nrs])
    t_nkt = prog.add("tensor", lambda t: t.transpose(
        ptrb[0:64, 0:2], nkh[:], ident[0:2, 0:2]), [t_nkh, t_ptr_free])
    t_nkt2 = prog.add("tensor", lambda t: t.transpose(
        ptrb[64:128, 0:2], nkh[:], ident[0:2, 0:2], tile_position=(0, 64)),
        [t_nkh, t_ptr_free])
    t_ptr_free = prog.add("vector", lambda v: v.tensor_scalar(
        nkhsT[:], ptrb[:, 0:2], s8col[:], None, ALU.mult),
        [t_nkt, t_nkt2, t_s8])
    t_nkhsT = t_ptr_free

    t_s8t = prog.add("tensor", lambda t: t.transpose(
        pav[0:1, 384:512], s8col[:], identf[:]), [t_s8, t_pav_free])
    t_cb = prog.add("vector", lambda v: v.reduce_max(
        gsc[:, 6:7], pav[0:1, 384:512], axis=AX.X,
        apply_absolute_value=True), [t_s8t])
    t_cbb = prog.add("tensor", lambda t: t.matmul(
        pav[0:128, 440:441], ones_row_f[:], gsc[:, 6:7], start=True,
        stop=True), [t_cb])
    t_negC = prog.add("vector", lambda v: v.tensor_scalar(
        negC[:], pav[:, 440:441], -1.0, None, ALU.mult), [t_cbb])
    t_pav_free = t_negC

    x_tok = [None] * NT

    def xload_A(i, dep):
        x_tok[i] = prog.dma("gpsimd", lambda g, i=i: g.dma_start(
            out=xbufA[:, i % 2, :], in_=xr[i]), [dep])

    xload_A(0, None)
    xload_A(1, None)

    qstate = dict(wst=t_wst, ptr_free=None, pav_free=None, qtmp_free=None,
                  pabs_done=[], g1=None, wto=[None], wqbf_free=None,
                  qsc_free=None, w1qt_done=[], w2qt_done=[])

    def quant_load(kind, idx, st):
        def go():
            slot = quant_load.slot
            quant_load.slot ^= 1
            st["slot"] = slot
            if kind == "w1":
                src_, wdt = W1r[idx], D
            else:
                dd, q = divmod(idx, NQ2)
                src_, wdt = W2r[dd][:, q * PW:(q + 1) * PW], PW
            st["ld"] = prog.dma("scalar", lambda g: g.dma_start(
                out=wstage[:, slot, 0:wdt], in_=src_), [qstate["wst"][slot]])
        return go
    quant_load.slot = 0

    def quant_abs(kind, idx, st):
        def go():
            slot = st["slot"]
            if kind == "w1":
                wdt, col = D, idx
            else:
                wdt, col = PW, NF + idx
            tr = prog.add("vector", lambda v: v.reduce_sum(
                pabs[:, col:col + 1], wstage[:, slot, 0:wdt], axis=AX.X,
                apply_absolute_value=True), [st["ld"]])
            qstate["wst"][slot] = tr
            qstate["pabs_done"].append(tr)
        return go

    def quant_gamma():
        def go():
            nonlocal ring_ctr
            deps = list(qstate["pabs_done"])
            t_c1 = prog.add("vector", lambda v: v.reduce_sum(
                gcol[:, 0:1], pabs[:, 0:NF], axis=AX.X), deps)
            t_c2 = prog.add("vector", lambda v: v.reduce_sum(
                gcol[:, 1:2], pabs[:, NF:NF + W2ST], axis=AX.X), deps)
            s1 = ring_ctr % NRS
            ring_ctr += 1
            t_tm = prog.add("tensor", lambda t, s1=s1: t.matmul(
                ring_ap2(s1, 0, 1, 0, 2), ones_col_f[:], gcol[:], start=True,
                stop=True), [t_c1, t_c2, ring_tok[s1]])
            t_g = prog.add("vector", lambda v, s1=s1: v.tensor_scalar(
                gsc[:, 0:2], ring_ap2(s1, 0, 1, 0, 2), 1.0 / (F * D), Q_EPS,
                ALU.mult, ALU.add), [t_tm])
            ring_tok[s1] = t_g
            t_r = prog.add("vector", lambda v: v.reciprocal(gsc[:, 2:4],
                                                            gsc[:, 0:2]), [t_g])
            s2 = ring_ctr % NRS
            ring_ctr += 1
            t_bc = prog.add("tensor", lambda t, s2=s2: t.matmul(
                ring_ap2(s2, 0, 128, 0, 2), ones_row_f[:], gsc[:, 2:4],
                start=True, stop=True), [t_r, t_g, ring_tok[s2]])
            t_gv = prog.add("vector", lambda v, s2=s2: v.tensor_copy(
                ginv[:], ring_ap2(s2, 0, 128, 0, 2)), [t_bc])
            ring_tok[s2] = t_gv
            qstate["g1"] = t_gv
        return go

    def quant_pass2c(kind, idx, st):
        def go():
            slot = st["slot"]
            if kind == "w1":
                wdt, gc = D, 0
            else:
                dd, q = divmod(idx, NQ2)
                wdt, gc = PW, 1
            t1 = prog.add("vector", lambda v: v.tensor_scalar(
                qsc[:, 0:wdt], wstage[:, slot, 0:wdt], ginv[:, gc:gc + 1],
                MAGIC, ALU.mult, ALU.add), [st["ld"], qstate["g1"],
                                            qstate["qsc_free"]])
            t2 = prog.add("scalar", lambda s: s.add(
                wstage[:, slot, 0:wdt], qsc[:, 0:wdt], magc[:, 0:1]),
                [t1, t_mag])
            qstate["qsc_free"] = t2
            t3 = prog.add("vector", lambda v: v.tensor_scalar(
                wqbf[:, 0, 0:wdt], wstage[:, slot, 0:wdt], 1.0, -1.0,
                ALU.min, ALU.max), [t2, qstate["wqbf_free"]])
            qstate["wst"][slot] = t3
            nsub = wdt // 128
            tdt = prog.dma("sync", lambda y, nsub=nsub, wdt=wdt: y.dma_start_transpose(
                wtout[:, 0:nsub, :], wqbf[:, 0, 0:wdt]), [t3, qstate["wto"][0]])
            qstate["wqbf_free"] = tdt
            if kind == "w1":
                dst = w1qt_rw[:, :, idx * 128:(idx + 1) * 128]
            else:
                dd, q = divmod(idx, NQ2)
                dst = w2qt_wr[:, q, :, dd * 128:(dd + 1) * 128]
            tst = prog.dma("gpsimd", lambda g, dst=dst, nsub=nsub:
                           g.dma_start(out=dst, in_=wtout[:, 0:nsub, :]), [tdt])
            qstate["wto"][0] = tst
            (qstate["w1qt_done"] if kind == "w1" else
             qstate["w2qt_done"]).append(tst)
        return go

    _stages = []
    for kind, n in (("w1", NF), ("w2", W2ST)):
        for i in range(n):
            st = {}
            _stages.append((quant_load(kind, i, st), quant_abs(kind, i, st)))
    _stages.append((None, quant_gamma()))
    for kind, n in (("w1", NF), ("w2", W2ST)):
        for i in range(n):
            st = {}
            _stages.append((quant_load(kind, i, st), quant_pass2c(kind, i, st)))

    NQS = len(_stages)

    def _unit(k):
        def go():
            if k - 2 >= 0:
                _stages[k - 2][1]()
            if k < NQS and _stages[k][0] is not None:
                _stages[k][0]()
        return go
    quant_ops = [_unit(k) for k in range(NQS + 2)]
    qi = 0

    def run_quant(n):
        nonlocal qi, t_ptr_free, t_pav_free, t_qtmp_free
        for _ in range(n):
            if qi < len(quant_ops):
                qstate["ptr_free"] = t_ptr_free
                qstate["pav_free"] = t_pav_free
                qstate["qtmp_free"] = t_qtmp_free
                quant_ops[qi]()
                t_ptr_free = qstate["ptr_free"]
                t_pav_free = qstate["pav_free"]
                t_qtmp_free = qstate["qtmp_free"]
                qi += 1

    t_xnT = [None] * NT
    t_xnst = [None] * NT
    t_kv_done = [None] * NT
    t_qT = [None] * NT
    t_outun = [[None] * ND for _ in range(NT)]

    stat_free = [None, None]
    xnT_free = [None, None]
    mu2_free = [None, None]
    mu2row_tok = [None, None]
    xnbf_free = None
    qchain_free = None
    kchain_free = None
    blk_tail = {}

    def stage0_steps(i):
        par = i % 2
        xb = xbufA[:, par, :]
        sl = stat[:, par, :]
        st_ = {}

        def s_stats():
            nonlocal t_qtmp_free
            st_["mu"] = prog.add("vector", lambda v: v.reduce_sum(
                sl[:, 0:1], xb, axis=AX.X), [x_tok[i], stat_free[par]])
            t_sq = prog.add("vector", lambda v: v.scalar_tensor_tensor(
                qtmp[:, 0:D], xb, 1.0, xb, ALU.mult, ALU.mult,
                accum_out=sl[:, 1:2]), [x_tok[i], stat_free[par], t_qtmp_free])
            t_qtmp_free = t_sq
            t_var = prog.add("vector", lambda v: v.scalar_tensor_tensor(
                sl[:, 2:3], sl[:, 0:1], 1.0 / D, sl[:, 0:1], ALU.mult, ALU.mult),
                [st_["mu"]])
            t_var2 = prog.add("vector", lambda v: v.scalar_tensor_tensor(
                sl[:, 3:4], sl[:, 1:2], 1.0, sl[:, 2:3], ALU.bypass,
                ALU.subtract), [t_sq, t_var])
            st_["var2"] = t_var2

        def s_ln():
            t_ln = prog.add("scalar", lambda s: s.activation(
                sl[:, 4:5], sl[:, 3:4], AF.Ln, scale=1.0 / D,
                bias=epsc[:, 0:1]), [st_["var2"], t_eps])
            t_rsig = prog.add("scalar", lambda s: s.activation(
                sl[:, 3:4], sl[:, 4:5], AF.Exp, scale=-0.5), [t_ln])
            st_["nmr"] = prog.add("vector", lambda v: v.scalar_tensor_tensor(
                sl[:, 5:6], sl[:, 0:1], -1.0 / D, sl[:, 3:4], ALU.mult,
                ALU.mult), [t_rsig, st_["mu"]])

        def s_xnf():
            nonlocal xnbf_free
            st_["xnf"] = prog.add("vector", lambda v: v.tensor_scalar(
                xnbf[:], xb, sl[:, 3:4], sl[:, 5:6], ALU.mult, ALU.add),
                [st_["nmr"], xnbf_free])
            st_["mu2"] = prog.add("vector", lambda v: v.reduce_sum(
                sl[:, 6:7], xnbf[:], axis=AX.X), [st_["xnf"]])
            st_["xnbf"] = st_["xnf"]
            stat_free[par] = [st_["xnf"], st_["mu2"]]

        def s_mu2xn():
            nonlocal t_ptr_free, xnbf_free
            t_mu2b = prog.add("vector", lambda v: v.tensor_copy(
                mu2b[:, par:par + 1], sl[:, 6:7]), [st_["mu2"], mu2_free[par]])
            t_mu2t = prog.add("tensor", lambda t: t.transpose(
                ptrb[0:1, 768:896], mu2b[:, par:par + 1], ident[:]),
                [t_mu2b, t_ptr_free])
            st_["mu2r"] = prog.add("vector", lambda v: v.tensor_copy(
                mu2row[:, par, :], ptrb[0:1, 768:896]), [t_mu2t])
            mu2row_tok[par] = st_["mu2r"]
            t_ptr_free = st_["mu2r"]
            tdt = prog.dma("sync", lambda y: y.dma_start_transpose(
                xnT[:, par, :, :], xnbf[:, 0:D]),
                [st_["xnbf"], xnT_free[par]])
            t_xnT[i] = tdt
            tst = prog.dma("gpsimd", lambda g: g.dma_start(
                out=xnt_rw[:, :, i * 128:(i + 1) * 128], in_=xnT[:, par, :, :]),
                [tdt])
            st_["xnst"] = tst
            t_xnst[i] = tst
            xnbf_free = [tdt, st_["mu2"]]
            if i + 2 < NT:
                xload_A(i + 2, [st_["xnf"], t_qtmp_free])

        def s_kv():
            nonlocal ring_ctr, t_ptr_free, kchain_free, t_qtmp_free
            slot = ring_ctr % NRS
            ring_ctr += 1
            t_kvmm = None
            for c in range(ND):
                t_kvmm = prog.add("tensor", lambda t, c=c, slot=slot, par=par:
                                  t.matmul(
                    ring_ap2(slot, 0, 128, 0, 128), Wkv_sb[:, c, :],
                    xnT[:, par, c, :],
                    start=(c == 0), stop=(c == ND - 1)),
                    [t_wkv, t_xnT[i], ring_tok[slot], t_kvmm])
            t_kvev = prog.add("vector", lambda v, slot=slot: v.tensor_copy(
                kvT_sb[:], ring_ap2(slot, 0, 128, 0, 128)), [t_kvmm, kchain_free])
            ring_tok[slot] = t_kvev
            t_ktok = prog.dma("sync", lambda y: y.dma_start_transpose(
                ktok[:], kvT_sb[:]), [t_kvev, kchain_free])
            t_kss = prog.add("vector", lambda v: v.scalar_tensor_tensor(
                qtmp[:, 0:64], ktok[:, 0:64], 1.0, ktok[:, 0:64], ALU.mult,
                ALU.mult, accum_out=kss[:, 0:1]), [t_ktok, t_vnull, t_qtmp_free])
            t_qtmp_free = t_kss
            st_["kss"] = t_kss
            st_["ktok"] = t_ktok

        def s_kv2():
            nonlocal kchain_free
            t_kss = st_["kss"]
            t_ktok = st_["ktok"]
            t_klog = prog.add("scalar", lambda s: s.activation(
                kss[:, 1:2], kss[:, 0:1], AF.Ln), [t_kss])
            t_krs = prog.add("scalar", lambda s: s.activation(
                krs[:, 0:1], kss[:, 1:2], AF.Exp, scale=-0.5), [t_klog])
            t_khat = prog.add("vector", lambda v: v.tensor_scalar(
                ktok[:, 0:64], ktok[:, 0:64], krs[:, 0:1], None, ALU.mult),
                [t_krs])
            t_vcp = prog.add("vector", lambda v: v.tensor_copy(
                v_sb[:, i, :], ktok[:, 64:128]), [t_ktok])
            t_kdup = prog.add("vector", lambda v: v.tensor_copy(
                ktok[:, 64:128], ktok[:, 0:64]), [t_khat, t_vcp])
            t_kt2 = prog.dma("sync", lambda y: y.dma_start_transpose(
                ktr2[:], ktok[:]), [t_kdup, kchain_free])
            t_khsT = prog.add("vector", lambda v: v.tensor_scalar(
                khsT[:, i * 128:(i + 1) * 128], ktr2[:], s8col[:], None,
                ALU.mult), [t_kt2, t_s8])
            kchain_free = [t_khat, t_vcp, t_khsT, t_kss]
            t_kv_done[i] = [t_khsT, t_vcp, t_nkhsT]

        def s_qproj():
            nonlocal ring_ctr
            _ = qchain_free
            st_["qslots"] = []
            st_["qmm"] = []
            for hh in range(NH):
                w = min(512, D - hh * 512)
                slot = ring_ctr % NRS
                ring_ctr += 1
                st_["qslots"].append((slot, hh, w))
                t_qmm = None
                for c in range(ND):
                    t_qmm = prog.add("tensor", lambda t, c=c, hh=hh, w=w, slot=slot,
                                     par=par: t.matmul(
                        ring_ap2(slot, 0, 128, 0, w),
                        xnT[:, par, c, :],
                        Wq_sb[:, c, hh * 512:hh * 512 + w],
                        start=(c == 0), stop=False),
                        [t_wq, t_xnT[i], ring_tok[slot], t_qmm])
                t_qmm = prog.add("tensor", lambda t, hh=hh, w=w, slot=slot: t.matmul(
                    ring_ap2(slot, 0, 128, 0, w), mu2row[:, par, :],
                    negc[:, hh * 512:hh * 512 + w], start=False, stop=True),
                    [t_negc, st_["mu2r"], t_qmm])
                st_["qmm"].append(t_qmm)
                tq = prog.add("vector", lambda v, slot=slot, hh=hh, w=w: v.tensor_copy(
                    qraw[:, hh * 512:hh * 512 + w], ring_ap2(slot, 0, 128, 0, w)),
                    [t_qmm, qchain_free])
                ring_tok[slot] = tq
                st_.setdefault("qraw", []).append(tq)
            mu2_free[par] = list(st_["qmm"])
            xnT_free[par] = [st_["qmm"], st_["xnst"]]

        def s_qss():
            nonlocal t_qtmp_free
            t_qraw = st_["qraw"]
            t_qssl = []
            for h in range(H):
                tq = prog.add("vector", lambda v, h=h: v.scalar_tensor_tensor(
                    qtmp[:, h * 64:(h + 1) * 64], qraw[:, h * 64:(h + 1) * 64],
                    1.0, qraw[:, h * 64:(h + 1) * 64], ALU.mult, ALU.mult,
                    accum_out=qss[:, h:h + 1]), [t_qraw, t_qtmp_free])
                t_qssl.append(tq)
            t_qtmp_free = t_qssl
            st_["qssl"] = t_qssl

        def s_qinv():
            t_qlog = prog.add("scalar", lambda s: s.activation(
                qlog[:], qss[:], AF.Ln), st_["qssl"])
            t_invq = prog.add("scalar", lambda s: s.activation(
                invq[:], qlog[:], AF.Exp, scale=-0.5), [t_qlog])
            st_["qhat"] = []
            for h in range(H):
                st_["qhat"].append(prog.add("vector", lambda v, h=h: v.tensor_scalar(
                    qhat[:, h * 64:(h + 1) * 64], qraw[:, h * 64:(h + 1) * 64],
                    invq[:, h:h + 1], None, ALU.mult), [t_invq]))

        def s_qT():
            nonlocal qchain_free
            bp = (i // BQ) % 3
            ic = i % BQ
            tdt = prog.dma("sync", lambda y, bp=bp, ic=ic: y.dma_start_transpose(
                qhatT[:, bp, :, ic * 128:(ic + 1) * 128], qhat[:, 0:D]),
                st_["qhat"] + [qchain_free, blk_tail.get(i // BQ - 3)])
            t_qT[i] = tdt
            qchain_free = [tdt] + st_["qhat"]

        return [s_stats, s_ln, s_xnf, s_mu2xn, s_kv, s_kv2, s_qproj,
                s_qss, s_qinv, s_qT]

    NRS = 5
    ring_tok = [None] * NRS
    ring_tok[4] = t_negc
    ring_ctr = 0

    def ring_ap(s, w=512):
        if s < 4:
            return psim[:, s * 512:s * 512 + w]
        return pwork[:, (s - 4) * 512:(s - 4) * 512 + w]

    def ring_ap2(s, p0, p1, c0, w):
        if s < 4:
            return psim[p0:p1, s * 512 + c0:s * 512 + c0 + w]
        return pwork[p0:p1, (s - 4) * 512 + c0:(s - 4) * 512 + c0 + w]

    attnT_free = [None] * ATS
    attnTn_free = [None] * ATS
    den_chain = [t_negc]
    pairden_free = None
    t_dinvst = {}

    def head_meta(B4, h):
        return dict(B4=B4, h=h, par=h % 2, ts=(B4 * H + h) % ATS,
                    po=(h % 2) * 64,
                    cc=h // 2, bp=B4 % 3, base=B4 * BQ, NC=B4 * BQ + BQ,
                    t_exps=[], t_nes=[], t_av=None, t_dn=None)

    def emit_score_chunk(hd, c):
        nonlocal ring_ctr
        B4, ts_, cc, po, bp, base = (hd["B4"], hd["ts"], hd["cc"], hd["po"],
                                     hd["bp"], hd["base"])
        j0 = max(0, c - base)
        w = BW - j0 * 128
        slot = ring_ctr % NRS
        ring_ctr += 1
        diag = c >= base
        t = prog.add("tensor", lambda tt, c=c, j0=j0, w=w, slot=slot, bp=bp,
                     cc=cc, po=po: tt.matmul(
            ring_ap2(slot, 0, 128, 0, w),
            khsT[po:po + 64, c * 128:(c + 1) * 128],
            qhatT[po:po + 64, bp, cc, j0 * 128:j0 * 128 + w],
            start=True, stop=not diag),
            [[t_qT[base + ib] for ib in range(j0, BQ)], t_kv_done[c],
             ring_tok[slot]])
        if diag:
            msk = maskCT if (c == 0 and B4 == 0) else maskAT
            t = prog.add("tensor", lambda tt, slot=slot, msk=msk: tt.matmul(
                ring_ap2(slot, 0, 128, 0, 128), ident[:], msk[:],
                start=False, stop=True), [t])
        te = prog.add("scalar", lambda s, c=c, j0=j0, w=w, slot=slot,
                      ts_=ts_: s.activation(
            attnT[:, ts_, c, j0 * 128:j0 * 128 + w],
            ring_ap2(slot, 0, 128, 0, w),
            AF.Exp, bias=negC[:, 0:1]),
            [t, t_negC, attnT_free[ts_]])
        ring_tok[slot] = te
        hd["t_exps"].append(te)

    def emit_nulls(hd):
        nonlocal ring_ctr
        ts_, cc, po, bp, base = (hd["ts"], hd["cc"], hd["po"], hd["bp"],
                                 hd["base"])
        for ib in range(BQ):
            slot = ring_ctr % NRS
            ring_ctr += 1
            lqn = qhatT[po:po + 64, bp, cc, ib * 128:(ib + 1) * 128]
            t_ns = prog.add("tensor", lambda tt, slot=slot, lqn=lqn, po=po:
                            tt.matmul(
                ring_ap2(slot, 0, 2, 0, 128), nkhsT[po:po + 64, :], lqn,
                start=True, stop=True),
                [t_qT[base + ib], t_kv_done[base + ib], ring_tok[slot]])
            t_ne = prog.add("scalar", lambda s, slot=slot, ts_=ts_, ib=ib:
                            s.activation(
                attnTn[:, ts_, ib * 128:(ib + 1) * 128],
                ring_ap2(slot, 0, 2, 0, 128), AF.Exp,
                bias=negC[0:2, 0:1]), [t_ns, t_negC, attnTn_free[ts_]])
            ring_tok[slot] = t_ne
            hd["t_nes"].append(t_ne)

    def emit_den_chunk(hd, c):
        p, ts_, base = hd["par"], hd["ts"], hd["base"]
        j0 = max(0, c - base)
        w = BW - j0 * 128
        hd["t_dn"] = prog.add("tensor", lambda tt, c=c, j0=j0, w=w, p=p,
                              ts_=ts_: tt.matmul(
            pwork[0:2, 512 + j0 * 128:512 + j0 * 128 + w],
            dind[:, 2 * p:2 * p + 2],
            attnT[:, ts_, c, j0 * 128:j0 * 128 + w],
            start=(c == 0 and p == 0), stop=False),
            [hd["t_exps"][c], den_chain, hd["t_dn"]])

    def emit_den_null(hd):
        nonlocal den_chain, pairden_free
        B4, ts_, cc, p = hd["B4"], hd["ts"], hd["cc"], hd["par"]
        t = prog.add("tensor", lambda tt, p=p, ts_=ts_: tt.matmul(
            pwork[0:2, 512:512 + BW], nind[0:2, 2 * p:2 * p + 2],
            attnTn[0:2, ts_, :], start=False, stop=(p == 1)),
            [hd["t_nes"], den_chain, hd["t_dn"]])
        if p == 1:
            t_dln = prog.add("scalar", lambda s: s.activation(
                pairden[:, 1, 0:BW], pwork[0:2, 512:512 + BW], AF.Ln),
                [t, pairden_free])
            den_chain = [t_dln]
            t_rec = prog.add("scalar", lambda s: s.activation(
                pairden[:, 0, 0:BW], pairden[:, 1, 0:BW], AF.Exp,
                scale=-1.0), [t_dln])
            t_st = prog.dma("gpsimd", lambda g, cc=cc, B4=B4: g.dma_start(
                out=dinv[:, cc, B4 * BW:(B4 + 1) * BW],
                in_=pairden[:, 0, 0:BW]),
                [t_rec])
            pairden_free = [t_st, t_rec]
            t_dinvst[(cc, B4)] = t_st

    def emit_av_chunk(hd, c):
        B4, po, ts_ = hd["B4"], hd["po"], hd["ts"]
        j0 = max(0, c - BQ * B4)
        w = (BQ - j0) * 128
        hd["t_av"] = prog.add("tensor", lambda t, c=c, po=po, ts_=ts_, j0=j0,
                              w=w: t.matmul(
            pav[po:po + 64, j0 * 128:j0 * 128 + w],
            v_sb[:, c, :],
            attnT[:, ts_, c, j0 * 128:j0 * 128 + w],
            start=(c == 0), stop=False,
            tile_position=(0, po)),
            [hd["t_exps"][c], t_pav_free, hd["t_av"]])

    def emit_av_null(hd):
        nonlocal t_pav_free
        h, po, ts_, B4 = hd["h"], hd["po"], hd["ts"], hd["B4"]
        t_av = prog.add("tensor", lambda t, po=po, ts_=ts_: t.matmul(
            pav[po:po + 64, 0:BW], vnull[:], attnTn[:, ts_, :], start=False,
            stop=True, tile_position=(0, po)), [hd["t_nes"], hd["t_av"],
                                                t_vnull])
        attnT_free[ts_] = t_av
        attnTn_free[ts_] = t_av
        if h % 2 == 1:
            t_pav_free = prog.add("vector", lambda v, B4=B4, cc=h // 2: v.tensor_copy(
                outunT[:, cc, B4 * BW:(B4 + 1) * BW], pav[:, 0:BW]),
                [t_av])
            for ib in range(BQ):
                t_outun[B4 * BQ + ib][h // 2] = t_pav_free
        return t_av

    for i0 in range(min(2 * BQ, NT)):
        for s in stage0_steps(i0):
            s()
    NBK = NT // BQ
    qper = max(1, (len(quant_ops) + NBK * H - 1) // (NBK * H))
    prev_hd = None
    for B4 in range(NBK):
        steps_next = []
        if (B4 + 2) * BQ < NT:
            for i2 in range((B4 + 2) * BQ, (B4 + 3) * BQ):
                steps_next += stage0_steps(i2)
        si = 0
        for h in range(H):
            hd = head_meta(B4, h)
            pNC = prev_hd["NC"] if prev_hd is not None else 0
            for c in range(hd["NC"]):
                emit_score_chunk(hd, c)
                if c < pNC:
                    emit_den_chunk(prev_hd, c)
                    emit_av_chunk(prev_hd, c)
            emit_nulls(hd)
            if prev_hd is not None:
                emit_den_null(prev_hd)
                tavp = emit_av_null(prev_hd)
                if prev_hd["h"] == H - 1:
                    blk_tail[prev_hd["B4"]] = tavp
            prev_hd = hd
            for _ in range(3):
                if si < len(steps_next):
                    steps_next[si]()
                    si += 1
            run_quant(1 if si < len(steps_next) else qper + 1)
        while si < len(steps_next):
            steps_next[si]()
            si += 1
    for c in range(prev_hd["NC"]):
        emit_den_chunk(prev_hd, c)
        emit_av_chunk(prev_hd, c)
    emit_den_null(prev_hd)
    blk_tail[prev_hd["B4"]] = emit_av_null(prev_hd)
    run_quant(len(quant_ops))

    t_w1qt = qstate["w1qt_done"]
    t_w2qt = qstate["w2qt_done"]

    ctxA.close()
    prog.fence()

    ctxB = contextlib.ExitStack()
    SB_ = ctxB.enter_context

    def sbb(name, shape, dtype):
        return SB_(nc.sbuf_tensor(name, shape, dtype))

    NB2 = max(1, (NF + 3) // 4)
    NSG = NT // 4 if NT % 4 == 0 else NT // 2
    QSG = NT // NSG
    TW = QSG * 128

    Wout_sb = sbb("Wout_sb", [128, ND, D], BF16)
    W1B = sbb("W1B", [128, ND, F], BF16)
    w2ring = sbb("w2ring", [128, 8, 512], BF16)
    xnTg = sbb("xnTg", [128, 2, ND, TW], BF16)
    h1full = sbb("h1full", [128, NF, TW], BF16)
    y_sb = sbb("y_sb", [128, 2, 512], F32)
    xbufB = sbb("xbufB", [128, 2, 512], F32)
    invdenT = sbb("invdenT", [2, 2, ND, 128], F32)

    t_wout = prog.dma("gpsimd", lambda g: g.dma_start(out=Wout_sb[:], in_=Woutr))
    t_w1b = []
    for c4 in range(0, ND, 2):
        nn = min(2, ND - c4)
        t_w1b.append(prog.dma("scalar" if (c4 // 2) % 2 == 0 else "sync",
                              lambda g, c4=c4, nn=nn: g.dma_start(
            out=W1B[:, c4:c4 + nn, :],
            in_=w1qt_rw[:, c4:c4 + nn, :]), [t_w1qt]))

    t_y_done = [None] * NT * 2
    w2_batch_done = {}
    t_xb_free = [None, None]
    t_ysb_free = [None, None]
    t_xnTg_free = [None, None]
    t_ivT_readers = {0: [], 1: []}
    t_psimA = [ring_tok[0], ring_tok[1], ring_tok[2], ring_tok[3]]
    PM1 = [psim[:, 0:512], psim[:, 512:1024]]
    t_pm1_free = [[t_psimA], [t_psimA]]
    PACC = [psim[:, 1024:1536], psim[:, 1536:2048], pwork[:, 0:512],
            pwork[:, 512:1024]]
    t_pacc_free = [[t_psimA], [t_psimA], [ring_tok[4]], [den_chain]]
    t_bc_free = {}
    bank_chain = {}
    t_h1cons = {}
    t_h1_free = [None] * NF
    t_xg = [None, None]
    CCX_ORDER = [c for pr in zip(range(ND // 2), range(ND // 2, ND))
                 for c in pr] or list(range(ND))

    def den_steps(sg):
        spp = sg % 2
        jlist_ = list(range(QSG * sg, QSG * sg + QSG))
        ds = {}
        steps = []
        for jj, j in enumerate(jlist_):
            def s_load(j=j, spp=spp):
                ds["ivl"] = prog.dma("scalar", lambda g, j=j, spp=spp:
                                     g.dma_start(
                    out=invdenT[:, spp, :, :],
                    in_=dinv[:, :, j * 128:(j + 1) * 128]),
                    [[t_dinvst[(cc2, j // BQ)] for cc2 in range(ND)],
                     t_ivT_readers[spp]])
                t_ivT_readers[spp] = []
            steps.append(s_load)
            for ccx in CCX_ORDER:
                def s_bc(ccx=ccx, j=j, spp=spp):
                    pb = psim[:, ccx * 128:(ccx + 1) * 128]
                    bank = ccx // 4
                    t_bc = prog.add("tensor", lambda t, pb=pb, ccx=ccx, spp=spp:
                                    t.matmul(
                        pb, indic[:], invdenT[:, spp, ccx, :],
                        start=True, stop=True),
                        [ds["ivl"], bank_chain.get(bank), t_pm1_free[bank]])
                    t_ivT_readers[spp].append(t_bc)
                    t_oh = prog.add("vector", lambda v, pb=pb, ccx=ccx, j=j:
                                    v.tensor_tensor(
                        outunT[:, ccx, j * 128:(j + 1) * 128],
                        outunT[:, ccx, j * 128:(j + 1) * 128], pb, ALU.mult),
                        [t_bc, t_outun[j][ccx]])
                    t_bc_free[(spp, ccx)] = t_oh
                    bank_chain[bank] = t_oh
                steps.append(s_bc)

        def s_fin(spp=spp):
            t_pm1_free[0] = [t_pm1_free[0]] + [t_bc_free[(spp, c)]
                                               for c in range(0, min(4, ND))]
            t_pm1_free[1] = [t_pm1_free[1]] + [t_bc_free[(spp, c)]
                                               for c in range(4, ND)]
        steps.append(s_fin)
        return steps

    yi = 0
    for sg in range(NSG):
        sp = sg % 2
        jlist = list(range(QSG * sg, QSG * sg + QSG))
        if sg == 0:
            t_xg[0] = prog.dma("scalar", lambda g: g.dma_start(
                out=xnTg[:, 0, :, :], in_=xnt_rw[:, :, 0:TW]),
                [[t_xnst[j] for j in jlist]])
            for s_ in den_steps(0):
                s_()
        if sg + 1 < NSG:
            jn = list(range(QSG * (sg + 1), QSG * (sg + 1) + QSG))
            t_xg[1 - sp] = prog.dma("scalar", lambda g, sg=sg, sp=sp:
                                    g.dma_start(
                out=xnTg[:, 1 - sp, :, :],
                in_=xnt_rw[:, :, (sg + 1) * TW:(sg + 2) * TW]),
                [[t_xnst[j] for j in jn], t_xnTg_free[1 - sp]])
            dsq = den_steps(sg + 1)
        else:
            dsq = []
        dsi = 0
        for half in range(NH):
            hw_ = min(512, D - half * 512)
            t_w2tok = [None] * NF

            def issue_w2(b, half=half, sg=sg, t_w2tok=t_w2tok):
                nbc = min(4, NF - 4 * b)
                s0 = (4 * b) % 8
                gb = (sg * NH + half) * NB2 + b
                war = w2_batch_done.get(gb - 2)
                tk = prog.dma("scalar" if b % 2 == 0 else "sync",
                              lambda g, b=b, s0=s0, nbc=nbc, half=half:
                              g.dma_start(
                    out=w2ring[:, s0:s0 + nbc, 0:hw_],
                    in_=w2qt_rd[:, 4 * b:4 * b + nbc,
                                half * 512:half * 512 + hw_]),
                    [t_w2qt, war])
                for k in range(nbc):
                    t_w2tok[4 * b + k] = tk

            for b in range(min(2, NB2)):
                issue_w2(b)
            t_mmq = [None] * QSG
            for jj, j in enumerate(jlist):
                for ccx in range(ND):
                    t_mmq[jj] = prog.add("tensor", lambda t, ccx=ccx, j=j,
                                         jj=jj, half=half: t.matmul(
                        PACC[jj][:, 0:hw_],
                        outunT[:, ccx, j * 128:(j + 1) * 128],
                        Wout_sb[:, ccx, half * 512:half * 512 + hw_],
                        start=(ccx == 0), stop=False),
                        [t_bc_free[(sp, ccx)], t_wout, t_pacc_free[jj],
                         t_mmq[jj]])
                t_mmq[jj] = prog.add("tensor", lambda t, jj=jj, half=half: t.matmul(
                    PACC[jj][:, 0:hw_], ones_row_b[:],
                    b2_sb[:, half * 512:half * 512 + hw_], start=False,
                    stop=False), [t_b2, t_mmq[jj]])
            if half == 0:
                def emit_mm1(fc, sg=sg, sp=sp):
                    rot = fc % 2
                    t = None
                    for c in range(ND):
                        t = prog.add("tensor", lambda tt, c=c, fc=fc, rot=rot,
                                     sp=sp: tt.matmul(
                            PM1[rot][:, 0:TW], W1B[:, c, fc * 128:(fc + 1) * 128],
                            xnTg[:, sp, c, :],
                            start=(c == 0), stop=(c == ND - 1)),
                            [t_w1b[c // 2], t_xg[sp], t_pm1_free[rot], t])
                    return t

                t_m1 = [None] * NF
                t_m1[0] = emit_mm1(0)
                if NF > 1:
                    t_m1[1] = emit_mm1(1)
            for fc in range(NF):
                if half == 0:
                    rot = fc % 2
                    t_h1_free[fc] = prog.add("scalar", lambda s, fc=fc, rot=rot:
                                             s.activation(
                        h1full[:, fc, 0:TW], PM1[rot][:, 0:TW], AF.Relu,
                        bias=b1_sb[:, fc:fc + 1]),
                        [t_m1[fc], t_b1, t_h1cons.get(fc)])
                    t_pm1_free[rot] = t_h1_free[fc]
                    if fc + 2 < NF:
                        t_m1[fc + 2] = emit_mm1(fc + 2)
                for jj in range(QSG):
                    t_mmq[jj] = prog.add("tensor", lambda t, fc=fc, jj=jj,
                                         half=half: t.matmul(
                        PACC[jj][:, 0:hw_],
                        h1full[:, fc, jj * 128:(jj + 1) * 128],
                        w2ring[:, fc % 8, 0:hw_],
                        start=False, stop=(fc == NF - 1)),
                        [t_h1_free[fc], t_w2tok[fc], t_mmq[jj]])
                if fc % 4 == 3:
                    gb = (sg * NH + half) * NB2 + fc // 4
                    w2_batch_done[gb] = [t_mmq[jj] for jj in range(QSG)]
                    if fc // 4 + 2 < NB2:
                        issue_w2(fc // 4 + 2)
                if half == NH - 1:
                    t_h1cons[fc] = [t_mmq[jj] for jj in range(QSG)]
                    if NH >= 2:
                        for _ in range(2):
                            if dsi < len(dsq):
                                dsq[dsi]()
                                dsi += 1
            for jj, j in enumerate(jlist):
                yp = yi % 2
                yi += 1
                t_xb = prog.dma("gpsimd", lambda g, j=j, half=half, yp=yp:
                                g.dma_start(
                    out=xbufB[:, yp, 0:hw_],
                    in_=xr[j][:, half * 512:half * 512 + hw_]), [t_xb_free[yp]])
                t_y = prog.add("vector", lambda v, jj=jj, yp=yp, half=half:
                               v.scalar_tensor_tensor(
                    y_sb[:, yp, 0:hw_], PACC[jj][:, 0:hw_], 1.0,
                    xbufB[:, yp, 0:hw_], ALU.mult, ALU.add),
                    [t_mmq[jj], t_xb, t_ysb_free[yp]])
                t_pacc_free[jj] = t_y
                t_xb_free[yp] = t_y
                td = prog.dma("gpsimd" if yp == 0 else "sync",
                              lambda g, j=j, half=half, yp=yp:
                              g.dma_start(
                    out=yr[j][:, half * 512:half * 512 + hw_],
                    in_=y_sb[:, yp, 0:hw_]), [t_y])
                t_ysb_free[yp] = td
                t_y_done[2 * j + half] = td
        while dsi < len(dsq):
            dsq[dsi]()
            dsi += 1
        t_xnTg_free[sp] = [t_m1[fc] for fc in range(NF)]

    prog.add("gpsimd", None, t_y_done)

    with nc.Block() as block:
        prog.emit(nc, block, stack)
    ctxB.close()
    stack.close()
    return nc



_CACHE = {}


def _get_prog(cfg_key):
    if cfg_key not in _CACHE:
        _CACHE[cfg_key] = build_program(FULL_CFG)
    return _CACHE[cfg_key]


def kernel(L0_attn_x, L0_attn_gamma, L0_attn_Wq, L0_attn_Wkv, L0_attn_q_scale,
           L0_attn_k_scale, L0_attn_null_kv, L0_attn_Wout, L0_ffn_W1, L0_ffn_b1,
           L0_ffn_W2, L0_ffn_b2):
    x = np.ascontiguousarray(np.asarray(L0_attn_x, dtype=np.float32))
    B = x.shape[0]
    nc = _get_prog("full")
    consts = make_consts(FULL_CFG)
    shared = {
        "gamma": np.ascontiguousarray(L0_attn_gamma, dtype=np.float32),
        "Wq": np.ascontiguousarray(L0_attn_Wq, dtype=np.float32),
        "Wkv": np.ascontiguousarray(L0_attn_Wkv, dtype=np.float32),
        "q_scale": np.ascontiguousarray(L0_attn_q_scale, dtype=np.float32),
        "k_scale": np.ascontiguousarray(L0_attn_k_scale, dtype=np.float32),
        "null_kv": np.ascontiguousarray(L0_attn_null_kv, dtype=np.float32),
        "Wout": np.ascontiguousarray(L0_attn_Wout, dtype=np.float32),
        "W1": np.ascontiguousarray(L0_ffn_W1, dtype=np.float32),
        "b1": np.ascontiguousarray(L0_ffn_b1, dtype=np.float32),
        "W2": np.ascontiguousarray(L0_ffn_W2, dtype=np.float32),
        "b2": np.ascontiguousarray(L0_ffn_b2, dtype=np.float32),
    }
    shared.update(consts)
    in_maps = [dict(shared, x=np.ascontiguousarray(x[b])) for b in range(B)]
    res = run_bass_kernel_spmd(nc, in_maps, core_ids=list(range(B)))
    return np.stack([r["y"] for r in res.results], axis=0)


# version 34
# speedup vs baseline: 1.0550x; 1.0550x over previous
import contextlib
import numpy as np
import ml_dtypes
import concourse.bass as bass
import concourse.mybir as mybir
from concourse.bass_utils import run_bass_kernel_spmd

F32 = mybir.dt.float32
BF16 = mybir.dt.bfloat16
AF = mybir.ActivationFunctionType
ALU = mybir.AluOpType
AX = mybir.AxisListType

MAGIC = 12582912.0
NEGBIG = -1.0e30
LN_EPS = 1e-5
Q_EPS = 1e-5
SCALE = 8.0

FULL_CFG = dict(T=2048, D=1024, H=16, F=4096, PRELEN=64)



class Op:
    __slots__ = ("eng", "fn", "deps", "dma", "sem_val", "needed", "idx")

    def __init__(self, eng, fn, deps, dma):
        self.eng = eng
        self.fn = fn
        self.deps = deps
        self.dma = dma
        self.sem_val = None
        self.needed = False
        self.idx = None


def _flat(deps):
    out = []

    def rec(d):
        if d is None:
            return
        if isinstance(d, (list, tuple)):
            for e in d:
                rec(e)
        else:
            out.append(d)
    rec(deps)
    return out


class Prog:
    CE = ("tensor", "vector", "scalar", "gpsimd")

    def __init__(self, n_g=8, n_y=8, n_s=16):
        self.ops = []
        self.n_dma = {"gpsimd": n_g, "sync": n_y, "scalar": n_s}
        self.extra = []

    def add(self, eng, fn, deps=()):
        op = Op(eng, fn, _flat(deps) + self.extra, dma=False)
        self.ops.append(op)
        return op

    def dma(self, eng, fn, deps=()):
        op = Op(eng, fn, _flat(deps) + self.extra, dma=True)
        self.ops.append(op)
        return op

    def fence(self):
        last = {}
        dix = {e: 0 for e in self.n_dma}
        for op in self.ops:
            if op.dma:
                last[(op.eng, dix[op.eng] % self.n_dma[op.eng])] = op
                dix[op.eng] += 1
            else:
                last[op.eng] = op
        self.extra = list(last.values())

    def emit(self, nc, block, stack):
        for op in self.ops:
            for d in op.deps:
                d.needed = True
        counters = {}
        dma_idx = {"gpsimd": 0, "sync": 0, "scalar": 0}
        for op in self.ops:
            if op.dma:
                i = dma_idx[op.eng]
                dma_idx[op.eng] = i + 1
                op.idx = i
                op.sem_val = 16 * (i // self.n_dma[op.eng] + 1)
            elif op.needed:
                c = counters.get(op.eng, 0) + 1
                counters[op.eng] = c
                op.sem_val = c

        sems = {e: stack.enter_context(nc.semaphore("sem_" + e)) for e in self.CE}
        dsems = {e: [stack.enter_context(nc.semaphore(f"sem_{e}{i}")) for i in range(n)]
                 for e, n in self.n_dma.items()}
        ops = self.ops
        n_dma = self.n_dma

        def runner(engname):
            def body(eng):
                hw = {}

                def wait(key, handle, val):
                    if hw.get(key, 0) < val:
                        eng.wait_ge(handle, val)
                        hw[key] = val

                for op in ops:
                    if op.eng != engname:
                        continue
                    for d in op.deps:
                        if d.dma:
                            slot = d.idx % n_dma[d.eng]
                            wait((d.eng, slot), dsems[d.eng][slot], d.sem_val)
                        else:
                            if d.eng == engname and engname == "tensor":
                                continue
                            if d.sem_val is None:
                                continue
                            wait((d.eng,), sems[d.eng], d.sem_val)
                    if op.dma:
                        slot = op.idx % n_dma[op.eng]
                        if op.sem_val > 16:
                            wait((op.eng, slot), dsems[op.eng][slot], op.sem_val - 16)
                        op.fn(eng).then_inc(dsems[op.eng][slot], 16)
                    elif op.fn is not None:
                        inst = op.fn(eng)
                        if op.needed:
                            inst.then_inc(sems[op.eng], 1)
            return body

        block.tensor(runner("tensor"))
        block.vector(runner("vector"))
        block.scalar(runner("scalar"))
        block.gpsimd(runner("gpsimd"))
        block.sync(runner("sync"))



def make_consts(cfg):
    P = 128
    pre = cfg["PRELEN"]
    t = np.arange(P)[:, None]
    kk = np.arange(P)[None, :]
    maskA = np.where(kk <= t, 0.0, NEGBIG)
    maskC = np.where((kk <= t) | ((t < pre) & (kk < pre)), 0.0, NEGBIG)
    ind = np.zeros((2, P), np.float32)
    ind[0, 0:64] = 1.0
    ind[1, 64:128] = 1.0
    dind = np.zeros((P, 4), np.float32)
    dind[:, 0] = 1.0
    dind[:, 3] = 1.0
    return {
        "c_maskA": maskA.astype(ml_dtypes.bfloat16),
        "c_maskC": maskC.astype(ml_dtypes.bfloat16),
        "c_maskAT": maskA.T.copy().astype(ml_dtypes.bfloat16),
        "c_maskCT": maskC.T.copy().astype(ml_dtypes.bfloat16),
        "c_dind": dind.astype(ml_dtypes.bfloat16),
        "c_nind": dind[0:2, :].copy().astype(ml_dtypes.bfloat16),
        "c_ident": np.eye(P).astype(ml_dtypes.bfloat16),
        "c_identf": np.eye(P).astype(np.float32),
        "c_ones_row_b": np.ones((1, P), ml_dtypes.bfloat16),
        "c_ones_row_f": np.ones((1, P), np.float32),
        "c_ones_col_b": np.ones((P, 1), ml_dtypes.bfloat16),
        "c_ones_col_f": np.ones((P, 1), np.float32),
        "c_indic": ind,
    }



def build_program(cfg):
    T, D, H, F = cfg["T"], cfg["D"], cfg["H"], cfg["F"]
    NT, ND, NF = T // 128, D // 128, F // 128
    assert H * 64 == D and NT % 2 == 0
    PW = min(F, 1024)
    NW2S = PW // 128
    NQ2 = F // PW
    W2ST = ND * NQ2
    NH = (D + 511) // 512
    BQ = 4 if NT % 4 == 0 else 2
    BW = BQ * 128
    AOFF = []
    _aw = 0
    for ib in range(BQ):
        AOFF.append(_aw)
        _aw += (NT - BQ + ib + 1) * 128 + 4
    AW = _aw

    nc = bass.Bass("TRN2", target_bir_lowering=False)

    dx = nc.dram_tensor("x", [T, D], F32, kind="ExternalInput").ap()
    dgamma = nc.dram_tensor("gamma", [D], F32, kind="ExternalInput").ap()
    dWq = nc.dram_tensor("Wq", [D, D], F32, kind="ExternalInput").ap()
    dWkv = nc.dram_tensor("Wkv", [D, 128], F32, kind="ExternalInput").ap()
    dqs = nc.dram_tensor("q_scale", [64], F32, kind="ExternalInput").ap()
    dks = nc.dram_tensor("k_scale", [64], F32, kind="ExternalInput").ap()
    dnkv = nc.dram_tensor("null_kv", [2, 2, 64], F32, kind="ExternalInput").ap()
    dWout = nc.dram_tensor("Wout", [D, D], F32, kind="ExternalInput").ap()
    dW1 = nc.dram_tensor("W1", [F, D], F32, kind="ExternalInput").ap()
    db1 = nc.dram_tensor("b1", [F], F32, kind="ExternalInput").ap()
    dW2 = nc.dram_tensor("W2", [D, F], F32, kind="ExternalInput").ap()
    db2 = nc.dram_tensor("b2", [D], F32, kind="ExternalInput").ap()
    dY = nc.dram_tensor("y", [T, D], F32, kind="ExternalOutput").ap()
    dw2qt = nc.dram_tensor("w2qt", [F, D], BF16).ap()
    dw1qt = nc.dram_tensor("w1qt", [D, F], BF16).ap()
    dxnt = nc.dram_tensor("xnt", [ND * 128, T], BF16).ap()
    dinv = nc.dram_tensor("dinvt", [2, ND, T], F32).ap()
    dcon = {}
    for n, a in make_consts(cfg).items():
        dt = BF16 if a.dtype == ml_dtypes.bfloat16 else F32
        dcon[n] = nc.dram_tensor(n, list(a.shape), dt, kind="ExternalInput").ap()

    xr = dx.rearrange("(i p) d -> i p d", p=128)
    yr = dY.rearrange("(i p) d -> i p d", p=128)
    Wqr = dWq.rearrange("(c p) e -> p c e", p=128)
    Wkvr = dWkv.rearrange("(c p) e -> p c e", p=128)
    Woutr = dWout.rearrange("(c p) e -> p c e", p=128)
    W1r = dW1.rearrange("(c p) d -> c p d", p=128)
    W2r = dW2.rearrange("(c p) f -> c p f", p=128)
    w2qt_rd = dw2qt.rearrange("(c p) d -> p c d", p=128)
    w2qt_wr = dw2qt.rearrange("(q s p) d -> p q s d", p=128, s=NW2S)
    w1qt_rw = dw1qt.rearrange("(c p) f -> p c f", p=128)
    xnt_rw = dxnt.rearrange("(c p) t -> p c t", p=128)
    gam_r = dgamma.rearrange("(c p) -> p c", p=128)
    b1_r = db1.rearrange("(c p) -> p c", p=128)

    prog = Prog()
    stack = contextlib.ExitStack()
    S = stack.enter_context

    def sb(name, shape, dtype):
        return S(nc.sbuf_tensor(name, shape, dtype))

    psim = S(nc.psum_tensor("psim", [128, 2048], F32))
    ptrb = S(nc.psum_tensor("ptrb", [128, 1024], BF16))
    pav = S(nc.psum_tensor("pav", [128, 512], F32))
    pwork = S(nc.psum_tensor("pwork", [128, 1024], F32))

    outunT = sb("outunT", [128, ND, T], BF16)
    qtmp = sb("qtmp", [128, max(D, PW)], F32)
    maskAT = sb("maskAT", [128, 128], BF16)
    maskCT = sb("maskCT", [128, 128], BF16)
    dind = sb("dind", [128, 4], BF16)
    nind = sb("nind", [2, 4], BF16)
    ident = sb("ident", [128, 128], BF16)
    identf = sb("identf", [128, 128], F32)
    ones_row_b = sb("ones_row_b", [1, 128], BF16)
    ones_row_f = sb("ones_row_f", [1, 128], F32)
    ones_col_b = sb("ones_col_b", [128, 1], BF16)
    ones_col_f = sb("ones_col_f", [128, 1], F32)
    indic = sb("indic", [2, 128], F32)
    b1_sb = sb("b1_sb", [128, NF], F32)
    b2_sb = sb("b2_sb", [1, D], BF16)
    gam_sb = sb("gam_sb", [128, ND], F32)
    s8col = sb("s8col", [128, 1], F32)
    qs_st = sb("qs_st", [128, 1], F32)
    ks_st = sb("ks_st", [128, 1], F32)
    mu2b = sb("mu2b", [128, 2], BF16)
    mu2row = sb("mu2row", [1, 2, 128], BF16)
    negc = sb("negc", [1, D], BF16)
    stat = sb("stat", [128, 2, 8], F32)
    pabs = sb("pabs", [128, NF + W2ST], F32)
    gsc = sb("gsc", [1, 8], F32)
    ginv = sb("ginv", [128, 2], F32)
    gcol = sb("gcol", [128, 2], F32)
    epsc = sb("epsc", [128, 1], F32)
    magc = sb("magc", [128, 1], F32)

    ctxA = contextlib.ExitStack()
    SA = ctxA.enter_context

    def sba(name, shape, dtype):
        return SA(nc.sbuf_tensor(name, shape, dtype))

    xbufA = sba("xbufA", [128, 2, D], F32)
    xnT = sba("xnT", [128, 2, ND, 128], BF16)
    xnbf = sba("xnbf", [128, D], BF16)
    qraw = sba("qraw", [128, D], BF16)
    qss = sba("qss", [128, H], F32)
    qlog = sba("qlog", [128, H], F32)
    invq = sba("invq", [128, H], F32)
    qhat = sba("qhat", [128, D], BF16)
    qhatT = sba("qhatT", [128, 3, ND, BW], BF16)
    ATS = 3
    attnT = sba("attnT", [128, ATS, NT, BW], BF16)
    attnTn = sba("attnTn", [2, ATS, BW], BF16)
    negC = sba("negC", [128, 1], F32)
    pairden = sba("pairden", [2, 2, BW], F32)
    Wq_sb = sba("Wq_sb", [128, ND, D], BF16)
    Wkv_sb = sba("Wkv_sb", [128, ND, 128], BF16)
    khsT = sba("khsT", [128, T], BF16)
    nkhsT = sba("nkhsT", [128, 2], BF16)
    v_sb = sba("v_sb", [128, NT, 64], BF16)
    vnull = sba("vnull", [2, 64], BF16)
    kvT_sb = sba("kvT_sb", [128, 128], BF16)
    ktok = sba("ktok", [128, 128], BF16)
    ktr2 = sba("ktr2", [128, 128], BF16)
    kss = sba("kss", [128, 2], F32)
    krs = sba("krs", [128, 1], F32)
    nst = sba("nst", [2, 64], F32)
    nss = sba("nss", [2, 2], F32)
    nkh = sba("nkh", [2, 64], BF16)
    wstage = sba("wstage", [128, 2, max(D, PW)], F32)
    qsc = sba("qsc", [128, max(D, PW)], F32)
    wqbf = sba("wqbf", [128, 1, max(D, PW)], BF16)
    wtout = sba("wtout", [128, 8, 128], BF16)

    for name, ap_sb in [("c_maskAT", maskAT), ("c_maskCT", maskCT),
                        ("c_dind", dind), ("c_nind", nind), ("c_ident", ident),
                        ("c_identf", identf), ("c_ones_row_b", ones_row_b),
                        ("c_ones_row_f", ones_row_f), ("c_ones_col_b", ones_col_b),
                        ("c_ones_col_f", ones_col_f), ("c_indic", indic)]:
        prog.dma("gpsimd", lambda g, a=ap_sb, d=dcon[name]: g.dma_start(out=a[:], in_=d))
    t_consts = prog.ops[-1]
    t_b1 = prog.dma("gpsimd", lambda g: g.dma_start(
        out=b1_sb[:], in_=b1_r, allow_slow_non_contiguous=True))
    t_b2 = prog.dma("gpsimd", lambda g: g.dma_start(
        out=b2_sb[:], in_=db2.rearrange("(o d) -> o d", o=1)))
    t_gam = prog.dma("gpsimd", lambda g: g.dma_start(
        out=gam_sb[:], in_=gam_r, allow_slow_non_contiguous=True))
    t_qs = [prog.dma("gpsimd", lambda g, o=o: g.dma_start(
        out=qs_st[o * 64:(o + 1) * 64, :], in_=dqs.rearrange("(k o) -> k o", o=1)))
        for o in range(2)]
    t_ks = [prog.dma("gpsimd", lambda g, o=o: g.dma_start(
        out=ks_st[o * 64:(o + 1) * 64, :], in_=dks.rearrange("(k o) -> k o", o=1)))
        for o in range(2)]
    t_s8 = prog.add("vector", lambda v: v.scalar_tensor_tensor(
        s8col[:], qs_st[:], SCALE, ks_st[:], ALU.mult, ALU.mult), [t_qs, t_ks])
    t_eps = prog.add("vector", lambda v: v.memset(epsc[:], LN_EPS))
    t_mag = prog.add("vector", lambda v: v.memset(magc[:], -MAGIC))

    t_wst = [None, None]
    for c in range(ND):
        tl = prog.dma("scalar", lambda g, c=c: g.dma_start(
            out=wstage[:, c % 2, 0:128], in_=Wkvr[:, c, :]), [t_wst[c % 2]])
        t_wst[c % 2] = prog.add("vector", lambda v, c=c: v.tensor_copy(
            Wkv_sb[:, c, :], wstage[:, c % 2, 0:128]), [tl])
    t_wkv = [t_wst[0], t_wst[1]]
    t_wq_chunks = []
    for c in range(ND):
        tl = prog.dma("scalar", lambda g, c=c: g.dma_start(
            out=wstage[:, c % 2, 0:D], in_=Wqr[:, c, :]), [t_wst[c % 2]])
        t_wst[c % 2] = prog.add("vector", lambda v, c=c: v.tensor_scalar(
            Wq_sb[:, c, :], wstage[:, c % 2, 0:D], gam_sb[:, c:c + 1], None,
            ALU.mult), [tl, t_gam])
        t_wq_chunks.append(t_wst[c % 2])
    t_wq = list(t_wq_chunks)

    t_ptr_free = t_consts
    t_pav_free = None
    t_pwork_free = None
    t_qtmp_free = None

    t_negc_parts = []
    for hh in range(NH):
        w = min(512, D - hh * 512)
        t_mm = None
        for c in range(ND):
            t_mm = prog.add("tensor", lambda t, c=c, hh=hh, w=w: t.matmul(
                pwork[0:1, hh * 512:hh * 512 + w], ones_col_b[:],
                Wq_sb[:, c, hh * 512:hh * 512 + w],
                start=(c == 0), stop=(c == ND - 1)),
                [t_wq, t_mm, t_consts])
        t_negc_parts.append(t_mm)
    t_negc = prog.add("vector", lambda v: v.tensor_scalar(
        negc[:], pwork[0:1, 0:D], -1.0 / D, None, ALU.mult), t_negc_parts)

    t_nk_l = prog.dma("gpsimd", lambda g: g.dma_start(out=nst[:], in_=dnkv[0]))
    t_nv_l = prog.dma("gpsimd", lambda g: g.dma_start(out=qtmp[0:2, 64:128], in_=dnkv[1]))
    t_vnull = prog.add("vector", lambda v: v.tensor_copy(vnull[:], qtmp[0:2, 64:128]),
                       [t_nv_l])
    t_qtmp_free = [t_qtmp_free, t_vnull]
    t_nss = prog.add("vector", lambda v: v.scalar_tensor_tensor(
        qtmp[0:2, 0:64], nst[:], 1.0, nst[:], ALU.mult, ALU.mult,
        accum_out=nss[:, 0:1]), [t_nk_l, t_qtmp_free])
    t_qtmp_free = t_nss
    t_nlog = prog.add("scalar", lambda s: s.activation(nss[:, 1:2], nss[:, 0:1],
                                                       AF.Ln), [t_nss])
    t_nrs = prog.add("scalar", lambda s: s.activation(nss[:, 0:1], nss[:, 1:2],
                                                      AF.Exp, scale=-0.5), [t_nlog])
    t_nkh = prog.add("vector", lambda v: v.tensor_scalar(
        nkh[:], nst[:], nss[:, 0:1], None, ALU.mult), [t_nrs])
    t_nkt = prog.add("tensor", lambda t: t.transpose(
        ptrb[0:64, 0:2], nkh[:], ident[0:2, 0:2]), [t_nkh, t_ptr_free])
    t_nkt2 = prog.add("tensor", lambda t: t.transpose(
        ptrb[64:128, 0:2], nkh[:], ident[0:2, 0:2], tile_position=(0, 64)),
        [t_nkh, t_ptr_free])
    t_ptr_free = prog.add("vector", lambda v: v.tensor_scalar(
        nkhsT[:], ptrb[:, 0:2], s8col[:], None, ALU.mult),
        [t_nkt, t_nkt2, t_s8])
    t_nkhsT = t_ptr_free

    t_s8t = prog.add("tensor", lambda t: t.transpose(
        pav[0:1, 384:512], s8col[:], identf[:]), [t_s8, t_pav_free])
    t_cb = prog.add("vector", lambda v: v.reduce_max(
        gsc[:, 6:7], pav[0:1, 384:512], axis=AX.X,
        apply_absolute_value=True), [t_s8t])
    t_cbb = prog.add("tensor", lambda t: t.matmul(
        pav[0:128, 440:441], ones_row_f[:], gsc[:, 6:7], start=True,
        stop=True), [t_cb])
    t_negC = prog.add("vector", lambda v: v.tensor_scalar(
        negC[:], pav[:, 440:441], -1.0, None, ALU.mult), [t_cbb])
    t_pav_free = t_negC

    x_tok = [None] * NT

    def xload_A(i, dep):
        x_tok[i] = prog.dma("gpsimd", lambda g, i=i: g.dma_start(
            out=xbufA[:, i % 2, :], in_=xr[i]), [dep])

    xload_A(0, None)
    xload_A(1, None)

    qstate = dict(wst=t_wst, ptr_free=None, pav_free=None, qtmp_free=None,
                  pabs_done=[], g1=None, wto=[None], wqbf_free=None,
                  qsc_free=None, w1qt_done=[], w2qt_done=[])

    def quant_load(kind, idx, st):
        def go():
            slot = quant_load.slot
            quant_load.slot ^= 1
            st["slot"] = slot
            if kind == "w1":
                src_, wdt = W1r[idx], D
            else:
                dd, q = divmod(idx, NQ2)
                src_, wdt = W2r[dd][:, q * PW:(q + 1) * PW], PW
            st["ld"] = prog.dma("scalar", lambda g: g.dma_start(
                out=wstage[:, slot, 0:wdt], in_=src_), [qstate["wst"][slot]])
        return go
    quant_load.slot = 0

    def quant_abs(kind, idx, st):
        def go():
            slot = st["slot"]
            if kind == "w1":
                wdt, col = D, idx
            else:
                wdt, col = PW, NF + idx
            tr = prog.add("vector", lambda v: v.reduce_sum(
                pabs[:, col:col + 1], wstage[:, slot, 0:wdt], axis=AX.X,
                apply_absolute_value=True), [st["ld"]])
            qstate["wst"][slot] = tr
            qstate["pabs_done"].append(tr)
        return go

    def quant_gamma():
        def go():
            nonlocal ring_ctr
            deps = list(qstate["pabs_done"])
            t_c1 = prog.add("vector", lambda v: v.reduce_sum(
                gcol[:, 0:1], pabs[:, 0:NF], axis=AX.X), deps)
            t_c2 = prog.add("vector", lambda v: v.reduce_sum(
                gcol[:, 1:2], pabs[:, NF:NF + W2ST], axis=AX.X), deps)
            s1 = ring_ctr % NRS
            ring_ctr += 1
            t_tm = prog.add("tensor", lambda t, s1=s1: t.matmul(
                ring_ap2(s1, 0, 1, 0, 2), ones_col_f[:], gcol[:], start=True,
                stop=True), [t_c1, t_c2, ring_tok[s1]])
            t_g = prog.add("vector", lambda v, s1=s1: v.tensor_scalar(
                gsc[:, 0:2], ring_ap2(s1, 0, 1, 0, 2), 1.0 / (F * D), Q_EPS,
                ALU.mult, ALU.add), [t_tm])
            ring_tok[s1] = t_g
            t_r = prog.add("vector", lambda v: v.reciprocal(gsc[:, 2:4],
                                                            gsc[:, 0:2]), [t_g])
            s2 = ring_ctr % NRS
            ring_ctr += 1
            t_bc = prog.add("tensor", lambda t, s2=s2: t.matmul(
                ring_ap2(s2, 0, 128, 0, 2), ones_row_f[:], gsc[:, 2:4],
                start=True, stop=True), [t_r, t_g, ring_tok[s2]])
            t_gv = prog.add("vector", lambda v, s2=s2: v.tensor_copy(
                ginv[:], ring_ap2(s2, 0, 128, 0, 2)), [t_bc])
            ring_tok[s2] = t_gv
            qstate["g1"] = t_gv
        return go

    def quant_pass2c(kind, idx, st):
        def go():
            slot = st["slot"]
            if kind == "w1":
                wdt, gc = D, 0
            else:
                dd, q = divmod(idx, NQ2)
                wdt, gc = PW, 1
            t1 = prog.add("vector", lambda v: v.tensor_scalar(
                qsc[:, 0:wdt], wstage[:, slot, 0:wdt], ginv[:, gc:gc + 1],
                MAGIC, ALU.mult, ALU.add), [st["ld"], qstate["g1"],
                                            qstate["qsc_free"]])
            t2 = prog.add("scalar", lambda s: s.add(
                wstage[:, slot, 0:wdt], qsc[:, 0:wdt], magc[:, 0:1]),
                [t1, t_mag])
            qstate["qsc_free"] = t2
            t3 = prog.add("vector", lambda v: v.tensor_scalar(
                wqbf[:, 0, 0:wdt], wstage[:, slot, 0:wdt], 1.0, -1.0,
                ALU.min, ALU.max), [t2, qstate["wqbf_free"]])
            qstate["wst"][slot] = t3
            nsub = wdt // 128
            tdt = prog.dma("sync", lambda y, nsub=nsub, wdt=wdt: y.dma_start_transpose(
                wtout[:, 0:nsub, :], wqbf[:, 0, 0:wdt]), [t3, qstate["wto"][0]])
            qstate["wqbf_free"] = tdt
            if kind == "w1":
                dst = w1qt_rw[:, :, idx * 128:(idx + 1) * 128]
            else:
                dd, q = divmod(idx, NQ2)
                dst = w2qt_wr[:, q, :, dd * 128:(dd + 1) * 128]
            tst = prog.dma("gpsimd", lambda g, dst=dst, nsub=nsub:
                           g.dma_start(out=dst, in_=wtout[:, 0:nsub, :]), [tdt])
            qstate["wto"][0] = tst
            (qstate["w1qt_done"] if kind == "w1" else
             qstate["w2qt_done"]).append(tst)
        return go

    _stages = []
    for kind, n in (("w1", NF), ("w2", W2ST)):
        for i in range(n):
            st = {}
            _stages.append((quant_load(kind, i, st), quant_abs(kind, i, st)))
    _stages.append((None, quant_gamma()))
    for kind, n in (("w1", NF), ("w2", W2ST)):
        for i in range(n):
            st = {}
            _stages.append((quant_load(kind, i, st), quant_pass2c(kind, i, st)))

    NQS = len(_stages)

    def _unit(k):
        def go():
            if k - 2 >= 0:
                _stages[k - 2][1]()
            if k < NQS and _stages[k][0] is not None:
                _stages[k][0]()
        return go
    quant_ops = [_unit(k) for k in range(NQS + 2)]
    qi = 0

    def run_quant(n):
        nonlocal qi, t_ptr_free, t_pav_free, t_qtmp_free
        for _ in range(n):
            if qi < len(quant_ops):
                qstate["ptr_free"] = t_ptr_free
                qstate["pav_free"] = t_pav_free
                qstate["qtmp_free"] = t_qtmp_free
                quant_ops[qi]()
                t_ptr_free = qstate["ptr_free"]
                t_pav_free = qstate["pav_free"]
                t_qtmp_free = qstate["qtmp_free"]
                qi += 1

    t_xnT = [None] * NT
    t_xnst = [None] * NT
    t_kv_done = [None] * NT
    t_qT = [None] * NT
    t_outun = [[None] * ND for _ in range(NT)]

    stat_free = [None, None]
    xnT_free = [None, None]
    mu2_free = [None, None]
    mu2row_tok = [None, None]
    xnbf_free = None
    qchain_free = None
    kchain_free = None
    blk_tail = {}

    def stage0_steps(i):
        par = i % 2
        xb = xbufA[:, par, :]
        sl = stat[:, par, :]
        st_ = {}

        def s_stats():
            nonlocal t_qtmp_free
            st_["mu"] = prog.add("vector", lambda v: v.reduce_sum(
                sl[:, 0:1], xb, axis=AX.X), [x_tok[i], stat_free[par]])
            t_sq = prog.add("vector", lambda v: v.scalar_tensor_tensor(
                qtmp[:, 0:D], xb, 1.0, xb, ALU.mult, ALU.mult,
                accum_out=sl[:, 1:2]), [x_tok[i], stat_free[par], t_qtmp_free])
            t_qtmp_free = t_sq
            t_var = prog.add("vector", lambda v: v.scalar_tensor_tensor(
                sl[:, 2:3], sl[:, 0:1], 1.0 / D, sl[:, 0:1], ALU.mult, ALU.mult),
                [st_["mu"]])
            t_var2 = prog.add("vector", lambda v: v.scalar_tensor_tensor(
                sl[:, 3:4], sl[:, 1:2], 1.0, sl[:, 2:3], ALU.bypass,
                ALU.subtract), [t_sq, t_var])
            st_["var2"] = t_var2

        def s_ln():
            t_ln = prog.add("scalar", lambda s: s.activation(
                sl[:, 4:5], sl[:, 3:4], AF.Ln, scale=1.0 / D,
                bias=epsc[:, 0:1]), [st_["var2"], t_eps])
            t_rsig = prog.add("scalar", lambda s: s.activation(
                sl[:, 3:4], sl[:, 4:5], AF.Exp, scale=-0.5), [t_ln])
            st_["nmr"] = prog.add("vector", lambda v: v.scalar_tensor_tensor(
                sl[:, 5:6], sl[:, 0:1], -1.0 / D, sl[:, 3:4], ALU.mult,
                ALU.mult), [t_rsig, st_["mu"]])

        def s_xnf():
            nonlocal xnbf_free
            st_["xnf"] = prog.add("vector", lambda v: v.tensor_scalar(
                xnbf[:], xb, sl[:, 3:4], sl[:, 5:6], ALU.mult, ALU.add),
                [st_["nmr"], xnbf_free])
            st_["mu2"] = prog.add("vector", lambda v: v.reduce_sum(
                sl[:, 6:7], xnbf[:], axis=AX.X), [st_["xnf"]])
            st_["xnbf"] = st_["xnf"]
            stat_free[par] = [st_["xnf"], st_["mu2"]]

        def s_mu2xn():
            nonlocal t_ptr_free, xnbf_free
            t_mu2b = prog.add("vector", lambda v: v.tensor_copy(
                mu2b[:, par:par + 1], sl[:, 6:7]), [st_["mu2"], mu2_free[par]])
            t_mu2t = prog.add("tensor", lambda t: t.transpose(
                ptrb[0:1, 768:896], mu2b[:, par:par + 1], ident[:]),
                [t_mu2b, t_ptr_free])
            st_["mu2r"] = prog.add("vector", lambda v: v.tensor_copy(
                mu2row[:, par, :], ptrb[0:1, 768:896]), [t_mu2t])
            mu2row_tok[par] = st_["mu2r"]
            t_ptr_free = st_["mu2r"]
            tdt = prog.dma("sync", lambda y: y.dma_start_transpose(
                xnT[:, par, :, :], xnbf[:, 0:D]),
                [st_["xnbf"], xnT_free[par]])
            t_xnT[i] = tdt
            tst = prog.dma("gpsimd", lambda g: g.dma_start(
                out=xnt_rw[:, :, i * 128:(i + 1) * 128], in_=xnT[:, par, :, :]),
                [tdt])
            st_["xnst"] = tst
            t_xnst[i] = tst
            xnbf_free = [tdt, st_["mu2"]]
            if i + 2 < NT:
                xload_A(i + 2, [st_["xnf"], t_qtmp_free])

        def s_kv():
            nonlocal ring_ctr, t_ptr_free, kchain_free, t_qtmp_free
            slot = ring_ctr % NRS
            ring_ctr += 1
            t_kvmm = None
            for c in range(ND):
                t_kvmm = prog.add("tensor", lambda t, c=c, slot=slot, par=par:
                                  t.matmul(
                    ring_ap2(slot, 0, 128, 0, 128), Wkv_sb[:, c, :],
                    xnT[:, par, c, :],
                    start=(c == 0), stop=(c == ND - 1)),
                    [t_wkv, t_xnT[i], ring_tok[slot], t_kvmm])
            t_kvev = prog.add("vector", lambda v, slot=slot: v.tensor_copy(
                kvT_sb[:], ring_ap2(slot, 0, 128, 0, 128)), [t_kvmm, kchain_free])
            ring_tok[slot] = t_kvev
            t_ktok = prog.dma("sync", lambda y: y.dma_start_transpose(
                ktok[:], kvT_sb[:]), [t_kvev, kchain_free])
            t_kss = prog.add("vector", lambda v: v.scalar_tensor_tensor(
                qtmp[:, 0:64], ktok[:, 0:64], 1.0, ktok[:, 0:64], ALU.mult,
                ALU.mult, accum_out=kss[:, 0:1]), [t_ktok, t_vnull, t_qtmp_free])
            t_qtmp_free = t_kss
            st_["kss"] = t_kss
            st_["ktok"] = t_ktok

        def s_kv2():
            nonlocal kchain_free
            t_kss = st_["kss"]
            t_ktok = st_["ktok"]
            t_klog = prog.add("scalar", lambda s: s.activation(
                kss[:, 1:2], kss[:, 0:1], AF.Ln), [t_kss])
            t_krs = prog.add("scalar", lambda s: s.activation(
                krs[:, 0:1], kss[:, 1:2], AF.Exp, scale=-0.5), [t_klog])
            t_khat = prog.add("vector", lambda v: v.tensor_scalar(
                ktok[:, 0:64], ktok[:, 0:64], krs[:, 0:1], None, ALU.mult),
                [t_krs])
            t_vcp = prog.add("vector", lambda v: v.tensor_copy(
                v_sb[:, i, :], ktok[:, 64:128]), [t_ktok])
            t_kdup = prog.add("vector", lambda v: v.tensor_copy(
                ktok[:, 64:128], ktok[:, 0:64]), [t_khat, t_vcp])
            t_kt2 = prog.dma("sync", lambda y: y.dma_start_transpose(
                ktr2[:], ktok[:]), [t_kdup, kchain_free])
            t_khsT = prog.add("vector", lambda v: v.tensor_scalar(
                khsT[:, i * 128:(i + 1) * 128], ktr2[:], s8col[:], None,
                ALU.mult), [t_kt2, t_s8])
            kchain_free = [t_khat, t_vcp, t_khsT, t_kss]
            t_kv_done[i] = [t_khsT, t_vcp, t_nkhsT]

        def s_qproj():
            nonlocal ring_ctr
            _ = qchain_free
            st_["qslots"] = []
            st_["qmm"] = []
            for hh in range(NH):
                w = min(512, D - hh * 512)
                slot = ring_ctr % NRS
                ring_ctr += 1
                st_["qslots"].append((slot, hh, w))
                t_qmm = None
                for c in range(ND):
                    t_qmm = prog.add("tensor", lambda t, c=c, hh=hh, w=w, slot=slot,
                                     par=par: t.matmul(
                        ring_ap2(slot, 0, 128, 0, w),
                        xnT[:, par, c, :],
                        Wq_sb[:, c, hh * 512:hh * 512 + w],
                        start=(c == 0), stop=False),
                        [t_wq, t_xnT[i], ring_tok[slot], t_qmm])
                t_qmm = prog.add("tensor", lambda t, hh=hh, w=w, slot=slot: t.matmul(
                    ring_ap2(slot, 0, 128, 0, w), mu2row[:, par, :],
                    negc[:, hh * 512:hh * 512 + w], start=False, stop=True),
                    [t_negc, st_["mu2r"], t_qmm])
                st_["qmm"].append(t_qmm)
                tq = prog.add("vector", lambda v, slot=slot, hh=hh, w=w: v.tensor_copy(
                    qraw[:, hh * 512:hh * 512 + w], ring_ap2(slot, 0, 128, 0, w)),
                    [t_qmm, qchain_free])
                ring_tok[slot] = tq
                st_.setdefault("qraw", []).append(tq)
            mu2_free[par] = list(st_["qmm"])
            xnT_free[par] = [st_["qmm"], st_["xnst"]]

        def s_qss():
            nonlocal t_qtmp_free
            t_qraw = st_["qraw"]
            t_qssl = []
            for h in range(H):
                tq = prog.add("vector", lambda v, h=h: v.scalar_tensor_tensor(
                    qtmp[:, h * 64:(h + 1) * 64], qraw[:, h * 64:(h + 1) * 64],
                    1.0, qraw[:, h * 64:(h + 1) * 64], ALU.mult, ALU.mult,
                    accum_out=qss[:, h:h + 1]), [t_qraw, t_qtmp_free])
                t_qssl.append(tq)
            t_qtmp_free = t_qssl
            st_["qssl"] = t_qssl

        def s_qinv():
            t_qlog = prog.add("scalar", lambda s: s.activation(
                qlog[:], qss[:], AF.Ln), st_["qssl"])
            t_invq = prog.add("scalar", lambda s: s.activation(
                invq[:], qlog[:], AF.Exp, scale=-0.5), [t_qlog])
            st_["qhat"] = []
            for h in range(H):
                st_["qhat"].append(prog.add("vector", lambda v, h=h: v.tensor_scalar(
                    qhat[:, h * 64:(h + 1) * 64], qraw[:, h * 64:(h + 1) * 64],
                    invq[:, h:h + 1], None, ALU.mult), [t_invq]))

        def s_qT():
            nonlocal qchain_free
            bp = (i // BQ) % 3
            ic = i % BQ
            tdt = prog.dma("sync", lambda y, bp=bp, ic=ic: y.dma_start_transpose(
                qhatT[:, bp, :, ic * 128:(ic + 1) * 128], qhat[:, 0:D]),
                st_["qhat"] + [qchain_free, blk_tail.get(i // BQ - 3)])
            t_qT[i] = tdt
            qchain_free = [tdt] + st_["qhat"]

        return [s_stats, s_ln, s_xnf, s_mu2xn, s_kv, s_kv2, s_qproj,
                s_qss, s_qinv, s_qT]

    NRS = 5
    ring_tok = [None] * NRS
    ring_tok[4] = t_negc
    ring_ctr = 0

    def ring_ap(s, w=512):
        if s < 4:
            return psim[:, s * 512:s * 512 + w]
        return pwork[:, (s - 4) * 512:(s - 4) * 512 + w]

    def ring_ap2(s, p0, p1, c0, w):
        if s < 4:
            return psim[p0:p1, s * 512 + c0:s * 512 + c0 + w]
        return pwork[p0:p1, (s - 4) * 512 + c0:(s - 4) * 512 + c0 + w]

    attnT_free = [None] * ATS
    attnTn_free = [None] * ATS
    den_chain = [t_negc]
    pairden_free = None
    t_dinvst = {}

    def head_meta(B4, h):
        return dict(B4=B4, h=h, par=h % 2, ts=(B4 * H + h) % ATS,
                    po=(h % 2) * 64,
                    cc=h // 2, bp=B4 % 3, base=B4 * BQ, NC=B4 * BQ + BQ,
                    t_exps=[], t_nes=[], t_av=None, t_dn=None)

    def emit_score_chunk(hd, c):
        nonlocal ring_ctr
        B4, ts_, cc, po, bp, base = (hd["B4"], hd["ts"], hd["cc"], hd["po"],
                                     hd["bp"], hd["base"])
        j0 = max(0, c - base)
        w = BW - j0 * 128
        slot = ring_ctr % NRS
        ring_ctr += 1
        diag = c >= base
        t = prog.add("tensor", lambda tt, c=c, j0=j0, w=w, slot=slot, bp=bp,
                     cc=cc, po=po: tt.matmul(
            ring_ap2(slot, 0, 128, 0, w),
            khsT[po:po + 64, c * 128:(c + 1) * 128],
            qhatT[po:po + 64, bp, cc, j0 * 128:j0 * 128 + w],
            start=True, stop=not diag),
            [[t_qT[base + ib] for ib in range(j0, BQ)], t_kv_done[c],
             ring_tok[slot]])
        if diag:
            msk = maskCT if (c == 0 and B4 == 0) else maskAT
            t = prog.add("tensor", lambda tt, slot=slot, msk=msk: tt.matmul(
                ring_ap2(slot, 0, 128, 0, 128), ident[:], msk[:],
                start=False, stop=True), [t])
        te = prog.add("scalar", lambda s, c=c, j0=j0, w=w, slot=slot,
                      ts_=ts_: s.activation(
            attnT[:, ts_, c, j0 * 128:j0 * 128 + w],
            ring_ap2(slot, 0, 128, 0, w),
            AF.Exp, bias=negC[:, 0:1]),
            [t, t_negC, attnT_free[ts_]])
        ring_tok[slot] = te
        hd["t_exps"].append(te)

    def emit_nulls(hd):
        nonlocal ring_ctr
        ts_, cc, po, bp, base = (hd["ts"], hd["cc"], hd["po"], hd["bp"],
                                 hd["base"])
        for ib in range(BQ):
            slot = ring_ctr % NRS
            ring_ctr += 1
            lqn = qhatT[po:po + 64, bp, cc, ib * 128:(ib + 1) * 128]
            t_ns = prog.add("tensor", lambda tt, slot=slot, lqn=lqn, po=po:
                            tt.matmul(
                ring_ap2(slot, 0, 2, 0, 128), nkhsT[po:po + 64, :], lqn,
                start=True, stop=True),
                [t_qT[base + ib], t_kv_done[base + ib], ring_tok[slot]])
            t_ne = prog.add("scalar", lambda s, slot=slot, ts_=ts_, ib=ib:
                            s.activation(
                attnTn[:, ts_, ib * 128:(ib + 1) * 128],
                ring_ap2(slot, 0, 2, 0, 128), AF.Exp,
                bias=negC[0:2, 0:1]), [t_ns, t_negC, attnTn_free[ts_]])
            ring_tok[slot] = t_ne
            hd["t_nes"].append(t_ne)

    def emit_den_chunk(hd, c):
        p, ts_, base = hd["par"], hd["ts"], hd["base"]
        j0 = max(0, c - base)
        w = BW - j0 * 128
        hd["t_dn"] = prog.add("tensor", lambda tt, c=c, j0=j0, w=w, p=p,
                              ts_=ts_: tt.matmul(
            pwork[0:2, 512 + j0 * 128:512 + j0 * 128 + w],
            dind[:, 2 * p:2 * p + 2],
            attnT[:, ts_, c, j0 * 128:j0 * 128 + w],
            start=(c == 0 and p == 0), stop=False),
            [hd["t_exps"][c], den_chain, hd["t_dn"]])

    def emit_den_null(hd):
        nonlocal den_chain, pairden_free
        B4, ts_, cc, p = hd["B4"], hd["ts"], hd["cc"], hd["par"]
        t = prog.add("tensor", lambda tt, p=p, ts_=ts_: tt.matmul(
            pwork[0:2, 512:512 + BW], nind[0:2, 2 * p:2 * p + 2],
            attnTn[0:2, ts_, :], start=False, stop=(p == 1)),
            [hd["t_nes"], den_chain, hd["t_dn"]])
        if p == 1:
            t_dln = prog.add("scalar", lambda s: s.activation(
                pairden[:, 1, 0:BW], pwork[0:2, 512:512 + BW], AF.Ln),
                [t, pairden_free])
            den_chain = [t_dln]
            t_rec = prog.add("scalar", lambda s: s.activation(
                pairden[:, 0, 0:BW], pairden[:, 1, 0:BW], AF.Exp,
                scale=-1.0), [t_dln])
            t_st = prog.dma("gpsimd", lambda g, cc=cc, B4=B4: g.dma_start(
                out=dinv[:, cc, B4 * BW:(B4 + 1) * BW],
                in_=pairden[:, 0, 0:BW]),
                [t_rec])
            pairden_free = [t_st, t_rec]
            t_dinvst[(cc, B4)] = t_st

    def emit_av_chunk(hd, c):
        B4, po, ts_ = hd["B4"], hd["po"], hd["ts"]
        j0 = max(0, c - BQ * B4)
        w = (BQ - j0) * 128
        hd["t_av"] = prog.add("tensor", lambda t, c=c, po=po, ts_=ts_, j0=j0,
                              w=w: t.matmul(
            pav[po:po + 64, j0 * 128:j0 * 128 + w],
            v_sb[:, c, :],
            attnT[:, ts_, c, j0 * 128:j0 * 128 + w],
            start=(c == 0), stop=False,
            tile_position=(0, po)),
            [hd["t_exps"][c], t_pav_free, hd["t_av"]])

    def emit_av_null(hd):
        nonlocal t_pav_free
        h, po, ts_, B4 = hd["h"], hd["po"], hd["ts"], hd["B4"]
        t_av = prog.add("tensor", lambda t, po=po, ts_=ts_: t.matmul(
            pav[po:po + 64, 0:BW], vnull[:], attnTn[:, ts_, :], start=False,
            stop=True, tile_position=(0, po)), [hd["t_nes"], hd["t_av"],
                                                t_vnull])
        attnT_free[ts_] = t_av
        attnTn_free[ts_] = t_av
        if h % 2 == 1:
            t_pav_free = prog.add("vector", lambda v, B4=B4, cc=h // 2: v.tensor_copy(
                outunT[:, cc, B4 * BW:(B4 + 1) * BW], pav[:, 0:BW]),
                [t_av])
            for ib in range(BQ):
                t_outun[B4 * BQ + ib][h // 2] = t_pav_free
        return t_av

    for i0 in range(min(2 * BQ, NT)):
        for s in stage0_steps(i0):
            s()
    NBK = NT // BQ
    qper = max(1, (len(quant_ops) + NBK * H - 1) // (NBK * H))
    prev_hd = None
    for B4 in range(NBK):
        steps_next = []
        if (B4 + 2) * BQ < NT:
            for i2 in range((B4 + 2) * BQ, (B4 + 3) * BQ):
                steps_next += stage0_steps(i2)
        si = 0
        for h in range(H):
            hd = head_meta(B4, h)
            pNC = prev_hd["NC"] if prev_hd is not None else 0
            for c in range(hd["NC"]):
                emit_score_chunk(hd, c)
                if c < pNC:
                    emit_av_chunk(prev_hd, c)
            emit_nulls(hd)
            if prev_hd is not None:
                for c in range(pNC):
                    emit_den_chunk(prev_hd, c)
                emit_den_null(prev_hd)
                tavp = emit_av_null(prev_hd)
                if prev_hd["h"] == H - 1:
                    blk_tail[prev_hd["B4"]] = tavp
            prev_hd = hd
            for _ in range(3):
                if si < len(steps_next):
                    steps_next[si]()
                    si += 1
            run_quant(1 if si < len(steps_next) else qper + 1)
        while si < len(steps_next):
            steps_next[si]()
            si += 1
    for c in range(prev_hd["NC"]):
        emit_av_chunk(prev_hd, c)
    for c in range(prev_hd["NC"]):
        emit_den_chunk(prev_hd, c)
    emit_den_null(prev_hd)
    blk_tail[prev_hd["B4"]] = emit_av_null(prev_hd)
    run_quant(len(quant_ops))

    t_w1qt = qstate["w1qt_done"]
    t_w2qt = qstate["w2qt_done"]

    ctxA.close()
    prog.fence()

    ctxB = contextlib.ExitStack()
    SB_ = ctxB.enter_context

    def sbb(name, shape, dtype):
        return SB_(nc.sbuf_tensor(name, shape, dtype))

    NB2 = max(1, (NF + 3) // 4)
    NSG = NT // 4 if NT % 4 == 0 else NT // 2
    QSG = NT // NSG
    TW = QSG * 128

    Wout_sb = sbb("Wout_sb", [128, ND, D], BF16)
    W1B = sbb("W1B", [128, ND, F], BF16)
    w2ring = sbb("w2ring", [128, 8, 512], BF16)
    xnTg = sbb("xnTg", [128, 2, ND, TW], BF16)
    h1full = sbb("h1full", [128, NF, TW], BF16)
    y_sb = sbb("y_sb", [128, 2, 512], F32)
    xbufB = sbb("xbufB", [128, 2, 512], F32)
    invdenT = sbb("invdenT", [2, 2, ND, 128], F32)

    t_wout = prog.dma("gpsimd", lambda g: g.dma_start(out=Wout_sb[:], in_=Woutr))
    t_w1b = []
    for c4 in range(0, ND, 2):
        nn = min(2, ND - c4)
        t_w1b.append(prog.dma("scalar" if (c4 // 2) % 2 == 0 else "sync",
                              lambda g, c4=c4, nn=nn: g.dma_start(
            out=W1B[:, c4:c4 + nn, :],
            in_=w1qt_rw[:, c4:c4 + nn, :]), [t_w1qt]))

    t_y_done = [None] * NT * 2
    w2_batch_done = {}
    t_xb_free = [None, None]
    t_ysb_free = [None, None]
    t_xnTg_free = [None, None]
    t_ivT_readers = {0: [], 1: []}
    t_psimA = [ring_tok[0], ring_tok[1], ring_tok[2], ring_tok[3]]
    PM1 = [psim[:, 0:512], psim[:, 512:1024]]
    t_pm1_free = [[t_psimA], [t_psimA]]
    PACC = [psim[:, 1024:1536], psim[:, 1536:2048], pwork[:, 0:512],
            pwork[:, 512:1024]]
    t_pacc_free = [[t_psimA], [t_psimA], [ring_tok[4]], [den_chain]]
    t_bc_free = {}
    bank_chain = {}
    t_h1cons = {}
    t_h1_free = [None] * NF
    t_xg = [None, None]
    CCX_ORDER = [c for pr in zip(range(ND // 2), range(ND // 2, ND))
                 for c in pr] or list(range(ND))

    def den_steps(sg):
        spp = sg % 2
        jlist_ = list(range(QSG * sg, QSG * sg + QSG))
        ds = {}
        steps = []
        for jj, j in enumerate(jlist_):
            def s_load(j=j, spp=spp):
                ds["ivl"] = prog.dma("scalar", lambda g, j=j, spp=spp:
                                     g.dma_start(
                    out=invdenT[:, spp, :, :],
                    in_=dinv[:, :, j * 128:(j + 1) * 128]),
                    [[t_dinvst[(cc2, j // BQ)] for cc2 in range(ND)],
                     t_ivT_readers[spp]])
                t_ivT_readers[spp] = []
            steps.append(s_load)
            for ccx in CCX_ORDER:
                def s_bc(ccx=ccx, j=j, spp=spp):
                    pb = psim[:, ccx * 128:(ccx + 1) * 128]
                    bank = ccx // 4
                    t_bc = prog.add("tensor", lambda t, pb=pb, ccx=ccx, spp=spp:
                                    t.matmul(
                        pb, indic[:], invdenT[:, spp, ccx, :],
                        start=True, stop=True),
                        [ds["ivl"], bank_chain.get(bank), t_pm1_free[bank]])
                    t_ivT_readers[spp].append(t_bc)
                    t_oh = prog.add("vector", lambda v, pb=pb, ccx=ccx, j=j:
                                    v.tensor_tensor(
                        outunT[:, ccx, j * 128:(j + 1) * 128],
                        outunT[:, ccx, j * 128:(j + 1) * 128], pb, ALU.mult),
                        [t_bc, t_outun[j][ccx]])
                    t_bc_free[(spp, ccx)] = t_oh
                    bank_chain[bank] = t_oh
                steps.append(s_bc)

        def s_fin(spp=spp):
            t_pm1_free[0] = [t_pm1_free[0]] + [t_bc_free[(spp, c)]
                                               for c in range(0, min(4, ND))]
            t_pm1_free[1] = [t_pm1_free[1]] + [t_bc_free[(spp, c)]
                                               for c in range(4, ND)]
        steps.append(s_fin)
        return steps

    yi = 0
    for sg in range(NSG):
        sp = sg % 2
        jlist = list(range(QSG * sg, QSG * sg + QSG))
        if sg == 0:
            t_xg[0] = prog.dma("scalar", lambda g: g.dma_start(
                out=xnTg[:, 0, :, :], in_=xnt_rw[:, :, 0:TW]),
                [[t_xnst[j] for j in jlist]])
            for s_ in den_steps(0):
                s_()
        if sg + 1 < NSG:
            jn = list(range(QSG * (sg + 1), QSG * (sg + 1) + QSG))
            t_xg[1 - sp] = prog.dma("scalar", lambda g, sg=sg, sp=sp:
                                    g.dma_start(
                out=xnTg[:, 1 - sp, :, :],
                in_=xnt_rw[:, :, (sg + 1) * TW:(sg + 2) * TW]),
                [[t_xnst[j] for j in jn], t_xnTg_free[1 - sp]])
            dsq = den_steps(sg + 1)
        else:
            dsq = []
        dsi = 0
        for half in range(NH):
            hw_ = min(512, D - half * 512)
            t_w2tok = [None] * NF

            def issue_w2(b, half=half, sg=sg, t_w2tok=t_w2tok):
                nbc = min(4, NF - 4 * b)
                s0 = (4 * b) % 8
                gb = (sg * NH + half) * NB2 + b
                war = w2_batch_done.get(gb - 2)
                tk = prog.dma("scalar" if b % 2 == 0 else "sync",
                              lambda g, b=b, s0=s0, nbc=nbc, half=half:
                              g.dma_start(
                    out=w2ring[:, s0:s0 + nbc, 0:hw_],
                    in_=w2qt_rd[:, 4 * b:4 * b + nbc,
                                half * 512:half * 512 + hw_]),
                    [t_w2qt, war])
                for k in range(nbc):
                    t_w2tok[4 * b + k] = tk

            for b in range(min(2, NB2)):
                issue_w2(b)
            t_mmq = [None] * QSG
            for jj, j in enumerate(jlist):
                for ccx in range(ND):
                    t_mmq[jj] = prog.add("tensor", lambda t, ccx=ccx, j=j,
                                         jj=jj, half=half: t.matmul(
                        PACC[jj][:, 0:hw_],
                        outunT[:, ccx, j * 128:(j + 1) * 128],
                        Wout_sb[:, ccx, half * 512:half * 512 + hw_],
                        start=(ccx == 0), stop=False),
                        [t_bc_free[(sp, ccx)], t_wout, t_pacc_free[jj],
                         t_mmq[jj]])
                t_mmq[jj] = prog.add("tensor", lambda t, jj=jj, half=half: t.matmul(
                    PACC[jj][:, 0:hw_], ones_row_b[:],
                    b2_sb[:, half * 512:half * 512 + hw_], start=False,
                    stop=False), [t_b2, t_mmq[jj]])
            if half == 0:
                def emit_mm1(fc, sg=sg, sp=sp):
                    rot = fc % 2
                    t = None
                    for c in range(ND):
                        t = prog.add("tensor", lambda tt, c=c, fc=fc, rot=rot,
                                     sp=sp: tt.matmul(
                            PM1[rot][:, 0:TW], W1B[:, c, fc * 128:(fc + 1) * 128],
                            xnTg[:, sp, c, :],
                            start=(c == 0), stop=(c == ND - 1)),
                            [t_w1b[c // 2], t_xg[sp], t_pm1_free[rot], t])
                    return t

                t_m1 = [None] * NF
                t_m1[0] = emit_mm1(0)
                if NF > 1:
                    t_m1[1] = emit_mm1(1)
            for fc in range(NF):
                if half == 0:
                    rot = fc % 2
                    t_h1_free[fc] = prog.add("scalar", lambda s, fc=fc, rot=rot:
                                             s.activation(
                        h1full[:, fc, 0:TW], PM1[rot][:, 0:TW], AF.Relu,
                        bias=b1_sb[:, fc:fc + 1]),
                        [t_m1[fc], t_b1, t_h1cons.get(fc)])
                    t_pm1_free[rot] = t_h1_free[fc]
                    if fc + 2 < NF:
                        t_m1[fc + 2] = emit_mm1(fc + 2)
                for jj in range(QSG):
                    t_mmq[jj] = prog.add("tensor", lambda t, fc=fc, jj=jj,
                                         half=half: t.matmul(
                        PACC[jj][:, 0:hw_],
                        h1full[:, fc, jj * 128:(jj + 1) * 128],
                        w2ring[:, fc % 8, 0:hw_],
                        start=False, stop=(fc == NF - 1)),
                        [t_h1_free[fc], t_w2tok[fc], t_mmq[jj]])
                if fc % 4 == 3:
                    gb = (sg * NH + half) * NB2 + fc // 4
                    w2_batch_done[gb] = [t_mmq[jj] for jj in range(QSG)]
                    if fc // 4 + 2 < NB2:
                        issue_w2(fc // 4 + 2)
                if half == NH - 1:
                    t_h1cons[fc] = [t_mmq[jj] for jj in range(QSG)]
                    if NH >= 2:
                        for _ in range(2):
                            if dsi < len(dsq):
                                dsq[dsi]()
                                dsi += 1
            for jj, j in enumerate(jlist):
                yp = yi % 2
                yi += 1
                t_xb = prog.dma("gpsimd", lambda g, j=j, half=half, yp=yp:
                                g.dma_start(
                    out=xbufB[:, yp, 0:hw_],
                    in_=xr[j][:, half * 512:half * 512 + hw_]), [t_xb_free[yp]])
                t_y = prog.add("vector", lambda v, jj=jj, yp=yp, half=half:
                               v.scalar_tensor_tensor(
                    y_sb[:, yp, 0:hw_], PACC[jj][:, 0:hw_], 1.0,
                    xbufB[:, yp, 0:hw_], ALU.mult, ALU.add),
                    [t_mmq[jj], t_xb, t_ysb_free[yp]])
                t_pacc_free[jj] = t_y
                t_xb_free[yp] = t_y
                td = prog.dma("gpsimd" if yp == 0 else "sync",
                              lambda g, j=j, half=half, yp=yp:
                              g.dma_start(
                    out=yr[j][:, half * 512:half * 512 + hw_],
                    in_=y_sb[:, yp, 0:hw_]), [t_y])
                t_ysb_free[yp] = td
                t_y_done[2 * j + half] = td
        while dsi < len(dsq):
            dsq[dsi]()
            dsi += 1
        t_xnTg_free[sp] = [t_m1[fc] for fc in range(NF)]

    prog.add("gpsimd", None, t_y_done)

    with nc.Block() as block:
        prog.emit(nc, block, stack)
    ctxB.close()
    stack.close()
    return nc



_CACHE = {}


def _get_prog(cfg_key):
    if cfg_key not in _CACHE:
        _CACHE[cfg_key] = build_program(FULL_CFG)
    return _CACHE[cfg_key]


def kernel(L0_attn_x, L0_attn_gamma, L0_attn_Wq, L0_attn_Wkv, L0_attn_q_scale,
           L0_attn_k_scale, L0_attn_null_kv, L0_attn_Wout, L0_ffn_W1, L0_ffn_b1,
           L0_ffn_W2, L0_ffn_b2):
    x = np.ascontiguousarray(np.asarray(L0_attn_x, dtype=np.float32))
    B = x.shape[0]
    nc = _get_prog("full")
    consts = make_consts(FULL_CFG)
    shared = {
        "gamma": np.ascontiguousarray(L0_attn_gamma, dtype=np.float32),
        "Wq": np.ascontiguousarray(L0_attn_Wq, dtype=np.float32),
        "Wkv": np.ascontiguousarray(L0_attn_Wkv, dtype=np.float32),
        "q_scale": np.ascontiguousarray(L0_attn_q_scale, dtype=np.float32),
        "k_scale": np.ascontiguousarray(L0_attn_k_scale, dtype=np.float32),
        "null_kv": np.ascontiguousarray(L0_attn_null_kv, dtype=np.float32),
        "Wout": np.ascontiguousarray(L0_attn_Wout, dtype=np.float32),
        "W1": np.ascontiguousarray(L0_ffn_W1, dtype=np.float32),
        "b1": np.ascontiguousarray(L0_ffn_b1, dtype=np.float32),
        "W2": np.ascontiguousarray(L0_ffn_W2, dtype=np.float32),
        "b2": np.ascontiguousarray(L0_ffn_b2, dtype=np.float32),
    }
    shared.update(consts)
    in_maps = [dict(shared, x=np.ascontiguousarray(x[b])) for b in range(B)]
    res = run_bass_kernel_spmd(nc, in_maps, core_ids=list(range(B)))
    return np.stack([r["y"] for r in res.results], axis=0)


# version 35
# speedup vs baseline: 1.2557x; 1.1902x over previous
import contextlib
import numpy as np
import ml_dtypes
import concourse.bass as bass
import concourse.mybir as mybir
from concourse.bass_utils import run_bass_kernel_spmd

F32 = mybir.dt.float32
BF16 = mybir.dt.bfloat16
AF = mybir.ActivationFunctionType
ALU = mybir.AluOpType
AX = mybir.AxisListType

MAGIC = 12582912.0
NEGBIG = -1.0e30
LN_EPS = 1e-5
Q_EPS = 1e-5
SCALE = 8.0

FULL_CFG = dict(T=2048, D=1024, H=16, F=4096, PRELEN=64)



class Op:
    __slots__ = ("eng", "fn", "deps", "dma", "sem_val", "needed", "idx")

    def __init__(self, eng, fn, deps, dma):
        self.eng = eng
        self.fn = fn
        self.deps = deps
        self.dma = dma
        self.sem_val = None
        self.needed = False
        self.idx = None


def _flat(deps):
    out = []

    def rec(d):
        if d is None:
            return
        if isinstance(d, (list, tuple)):
            for e in d:
                rec(e)
        else:
            out.append(d)
    rec(deps)
    return out


class Prog:
    CE = ("tensor", "vector", "scalar", "gpsimd")

    def __init__(self, n_g=8, n_y=8, n_s=16):
        self.ops = []
        self.n_dma = {"gpsimd": n_g, "sync": n_y, "scalar": n_s}
        self.extra = []

    def add(self, eng, fn, deps=()):
        op = Op(eng, fn, _flat(deps) + self.extra, dma=False)
        self.ops.append(op)
        return op

    def dma(self, eng, fn, deps=()):
        op = Op(eng, fn, _flat(deps) + self.extra, dma=True)
        self.ops.append(op)
        return op

    def fence(self):
        last = {}
        dix = {e: 0 for e in self.n_dma}
        for op in self.ops:
            if op.dma:
                last[(op.eng, dix[op.eng] % self.n_dma[op.eng])] = op
                dix[op.eng] += 1
            else:
                last[op.eng] = op
        self.extra = list(last.values())

    def emit(self, nc, block, stack):
        for op in self.ops:
            for d in op.deps:
                d.needed = True
        counters = {}
        dma_idx = {"gpsimd": 0, "sync": 0, "scalar": 0}
        for op in self.ops:
            if op.dma:
                i = dma_idx[op.eng]
                dma_idx[op.eng] = i + 1
                op.idx = i
                op.sem_val = 16 * (i // self.n_dma[op.eng] + 1)
            elif op.needed:
                c = counters.get(op.eng, 0) + 1
                counters[op.eng] = c
                op.sem_val = c

        sems = {e: stack.enter_context(nc.semaphore("sem_" + e)) for e in self.CE}
        dsems = {e: [stack.enter_context(nc.semaphore(f"sem_{e}{i}")) for i in range(n)]
                 for e, n in self.n_dma.items()}
        ops = self.ops
        n_dma = self.n_dma

        def runner(engname):
            def body(eng):
                hw = {}

                def wait(key, handle, val):
                    if hw.get(key, 0) < val:
                        eng.wait_ge(handle, val)
                        hw[key] = val

                for op in ops:
                    if op.eng != engname:
                        continue
                    for d in op.deps:
                        if d.dma:
                            slot = d.idx % n_dma[d.eng]
                            wait((d.eng, slot), dsems[d.eng][slot], d.sem_val)
                        else:
                            if d.eng == engname and engname == "tensor":
                                continue
                            if d.sem_val is None:
                                continue
                            wait((d.eng,), sems[d.eng], d.sem_val)
                    if op.dma:
                        slot = op.idx % n_dma[op.eng]
                        if op.sem_val > 16:
                            wait((op.eng, slot), dsems[op.eng][slot], op.sem_val - 16)
                        op.fn(eng).then_inc(dsems[op.eng][slot], 16)
                    elif op.fn is not None:
                        inst = op.fn(eng)
                        if op.needed:
                            inst.then_inc(sems[op.eng], 1)
            return body

        block.tensor(runner("tensor"))
        block.vector(runner("vector"))
        block.scalar(runner("scalar"))
        block.gpsimd(runner("gpsimd"))
        block.sync(runner("sync"))



def make_consts(cfg):
    P = 128
    pre = cfg["PRELEN"]
    t = np.arange(P)[:, None]
    kk = np.arange(P)[None, :]
    maskA = np.where(kk <= t, 0.0, NEGBIG)
    maskC = np.where((kk <= t) | ((t < pre) & (kk < pre)), 0.0, NEGBIG)
    ind = np.zeros((2, P), np.float32)
    ind[0, 0:64] = 1.0
    ind[1, 64:128] = 1.0
    dind = np.zeros((P, 4), np.float32)
    dind[:, 0] = 1.0
    dind[:, 3] = 1.0
    return {
        "c_maskA": maskA.astype(ml_dtypes.bfloat16),
        "c_maskC": maskC.astype(ml_dtypes.bfloat16),
        "c_maskAT": maskA.T.copy().astype(ml_dtypes.bfloat16),
        "c_maskCT": maskC.T.copy().astype(ml_dtypes.bfloat16),
        "c_dind": dind.astype(ml_dtypes.bfloat16),
        "c_nind": dind[0:2, :].copy().astype(ml_dtypes.bfloat16),
        "c_ident": np.eye(P).astype(ml_dtypes.bfloat16),
        "c_identf": np.eye(P).astype(np.float32),
        "c_ones_row_b": np.ones((1, P), ml_dtypes.bfloat16),
        "c_ones_row_f": np.ones((1, P), np.float32),
        "c_ones_col_b": np.ones((P, 1), ml_dtypes.bfloat16),
        "c_ones_col_f": np.ones((P, 1), np.float32),
        "c_indic": ind,
    }



def build_program(cfg):
    T, D, H, F = cfg["T"], cfg["D"], cfg["H"], cfg["F"]
    NT, ND, NF = T // 128, D // 128, F // 128
    assert H * 64 == D and NT % 2 == 0
    PW = min(F, 1024)
    NW2S = PW // 128
    NQ2 = F // PW
    W2ST = ND * NQ2
    NH = (D + 511) // 512
    BQ = 4 if NT % 4 == 0 else 2
    BW = BQ * 128
    AOFF = []
    _aw = 0
    for ib in range(BQ):
        AOFF.append(_aw)
        _aw += (NT - BQ + ib + 1) * 128 + 4
    AW = _aw

    nc = bass.Bass("TRN2", target_bir_lowering=False)

    dx = nc.dram_tensor("x", [T, D], F32, kind="ExternalInput").ap()
    dgamma = nc.dram_tensor("gamma", [D], F32, kind="ExternalInput").ap()
    dWq = nc.dram_tensor("Wq", [D, D], F32, kind="ExternalInput").ap()
    dWkv = nc.dram_tensor("Wkv", [D, 128], F32, kind="ExternalInput").ap()
    dqs = nc.dram_tensor("q_scale", [64], F32, kind="ExternalInput").ap()
    dks = nc.dram_tensor("k_scale", [64], F32, kind="ExternalInput").ap()
    dnkv = nc.dram_tensor("null_kv", [2, 2, 64], F32, kind="ExternalInput").ap()
    dWout = nc.dram_tensor("Wout", [D, D], F32, kind="ExternalInput").ap()
    dW1 = nc.dram_tensor("W1", [F, D], F32, kind="ExternalInput").ap()
    db1 = nc.dram_tensor("b1", [F], F32, kind="ExternalInput").ap()
    dW2 = nc.dram_tensor("W2", [D, F], F32, kind="ExternalInput").ap()
    db2 = nc.dram_tensor("b2", [D], F32, kind="ExternalInput").ap()
    dY = nc.dram_tensor("y", [T, D], F32, kind="ExternalOutput").ap()
    dw2qt = nc.dram_tensor("w2qt", [F, D], BF16).ap()
    dw1qt = nc.dram_tensor("w1qt", [D, F], BF16).ap()
    dxnt = nc.dram_tensor("xnt", [ND * 128, T], BF16).ap()
    dinv = nc.dram_tensor("dinvt", [2, ND, T], F32).ap()
    dcon = {}
    for n, a in make_consts(cfg).items():
        dt = BF16 if a.dtype == ml_dtypes.bfloat16 else F32
        dcon[n] = nc.dram_tensor(n, list(a.shape), dt, kind="ExternalInput").ap()

    xr = dx.rearrange("(i p) d -> i p d", p=128)
    yr = dY.rearrange("(i p) d -> i p d", p=128)
    Wqr = dWq.rearrange("(c p) e -> p c e", p=128)
    Wkvr = dWkv.rearrange("(c p) e -> p c e", p=128)
    Woutr = dWout.rearrange("(c p) e -> p c e", p=128)
    W1r = dW1.rearrange("(c p) d -> c p d", p=128)
    W2r = dW2.rearrange("(c p) f -> c p f", p=128)
    w2qt_rd = dw2qt.rearrange("(c p) d -> p c d", p=128)
    w2qt_wr = dw2qt.rearrange("(q s p) d -> p q s d", p=128, s=NW2S)
    w1qt_rw = dw1qt.rearrange("(c p) f -> p c f", p=128)
    xnt_rw = dxnt.rearrange("(c p) t -> p c t", p=128)
    gam_r = dgamma.rearrange("(c p) -> p c", p=128)
    b1_r = db1.rearrange("(c p) -> p c", p=128)

    prog = Prog()
    stack = contextlib.ExitStack()
    S = stack.enter_context

    def sb(name, shape, dtype):
        return S(nc.sbuf_tensor(name, shape, dtype))

    psim = S(nc.psum_tensor("psim", [128, 2048], F32))
    ptrb = S(nc.psum_tensor("ptrb", [128, 1024], BF16))
    pav = S(nc.psum_tensor("pav", [128, 512], F32))
    pwork = S(nc.psum_tensor("pwork", [128, 1024], F32))

    outunT = sb("outunT", [128, ND, T], BF16)
    qtmp = sb("qtmp", [128, max(D, PW)], F32)
    maskAT = sb("maskAT", [128, 128], BF16)
    maskCT = sb("maskCT", [128, 128], BF16)
    dind = sb("dind", [128, 4], BF16)
    nind = sb("nind", [2, 4], BF16)
    ident = sb("ident", [128, 128], BF16)
    identf = sb("identf", [128, 128], F32)
    ones_row_b = sb("ones_row_b", [1, 128], BF16)
    ones_row_f = sb("ones_row_f", [1, 128], F32)
    ones_col_b = sb("ones_col_b", [128, 1], BF16)
    ones_col_f = sb("ones_col_f", [128, 1], F32)
    indic = sb("indic", [2, 128], F32)
    b1_sb = sb("b1_sb", [128, NF], F32)
    b2_sb = sb("b2_sb", [1, D], BF16)
    gam_sb = sb("gam_sb", [128, ND], F32)
    s8col = sb("s8col", [128, 1], F32)
    qs_st = sb("qs_st", [128, 1], F32)
    ks_st = sb("ks_st", [128, 1], F32)
    mu2b = sb("mu2b", [128, 2], BF16)
    mu2row = sb("mu2row", [1, 2, 128], BF16)
    negc = sb("negc", [1, D], BF16)
    stat = sb("stat", [128, 2, 8], F32)
    pabs = sb("pabs", [128, NF + W2ST], F32)
    gsc = sb("gsc", [1, 8], F32)
    ginv = sb("ginv", [128, 2], F32)
    gcol = sb("gcol", [128, 2], F32)
    epsc = sb("epsc", [128, 1], F32)
    magc = sb("magc", [128, 1], F32)

    ctxA = contextlib.ExitStack()
    SA = ctxA.enter_context

    def sba(name, shape, dtype):
        return SA(nc.sbuf_tensor(name, shape, dtype))

    xbufA = sba("xbufA", [128, 2, D], F32)
    xnT = sba("xnT", [128, 2, ND, 128], BF16)
    xnbf = sba("xnbf", [128, D], BF16)
    qraw = sba("qraw", [128, D], BF16)
    qss = sba("qss", [128, H], F32)
    qlog = sba("qlog", [128, H], F32)
    invq = sba("invq", [128, H], F32)
    qhat = sba("qhat", [128, D], BF16)
    qhatT = sba("qhatT", [128, 3, ND, BW], BF16)
    ATS = 3
    attnT = sba("attnT", [128, ATS, NT, BW], BF16)
    attnTn = sba("attnTn", [2, ATS, BW], BF16)
    negC = sba("negC", [128, 1], F32)
    pairden = sba("pairden", [2, 2, BW], F32)
    Wq_sb = sba("Wq_sb", [128, ND, D], BF16)
    Wkv_sb = sba("Wkv_sb", [128, ND, 128], BF16)
    khsT = sba("khsT", [128, T], BF16)
    nkhsT = sba("nkhsT", [128, 2], BF16)
    v_sb = sba("v_sb", [128, NT, 64], BF16)
    vnull = sba("vnull", [2, 64], BF16)
    kvT_sb = sba("kvT_sb", [128, 128], BF16)
    ktok = sba("ktok", [128, 128], BF16)
    ktr2 = sba("ktr2", [128, 128], BF16)
    kss = sba("kss", [128, 2], F32)
    krs = sba("krs", [128, 1], F32)
    nst = sba("nst", [2, 64], F32)
    nss = sba("nss", [2, 2], F32)
    nkh = sba("nkh", [2, 64], BF16)
    wstage = sba("wstage", [128, 2, max(D, PW)], F32)
    qsc = sba("qsc", [128, max(D, PW)], F32)
    wqbf = sba("wqbf", [128, 1, max(D, PW)], BF16)
    wtout = sba("wtout", [128, 8, 128], BF16)

    for name, ap_sb in [("c_maskAT", maskAT), ("c_maskCT", maskCT),
                        ("c_dind", dind), ("c_nind", nind), ("c_ident", ident),
                        ("c_identf", identf), ("c_ones_row_b", ones_row_b),
                        ("c_ones_row_f", ones_row_f), ("c_ones_col_b", ones_col_b),
                        ("c_ones_col_f", ones_col_f), ("c_indic", indic)]:
        prog.dma("gpsimd", lambda g, a=ap_sb, d=dcon[name]: g.dma_start(out=a[:], in_=d))
    t_consts = prog.ops[-1]
    t_b1 = prog.dma("gpsimd", lambda g: g.dma_start(
        out=b1_sb[:], in_=b1_r, allow_slow_non_contiguous=True))
    t_b2 = prog.dma("gpsimd", lambda g: g.dma_start(
        out=b2_sb[:], in_=db2.rearrange("(o d) -> o d", o=1)))
    t_gam = prog.dma("gpsimd", lambda g: g.dma_start(
        out=gam_sb[:], in_=gam_r, allow_slow_non_contiguous=True))
    t_qs = [prog.dma("gpsimd", lambda g, o=o: g.dma_start(
        out=qs_st[o * 64:(o + 1) * 64, :], in_=dqs.rearrange("(k o) -> k o", o=1)))
        for o in range(2)]
    t_ks = [prog.dma("gpsimd", lambda g, o=o: g.dma_start(
        out=ks_st[o * 64:(o + 1) * 64, :], in_=dks.rearrange("(k o) -> k o", o=1)))
        for o in range(2)]
    t_s8 = prog.add("vector", lambda v: v.scalar_tensor_tensor(
        s8col[:], qs_st[:], SCALE, ks_st[:], ALU.mult, ALU.mult), [t_qs, t_ks])
    t_eps = prog.add("vector", lambda v: v.memset(epsc[:], LN_EPS))
    t_mag = prog.add("vector", lambda v: v.memset(magc[:], -MAGIC))

    t_wst = [None, None]
    for c in range(ND):
        tl = prog.dma("scalar", lambda g, c=c: g.dma_start(
            out=wstage[:, c % 2, 0:128], in_=Wkvr[:, c, :]), [t_wst[c % 2]])
        t_wst[c % 2] = prog.add("vector", lambda v, c=c: v.tensor_copy(
            Wkv_sb[:, c, :], wstage[:, c % 2, 0:128]), [tl])
    t_wkv = [t_wst[0], t_wst[1]]
    t_wq_chunks = []
    for c in range(ND):
        tl = prog.dma("scalar", lambda g, c=c: g.dma_start(
            out=wstage[:, c % 2, 0:D], in_=Wqr[:, c, :]), [t_wst[c % 2]])
        t_wst[c % 2] = prog.add("vector", lambda v, c=c: v.tensor_scalar(
            Wq_sb[:, c, :], wstage[:, c % 2, 0:D], gam_sb[:, c:c + 1], None,
            ALU.mult), [tl, t_gam])
        t_wq_chunks.append(t_wst[c % 2])
    t_wq = list(t_wq_chunks)

    t_ptr_free = t_consts
    t_pav_free = None
    t_pwork_free = None
    t_qtmp_free = None

    t_negc_parts = []
    for hh in range(NH):
        w = min(512, D - hh * 512)
        t_mm = None
        for c in range(ND):
            t_mm = prog.add("tensor", lambda t, c=c, hh=hh, w=w: t.matmul(
                pwork[0:1, hh * 512:hh * 512 + w], ones_col_b[:],
                Wq_sb[:, c, hh * 512:hh * 512 + w],
                start=(c == 0), stop=(c == ND - 1)),
                [t_wq, t_mm, t_consts])
        t_negc_parts.append(t_mm)
    t_negc = prog.add("vector", lambda v: v.tensor_scalar(
        negc[:], pwork[0:1, 0:D], -1.0 / D, None, ALU.mult), t_negc_parts)

    t_nk_l = prog.dma("gpsimd", lambda g: g.dma_start(out=nst[:], in_=dnkv[0]))
    t_nv_l = prog.dma("gpsimd", lambda g: g.dma_start(out=qtmp[0:2, 64:128], in_=dnkv[1]))
    t_vnull = prog.add("vector", lambda v: v.tensor_copy(vnull[:], qtmp[0:2, 64:128]),
                       [t_nv_l])
    t_qtmp_free = [t_qtmp_free, t_vnull]
    t_nss = prog.add("vector", lambda v: v.scalar_tensor_tensor(
        qtmp[0:2, 0:64], nst[:], 1.0, nst[:], ALU.mult, ALU.mult,
        accum_out=nss[:, 0:1]), [t_nk_l, t_qtmp_free])
    t_qtmp_free = t_nss
    t_nlog = prog.add("scalar", lambda s: s.activation(nss[:, 1:2], nss[:, 0:1],
                                                       AF.Ln), [t_nss])
    t_nrs = prog.add("scalar", lambda s: s.activation(nss[:, 0:1], nss[:, 1:2],
                                                      AF.Exp, scale=-0.5), [t_nlog])
    t_nkh = prog.add("vector", lambda v: v.tensor_scalar(
        nkh[:], nst[:], nss[:, 0:1], None, ALU.mult), [t_nrs])
    t_nkt = prog.add("tensor", lambda t: t.transpose(
        ptrb[0:64, 0:2], nkh[:], ident[0:2, 0:2]), [t_nkh, t_ptr_free])
    t_nkt2 = prog.add("tensor", lambda t: t.transpose(
        ptrb[64:128, 0:2], nkh[:], ident[0:2, 0:2], tile_position=(0, 64)),
        [t_nkh, t_ptr_free])
    t_ptr_free = prog.add("vector", lambda v: v.tensor_scalar(
        nkhsT[:], ptrb[:, 0:2], s8col[:], None, ALU.mult),
        [t_nkt, t_nkt2, t_s8])
    t_nkhsT = t_ptr_free

    t_s8t = prog.add("tensor", lambda t: t.transpose(
        pav[0:1, 384:512], s8col[:], identf[:]), [t_s8, t_pav_free])
    t_cb = prog.add("vector", lambda v: v.reduce_max(
        gsc[:, 6:7], pav[0:1, 384:512], axis=AX.X,
        apply_absolute_value=True), [t_s8t])
    t_cbb = prog.add("tensor", lambda t: t.matmul(
        pav[0:128, 440:441], ones_row_f[:], gsc[:, 6:7], start=True,
        stop=True), [t_cb])
    t_negC = prog.add("vector", lambda v: v.tensor_scalar(
        negC[:], pav[:, 440:441], -1.0, None, ALU.mult), [t_cbb])
    t_pav_free = t_negC

    x_tok = [None] * NT

    def xload_A(i, dep):
        x_tok[i] = prog.dma("gpsimd", lambda g, i=i: g.dma_start(
            out=xbufA[:, i % 2, :], in_=xr[i]), [dep])

    xload_A(0, None)
    xload_A(1, None)

    qstate = dict(wst=t_wst, ptr_free=None, pav_free=None, qtmp_free=None,
                  pabs_done=[], g1=None, wto=[None], wqbf_free=None,
                  qsc_free=[None, None], w1qt_done=[], w2qt_done=[])

    def quant_load(kind, idx, st):
        def go():
            slot = quant_load.slot
            quant_load.slot ^= 1
            st["slot"] = slot
            if kind == "w1":
                src_, wdt = W1r[idx], D
            else:
                dd, q = divmod(idx, NQ2)
                src_, wdt = W2r[dd][:, q * PW:(q + 1) * PW], PW
            st["ld"] = prog.dma("scalar", lambda g: g.dma_start(
                out=wstage[:, slot, 0:wdt], in_=src_), [qstate["wst"][slot]])
        return go
    quant_load.slot = 0

    def quant_abs(kind, idx, st):
        def go():
            slot = st["slot"]
            if kind == "w1":
                wdt, col = D, idx
            else:
                wdt, col = PW, NF + idx
            tr = prog.add("vector", lambda v: v.reduce_sum(
                pabs[:, col:col + 1], wstage[:, slot, 0:wdt], axis=AX.X,
                apply_absolute_value=True), [st["ld"]])
            qstate["wst"][slot] = tr
            qstate["pabs_done"].append(tr)
        return go

    def quant_gamma():
        def go():
            nonlocal ring_ctr
            deps = list(qstate["pabs_done"])
            t_c1 = prog.add("vector", lambda v: v.reduce_sum(
                gcol[:, 0:1], pabs[:, 0:NF], axis=AX.X), deps)
            t_c2 = prog.add("vector", lambda v: v.reduce_sum(
                gcol[:, 1:2], pabs[:, NF:NF + W2ST], axis=AX.X), deps)
            s1 = ring_ctr % NRS
            ring_ctr += 1
            t_tm = prog.add("tensor", lambda t, s1=s1: t.matmul(
                ring_ap2(s1, 0, 1, 0, 2), ones_col_f[:], gcol[:], start=True,
                stop=True), [t_c1, t_c2, ring_tok[s1]])
            t_g = prog.add("vector", lambda v, s1=s1: v.tensor_scalar(
                gsc[:, 0:2], ring_ap2(s1, 0, 1, 0, 2), 1.0 / (F * D), Q_EPS,
                ALU.mult, ALU.add), [t_tm])
            ring_tok[s1] = t_g
            t_r = prog.add("vector", lambda v: v.reciprocal(gsc[:, 2:4],
                                                            gsc[:, 0:2]), [t_g])
            s2 = ring_ctr % NRS
            ring_ctr += 1
            t_bc = prog.add("tensor", lambda t, s2=s2: t.matmul(
                ring_ap2(s2, 0, 128, 0, 2), ones_row_f[:], gsc[:, 2:4],
                start=True, stop=True), [t_r, t_g, ring_tok[s2]])
            t_gv = prog.add("vector", lambda v, s2=s2: v.tensor_copy(
                ginv[:], ring_ap2(s2, 0, 128, 0, 2)), [t_bc])
            ring_tok[s2] = t_gv
            qstate["g1"] = t_gv
        return go

    def quant_pass2c(kind, idx, st):
        if kind == "w1":
            wdt, gc = D, 0
        else:
            wdt, gc = PW, 1
        hw2 = wdt // 2

        def m_t1(hf):
            def go():
                slot = st["slot"]
                a = hf * hw2
                st["t1%d" % hf] = prog.add("vector", lambda v, a=a, slot=slot:
                                           v.tensor_scalar(
                    qsc[:, a:a + hw2], wstage[:, slot, a:a + hw2],
                    ginv[:, gc:gc + 1],
                    MAGIC, ALU.mult, ALU.add), [st["ld"], qstate["g1"],
                                                qstate["qsc_free"][hf]])
            return go

        def m_t2(hf):
            def go():
                slot = st["slot"]
                a = hf * hw2
                t2 = prog.add("scalar", lambda s, a=a, slot=slot: s.add(
                    wstage[:, slot, a:a + hw2], qsc[:, a:a + hw2],
                    magc[:, 0:1]),
                    [st["t1%d" % hf], t_mag])
                qstate["qsc_free"][hf] = t2
                st["t2%d" % hf] = t2
            return go

        def m_t3(hf):
            def go():
                slot = st["slot"]
                a = hf * hw2
                st["t3%d" % hf] = prog.add("vector", lambda v, a=a, slot=slot:
                                           v.tensor_scalar(
                    wqbf[:, 0, a:a + hw2], wstage[:, slot, a:a + hw2],
                    1.0, -1.0,
                    ALU.min, ALU.max), [st["t2%d" % hf], qstate["wqbf_free"]])
            return go

        def m_out():
            def go():
                slot = st["slot"]
                nsub = wdt // 128
                tdt = prog.dma("sync", lambda y, nsub=nsub, wdt=wdt:
                               y.dma_start_transpose(
                    wtout[:, 0:nsub, :], wqbf[:, 0, 0:wdt]),
                    [st["t30"], st["t31"], qstate["wto"][0]])
                qstate["wqbf_free"] = tdt
                if kind == "w1":
                    dst = w1qt_rw[:, :, idx * 128:(idx + 1) * 128]
                else:
                    dd, q = divmod(idx, NQ2)
                    dst = w2qt_wr[:, q, :, dd * 128:(dd + 1) * 128]
                tst = prog.dma("gpsimd", lambda g, dst=dst, nsub=nsub:
                               g.dma_start(out=dst, in_=wtout[:, 0:nsub, :]),
                               [tdt])
                qstate["wto"][0] = tst
                qstate["wst"][st["slot"]] = [st["t30"], st["t31"]]
                (qstate["w1qt_done"] if kind == "w1" else
                 qstate["w2qt_done"]).append(tst)
            return go

        return [m_t1(0), m_t2(0), m_t3(0), m_t1(1), m_t2(1), m_t3(1), m_out()]

    _stages = []
    for kind, n in (("w1", NF), ("w2", W2ST)):
        for i in range(n):
            st = {}
            _stages.append((quant_load(kind, i, st), [quant_abs(kind, i, st)]))
    _stages.append((None, [quant_gamma()]))
    for kind, n in (("w1", NF), ("w2", W2ST)):
        for i in range(n):
            st = {}
            _stages.append((quant_load(kind, i, st), quant_pass2c(kind, i, st)))

    NQS = len(_stages)
    quant_ops = []
    for k in range(NQS + 2):
        if k - 2 >= 0:
            quant_ops.extend(_stages[k - 2][1])
        if k < NQS and _stages[k][0] is not None:
            quant_ops.append(_stages[k][0])
    qi = 0

    def run_quant(n):
        nonlocal qi, t_ptr_free, t_pav_free, t_qtmp_free
        for _ in range(n):
            if qi < len(quant_ops):
                qstate["ptr_free"] = t_ptr_free
                qstate["pav_free"] = t_pav_free
                qstate["qtmp_free"] = t_qtmp_free
                quant_ops[qi]()
                t_ptr_free = qstate["ptr_free"]
                t_pav_free = qstate["pav_free"]
                t_qtmp_free = qstate["qtmp_free"]
                qi += 1

    t_xnT = [None] * NT
    t_xnst = [None] * NT
    t_kv_done = [None] * NT
    t_qT = [None] * NT
    t_outun = [[None] * ND for _ in range(NT)]

    stat_free = [None, None]
    xnT_free = [None, None]
    mu2_free = [None, None]
    mu2row_tok = [None, None]
    xnbf_free = None
    qchain_free = None
    kchain_free = None
    blk_tail = {}

    def stage0_steps(i):
        par = i % 2
        xb = xbufA[:, par, :]
        sl = stat[:, par, :]
        st_ = {}

        def s_stats():
            nonlocal t_qtmp_free
            st_["mu"] = prog.add("vector", lambda v: v.reduce_sum(
                sl[:, 0:1], xb, axis=AX.X), [x_tok[i], stat_free[par]])
            t_sq = prog.add("vector", lambda v: v.scalar_tensor_tensor(
                qtmp[:, 0:D], xb, 1.0, xb, ALU.mult, ALU.mult,
                accum_out=sl[:, 1:2]), [x_tok[i], stat_free[par], t_qtmp_free])
            t_qtmp_free = t_sq
            t_var = prog.add("vector", lambda v: v.scalar_tensor_tensor(
                sl[:, 2:3], sl[:, 0:1], 1.0 / D, sl[:, 0:1], ALU.mult, ALU.mult),
                [st_["mu"]])
            t_var2 = prog.add("vector", lambda v: v.scalar_tensor_tensor(
                sl[:, 3:4], sl[:, 1:2], 1.0, sl[:, 2:3], ALU.bypass,
                ALU.subtract), [t_sq, t_var])
            st_["var2"] = t_var2

        def s_ln():
            t_ln = prog.add("scalar", lambda s: s.activation(
                sl[:, 4:5], sl[:, 3:4], AF.Ln, scale=1.0 / D,
                bias=epsc[:, 0:1]), [st_["var2"], t_eps])
            t_rsig = prog.add("scalar", lambda s: s.activation(
                sl[:, 3:4], sl[:, 4:5], AF.Exp, scale=-0.5), [t_ln])
            st_["nmr"] = prog.add("vector", lambda v: v.scalar_tensor_tensor(
                sl[:, 5:6], sl[:, 0:1], -1.0 / D, sl[:, 3:4], ALU.mult,
                ALU.mult), [t_rsig, st_["mu"]])

        def s_xnf():
            nonlocal xnbf_free
            st_["xnf"] = prog.add("vector", lambda v: v.tensor_scalar(
                xnbf[:], xb, sl[:, 3:4], sl[:, 5:6], ALU.mult, ALU.add),
                [st_["nmr"], xnbf_free])
            st_["mu2"] = prog.add("vector", lambda v: v.reduce_sum(
                sl[:, 6:7], xnbf[:], axis=AX.X), [st_["xnf"]])
            st_["xnbf"] = st_["xnf"]
            stat_free[par] = [st_["xnf"], st_["mu2"]]

        def s_mu2xn():
            nonlocal t_ptr_free, xnbf_free
            t_mu2b = prog.add("vector", lambda v: v.tensor_copy(
                mu2b[:, par:par + 1], sl[:, 6:7]), [st_["mu2"], mu2_free[par]])
            t_mu2t = prog.add("tensor", lambda t: t.transpose(
                ptrb[0:1, 768:896], mu2b[:, par:par + 1], ident[:]),
                [t_mu2b, t_ptr_free])
            st_["mu2r"] = prog.add("vector", lambda v: v.tensor_copy(
                mu2row[:, par, :], ptrb[0:1, 768:896]), [t_mu2t])
            mu2row_tok[par] = st_["mu2r"]
            t_ptr_free = st_["mu2r"]
            tdt = prog.dma("sync", lambda y: y.dma_start_transpose(
                xnT[:, par, :, :], xnbf[:, 0:D]),
                [st_["xnbf"], xnT_free[par]])
            t_xnT[i] = tdt
            tst = prog.dma("gpsimd", lambda g: g.dma_start(
                out=xnt_rw[:, :, i * 128:(i + 1) * 128], in_=xnT[:, par, :, :]),
                [tdt])
            st_["xnst"] = tst
            t_xnst[i] = tst
            xnbf_free = [tdt, st_["mu2"]]
            if i + 2 < NT:
                xload_A(i + 2, [st_["xnf"], t_qtmp_free])

        def s_kv():
            nonlocal ring_ctr, t_ptr_free, kchain_free, t_qtmp_free
            slot = ring_ctr % NRS
            ring_ctr += 1
            t_kvmm = None
            for c in range(ND):
                t_kvmm = prog.add("tensor", lambda t, c=c, slot=slot, par=par:
                                  t.matmul(
                    ring_ap2(slot, 0, 128, 0, 128), Wkv_sb[:, c, :],
                    xnT[:, par, c, :],
                    start=(c == 0), stop=(c == ND - 1)),
                    [t_wkv, t_xnT[i], ring_tok[slot], t_kvmm])
            t_kvev = prog.add("vector", lambda v, slot=slot: v.tensor_copy(
                kvT_sb[:], ring_ap2(slot, 0, 128, 0, 128)), [t_kvmm, kchain_free])
            ring_tok[slot] = t_kvev
            t_ktok = prog.dma("sync", lambda y: y.dma_start_transpose(
                ktok[:], kvT_sb[:]), [t_kvev, kchain_free])
            t_kss = prog.add("vector", lambda v: v.scalar_tensor_tensor(
                qtmp[:, 0:64], ktok[:, 0:64], 1.0, ktok[:, 0:64], ALU.mult,
                ALU.mult, accum_out=kss[:, 0:1]), [t_ktok, t_vnull, t_qtmp_free])
            t_qtmp_free = t_kss
            st_["kss"] = t_kss
            st_["ktok"] = t_ktok

        def s_kv2():
            nonlocal kchain_free
            t_kss = st_["kss"]
            t_ktok = st_["ktok"]
            t_klog = prog.add("scalar", lambda s: s.activation(
                kss[:, 1:2], kss[:, 0:1], AF.Ln), [t_kss])
            t_krs = prog.add("scalar", lambda s: s.activation(
                krs[:, 0:1], kss[:, 1:2], AF.Exp, scale=-0.5), [t_klog])
            t_khat = prog.add("vector", lambda v: v.tensor_scalar(
                ktok[:, 0:64], ktok[:, 0:64], krs[:, 0:1], None, ALU.mult),
                [t_krs])
            t_vcp = prog.add("vector", lambda v: v.tensor_copy(
                v_sb[:, i, :], ktok[:, 64:128]), [t_ktok])
            t_kdup = prog.add("vector", lambda v: v.tensor_copy(
                ktok[:, 64:128], ktok[:, 0:64]), [t_khat, t_vcp])
            t_kt2 = prog.dma("sync", lambda y: y.dma_start_transpose(
                ktr2[:], ktok[:]), [t_kdup, kchain_free])
            t_khsT = prog.add("vector", lambda v: v.tensor_scalar(
                khsT[:, i * 128:(i + 1) * 128], ktr2[:], s8col[:], None,
                ALU.mult), [t_kt2, t_s8])
            kchain_free = [t_khat, t_vcp, t_khsT, t_kss]
            t_kv_done[i] = [t_khsT, t_vcp, t_nkhsT]

        def s_qproj():
            nonlocal ring_ctr
            _ = qchain_free
            st_["qslots"] = []
            st_["qmm"] = []
            for hh in range(NH):
                w = min(512, D - hh * 512)
                slot = ring_ctr % NRS
                ring_ctr += 1
                st_["qslots"].append((slot, hh, w))
                t_qmm = None
                for c in range(ND):
                    t_qmm = prog.add("tensor", lambda t, c=c, hh=hh, w=w, slot=slot,
                                     par=par: t.matmul(
                        ring_ap2(slot, 0, 128, 0, w),
                        xnT[:, par, c, :],
                        Wq_sb[:, c, hh * 512:hh * 512 + w],
                        start=(c == 0), stop=False),
                        [t_wq, t_xnT[i], ring_tok[slot], t_qmm])
                t_qmm = prog.add("tensor", lambda t, hh=hh, w=w, slot=slot: t.matmul(
                    ring_ap2(slot, 0, 128, 0, w), mu2row[:, par, :],
                    negc[:, hh * 512:hh * 512 + w], start=False, stop=True),
                    [t_negc, st_["mu2r"], t_qmm])
                st_["qmm"].append(t_qmm)
                tq = prog.add("vector", lambda v, slot=slot, hh=hh, w=w: v.tensor_copy(
                    qraw[:, hh * 512:hh * 512 + w], ring_ap2(slot, 0, 128, 0, w)),
                    [t_qmm, qchain_free])
                ring_tok[slot] = tq
                st_.setdefault("qraw", []).append(tq)
            mu2_free[par] = list(st_["qmm"])
            xnT_free[par] = [st_["qmm"], st_["xnst"]]

        def s_qss():
            nonlocal t_qtmp_free
            t_qraw = st_["qraw"]
            t_qssl = []
            for h in range(H):
                tq = prog.add("vector", lambda v, h=h: v.scalar_tensor_tensor(
                    qtmp[:, h * 64:(h + 1) * 64], qraw[:, h * 64:(h + 1) * 64],
                    1.0, qraw[:, h * 64:(h + 1) * 64], ALU.mult, ALU.mult,
                    accum_out=qss[:, h:h + 1]), [t_qraw, t_qtmp_free])
                t_qssl.append(tq)
            t_qtmp_free = t_qssl
            st_["qssl"] = t_qssl

        def s_qinv():
            t_qlog = prog.add("scalar", lambda s: s.activation(
                qlog[:], qss[:], AF.Ln), st_["qssl"])
            t_invq = prog.add("scalar", lambda s: s.activation(
                invq[:], qlog[:], AF.Exp, scale=-0.5), [t_qlog])
            st_["qhat"] = []
            for h in range(H):
                st_["qhat"].append(prog.add("vector", lambda v, h=h: v.tensor_scalar(
                    qhat[:, h * 64:(h + 1) * 64], qraw[:, h * 64:(h + 1) * 64],
                    invq[:, h:h + 1], None, ALU.mult), [t_invq]))

        def s_qT():
            nonlocal qchain_free
            bp = (i // BQ) % 3
            ic = i % BQ
            tdt = prog.dma("sync", lambda y, bp=bp, ic=ic: y.dma_start_transpose(
                qhatT[:, bp, :, ic * 128:(ic + 1) * 128], qhat[:, 0:D]),
                st_["qhat"] + [qchain_free, blk_tail.get(i // BQ - 3)])
            t_qT[i] = tdt
            qchain_free = [tdt] + st_["qhat"]

        return [s_stats, s_ln, s_xnf, s_mu2xn, s_kv, s_kv2, s_qproj,
                s_qss, s_qinv, s_qT]

    NRS = 5
    ring_tok = [None] * NRS
    ring_tok[4] = t_negc
    ring_ctr = 0

    def ring_ap(s, w=512):
        if s < 4:
            return psim[:, s * 512:s * 512 + w]
        return pwork[:, (s - 4) * 512:(s - 4) * 512 + w]

    def ring_ap2(s, p0, p1, c0, w):
        if s < 4:
            return psim[p0:p1, s * 512 + c0:s * 512 + c0 + w]
        return pwork[p0:p1, (s - 4) * 512 + c0:(s - 4) * 512 + c0 + w]

    attnT_free = [None] * ATS
    attnTn_free = [None] * ATS
    den_chain = [t_negc]
    pairden_free = None
    t_dinvst = {}

    def head_meta(B4, h):
        return dict(B4=B4, h=h, par=h % 2, ts=(B4 * H + h) % ATS,
                    po=(h % 2) * 64,
                    cc=h // 2, bp=B4 % 3, base=B4 * BQ, NC=B4 * BQ + BQ,
                    t_exps=[], t_nes=[], t_av=None, t_dn=None)

    def emit_score_chunk(hd, c):
        nonlocal ring_ctr
        B4, ts_, cc, po, bp, base = (hd["B4"], hd["ts"], hd["cc"], hd["po"],
                                     hd["bp"], hd["base"])
        j0 = max(0, c - base)
        w = BW - j0 * 128
        slot = ring_ctr % NRS
        ring_ctr += 1
        diag = c >= base
        t = prog.add("tensor", lambda tt, c=c, j0=j0, w=w, slot=slot, bp=bp,
                     cc=cc, po=po: tt.matmul(
            ring_ap2(slot, 0, 128, 0, w),
            khsT[po:po + 64, c * 128:(c + 1) * 128],
            qhatT[po:po + 64, bp, cc, j0 * 128:j0 * 128 + w],
            start=True, stop=not diag),
            [[t_qT[base + ib] for ib in range(j0, BQ)], t_kv_done[c],
             ring_tok[slot]])
        if diag:
            msk = maskCT if (c == 0 and B4 == 0) else maskAT
            t = prog.add("tensor", lambda tt, slot=slot, msk=msk: tt.matmul(
                ring_ap2(slot, 0, 128, 0, 128), ident[:], msk[:],
                start=False, stop=True), [t])
        te = prog.add("scalar", lambda s, c=c, j0=j0, w=w, slot=slot,
                      ts_=ts_: s.activation(
            attnT[:, ts_, c, j0 * 128:j0 * 128 + w],
            ring_ap2(slot, 0, 128, 0, w),
            AF.Exp, bias=negC[:, 0:1]),
            [t, t_negC, attnT_free[ts_]])
        ring_tok[slot] = te
        hd["t_exps"].append(te)

    def emit_nulls(hd):
        nonlocal ring_ctr
        ts_, cc, po, bp, base = (hd["ts"], hd["cc"], hd["po"], hd["bp"],
                                 hd["base"])
        for ib in range(BQ):
            slot = ring_ctr % NRS
            ring_ctr += 1
            lqn = qhatT[po:po + 64, bp, cc, ib * 128:(ib + 1) * 128]
            t_ns = prog.add("tensor", lambda tt, slot=slot, lqn=lqn, po=po:
                            tt.matmul(
                ring_ap2(slot, 0, 2, 0, 128), nkhsT[po:po + 64, :], lqn,
                start=True, stop=True),
                [t_qT[base + ib], t_kv_done[base + ib], ring_tok[slot]])
            t_ne = prog.add("scalar", lambda s, slot=slot, ts_=ts_, ib=ib:
                            s.activation(
                attnTn[:, ts_, ib * 128:(ib + 1) * 128],
                ring_ap2(slot, 0, 2, 0, 128), AF.Exp,
                bias=negC[0:2, 0:1]), [t_ns, t_negC, attnTn_free[ts_]])
            ring_tok[slot] = t_ne
            hd["t_nes"].append(t_ne)

    def emit_den_chunk(hd, c):
        p, ts_, base = hd["par"], hd["ts"], hd["base"]
        j0 = max(0, c - base)
        w = BW - j0 * 128
        hd["t_dn"] = prog.add("tensor", lambda tt, c=c, j0=j0, w=w, p=p,
                              ts_=ts_: tt.matmul(
            pwork[0:2, 512 + j0 * 128:512 + j0 * 128 + w],
            dind[:, 2 * p:2 * p + 2],
            attnT[:, ts_, c, j0 * 128:j0 * 128 + w],
            start=(c == 0 and p == 0), stop=False),
            [hd["t_exps"][c], den_chain, hd["t_dn"]])

    def emit_den_null(hd):
        nonlocal den_chain, pairden_free
        B4, ts_, cc, p = hd["B4"], hd["ts"], hd["cc"], hd["par"]
        t = prog.add("tensor", lambda tt, p=p, ts_=ts_: tt.matmul(
            pwork[0:2, 512:512 + BW], nind[0:2, 2 * p:2 * p + 2],
            attnTn[0:2, ts_, :], start=False, stop=(p == 1)),
            [hd["t_nes"], den_chain, hd["t_dn"]])
        if p == 1:
            t_dln = prog.add("scalar", lambda s: s.activation(
                pairden[:, 1, 0:BW], pwork[0:2, 512:512 + BW], AF.Ln),
                [t, pairden_free])
            den_chain = [t_dln]
            t_rec = prog.add("scalar", lambda s: s.activation(
                pairden[:, 0, 0:BW], pairden[:, 1, 0:BW], AF.Exp,
                scale=-1.0), [t_dln])
            t_st = prog.dma("gpsimd", lambda g, cc=cc, B4=B4: g.dma_start(
                out=dinv[:, cc, B4 * BW:(B4 + 1) * BW],
                in_=pairden[:, 0, 0:BW]),
                [t_rec])
            pairden_free = [t_st, t_rec]
            t_dinvst[(cc, B4)] = t_st

    def emit_av_chunk(hd, c):
        B4, po, ts_ = hd["B4"], hd["po"], hd["ts"]
        j0 = max(0, c - BQ * B4)
        w = (BQ - j0) * 128
        hd["t_av"] = prog.add("tensor", lambda t, c=c, po=po, ts_=ts_, j0=j0,
                              w=w: t.matmul(
            pav[po:po + 64, j0 * 128:j0 * 128 + w],
            v_sb[:, c, :],
            attnT[:, ts_, c, j0 * 128:j0 * 128 + w],
            start=(c == 0), stop=False,
            tile_position=(0, po)),
            [hd["t_exps"][c], t_pav_free, hd["t_av"]])

    def emit_av_null(hd):
        nonlocal t_pav_free
        h, po, ts_, B4 = hd["h"], hd["po"], hd["ts"], hd["B4"]
        t_av = prog.add("tensor", lambda t, po=po, ts_=ts_: t.matmul(
            pav[po:po + 64, 0:BW], vnull[:], attnTn[:, ts_, :], start=False,
            stop=True, tile_position=(0, po)), [hd["t_nes"], hd["t_av"],
                                                t_vnull])
        attnT_free[ts_] = t_av
        attnTn_free[ts_] = t_av
        if h % 2 == 1:
            t_pav_free = prog.add("vector", lambda v, B4=B4, cc=h // 2: v.tensor_copy(
                outunT[:, cc, B4 * BW:(B4 + 1) * BW], pav[:, 0:BW]),
                [t_av])
            for ib in range(BQ):
                t_outun[B4 * BQ + ib][h // 2] = t_pav_free
        return t_av

    for i0 in range(min(2 * BQ, NT)):
        for s in stage0_steps(i0):
            s()
    NBK = NT // BQ
    qper = max(1, (len(quant_ops) + NBK * H - 1) // (NBK * H))
    prev_hd = None
    for B4 in range(NBK):
        steps_next = []
        if (B4 + 2) * BQ < NT:
            for i2 in range((B4 + 2) * BQ, (B4 + 3) * BQ):
                steps_next += stage0_steps(i2)
        si = 0
        for h in range(H):
            hd = head_meta(B4, h)
            pNC = prev_hd["NC"] if prev_hd is not None else 0
            for c in range(hd["NC"]):
                emit_score_chunk(hd, c)
                if c < pNC:
                    emit_av_chunk(prev_hd, c)
                run_quant(1)
            emit_nulls(hd)
            if prev_hd is not None:
                for c in range(pNC):
                    emit_den_chunk(prev_hd, c)
                emit_den_null(prev_hd)
                tavp = emit_av_null(prev_hd)
                if prev_hd["h"] == H - 1:
                    blk_tail[prev_hd["B4"]] = tavp
            prev_hd = hd
            for _ in range(3):
                if si < len(steps_next):
                    steps_next[si]()
                    si += 1

        while si < len(steps_next):
            steps_next[si]()
            si += 1
    for c in range(prev_hd["NC"]):
        emit_av_chunk(prev_hd, c)
    for c in range(prev_hd["NC"]):
        emit_den_chunk(prev_hd, c)
    emit_den_null(prev_hd)
    blk_tail[prev_hd["B4"]] = emit_av_null(prev_hd)
    run_quant(len(quant_ops))

    t_w1qt = qstate["w1qt_done"]
    t_w2qt = qstate["w2qt_done"]

    ctxA.close()
    prog.fence()

    ctxB = contextlib.ExitStack()
    SB_ = ctxB.enter_context

    def sbb(name, shape, dtype):
        return SB_(nc.sbuf_tensor(name, shape, dtype))

    NB2 = max(1, (NF + 3) // 4)
    NSG = NT // 4 if NT % 4 == 0 else NT // 2
    QSG = NT // NSG
    TW = QSG * 128

    Wout_sb = sbb("Wout_sb", [128, ND, D], BF16)
    W1B = sbb("W1B", [128, ND, F], BF16)
    w2ring = sbb("w2ring", [128, 8, 512], BF16)
    xnTg = sbb("xnTg", [128, 2, ND, TW], BF16)
    h1full = sbb("h1full", [128, NF, TW], BF16)
    y_sb = sbb("y_sb", [128, 2, 512], F32)
    xbufB = sbb("xbufB", [128, 2, 512], F32)
    invdenT = sbb("invdenT", [2, 2, ND, 128], F32)

    t_wout = prog.dma("gpsimd", lambda g: g.dma_start(out=Wout_sb[:], in_=Woutr))
    t_w1b = []
    for c4 in range(0, ND, 2):
        nn = min(2, ND - c4)
        t_w1b.append(prog.dma("scalar" if (c4 // 2) % 2 == 0 else "sync",
                              lambda g, c4=c4, nn=nn: g.dma_start(
            out=W1B[:, c4:c4 + nn, :],
            in_=w1qt_rw[:, c4:c4 + nn, :]), [t_w1qt]))

    t_y_done = [None] * NT * 2
    w2_batch_done = {}
    t_xb_free = [None, None]
    t_ysb_free = [None, None]
    t_xnTg_free = [None, None]
    t_ivT_readers = {0: [], 1: []}
    t_psimA = [ring_tok[0], ring_tok[1], ring_tok[2], ring_tok[3]]
    PM1 = [psim[:, 0:512], psim[:, 512:1024]]
    t_pm1_free = [[t_psimA], [t_psimA]]
    PACC = [psim[:, 1024:1536], psim[:, 1536:2048], pwork[:, 0:512],
            pwork[:, 512:1024]]
    t_pacc_free = [[t_psimA], [t_psimA], [ring_tok[4]], [den_chain]]
    t_bc_free = {}
    bank_chain = {}
    t_h1cons = {}
    t_h1_free = [None] * NF
    t_xg = [None, None]
    CCX_ORDER = [c for pr in zip(range(ND // 2), range(ND // 2, ND))
                 for c in pr] or list(range(ND))

    def den_steps(sg):
        spp = sg % 2
        jlist_ = list(range(QSG * sg, QSG * sg + QSG))
        ds = {}
        steps = []
        for jj, j in enumerate(jlist_):
            def s_load(j=j, spp=spp):
                ds["ivl"] = prog.dma("scalar", lambda g, j=j, spp=spp:
                                     g.dma_start(
                    out=invdenT[:, spp, :, :],
                    in_=dinv[:, :, j * 128:(j + 1) * 128]),
                    [[t_dinvst[(cc2, j // BQ)] for cc2 in range(ND)],
                     t_ivT_readers[spp]])
                t_ivT_readers[spp] = []
            steps.append(s_load)
            for ccx in CCX_ORDER:
                def s_bc(ccx=ccx, j=j, spp=spp):
                    pb = psim[:, ccx * 128:(ccx + 1) * 128]
                    bank = ccx // 4
                    t_bc = prog.add("tensor", lambda t, pb=pb, ccx=ccx, spp=spp:
                                    t.matmul(
                        pb, indic[:], invdenT[:, spp, ccx, :],
                        start=True, stop=True),
                        [ds["ivl"], bank_chain.get(bank), t_pm1_free[bank]])
                    t_ivT_readers[spp].append(t_bc)
                    t_oh = prog.add("vector", lambda v, pb=pb, ccx=ccx, j=j:
                                    v.tensor_tensor(
                        outunT[:, ccx, j * 128:(j + 1) * 128],
                        outunT[:, ccx, j * 128:(j + 1) * 128], pb, ALU.mult),
                        [t_bc, t_outun[j][ccx]])
                    t_bc_free[(spp, ccx)] = t_oh
                    bank_chain[bank] = t_oh
                steps.append(s_bc)

        def s_fin(spp=spp):
            t_pm1_free[0] = [t_pm1_free[0]] + [t_bc_free[(spp, c)]
                                               for c in range(0, min(4, ND))]
            t_pm1_free[1] = [t_pm1_free[1]] + [t_bc_free[(spp, c)]
                                               for c in range(4, ND)]
        steps.append(s_fin)
        return steps

    yi = 0
    for sg in range(NSG):
        sp = sg % 2
        jlist = list(range(QSG * sg, QSG * sg + QSG))
        if sg == 0:
            t_xg[0] = prog.dma("scalar", lambda g: g.dma_start(
                out=xnTg[:, 0, :, :], in_=xnt_rw[:, :, 0:TW]),
                [[t_xnst[j] for j in jlist]])
            for s_ in den_steps(0):
                s_()
        if sg + 1 < NSG:
            jn = list(range(QSG * (sg + 1), QSG * (sg + 1) + QSG))
            t_xg[1 - sp] = prog.dma("scalar", lambda g, sg=sg, sp=sp:
                                    g.dma_start(
                out=xnTg[:, 1 - sp, :, :],
                in_=xnt_rw[:, :, (sg + 1) * TW:(sg + 2) * TW]),
                [[t_xnst[j] for j in jn], t_xnTg_free[1 - sp]])
            dsq = den_steps(sg + 1)
        else:
            dsq = []
        dsi = 0
        for half in range(NH):
            hw_ = min(512, D - half * 512)
            t_w2tok = [None] * NF

            def issue_w2(b, half=half, sg=sg, t_w2tok=t_w2tok):
                nbc = min(4, NF - 4 * b)
                s0 = (4 * b) % 8
                gb = (sg * NH + half) * NB2 + b
                war = w2_batch_done.get(gb - 2)
                tk = prog.dma("scalar" if b % 2 == 0 else "sync",
                              lambda g, b=b, s0=s0, nbc=nbc, half=half:
                              g.dma_start(
                    out=w2ring[:, s0:s0 + nbc, 0:hw_],
                    in_=w2qt_rd[:, 4 * b:4 * b + nbc,
                                half * 512:half * 512 + hw_]),
                    [t_w2qt, war])
                for k in range(nbc):
                    t_w2tok[4 * b + k] = tk

            for b in range(min(2, NB2)):
                issue_w2(b)
            t_mmq = [None] * QSG
            for jj, j in enumerate(jlist):
                for ccx in range(ND):
                    t_mmq[jj] = prog.add("tensor", lambda t, ccx=ccx, j=j,
                                         jj=jj, half=half: t.matmul(
                        PACC[jj][:, 0:hw_],
                        outunT[:, ccx, j * 128:(j + 1) * 128],
                        Wout_sb[:, ccx, half * 512:half * 512 + hw_],
                        start=(ccx == 0), stop=False),
                        [t_bc_free[(sp, ccx)], t_wout, t_pacc_free[jj],
                         t_mmq[jj]])
                t_mmq[jj] = prog.add("tensor", lambda t, jj=jj, half=half: t.matmul(
                    PACC[jj][:, 0:hw_], ones_row_b[:],
                    b2_sb[:, half * 512:half * 512 + hw_], start=False,
                    stop=False), [t_b2, t_mmq[jj]])
            if half == 0:
                def emit_mm1(fc, sg=sg, sp=sp):
                    rot = fc % 2
                    t = None
                    for c in range(ND):
                        t = prog.add("tensor", lambda tt, c=c, fc=fc, rot=rot,
                                     sp=sp: tt.matmul(
                            PM1[rot][:, 0:TW], W1B[:, c, fc * 128:(fc + 1) * 128],
                            xnTg[:, sp, c, :],
                            start=(c == 0), stop=(c == ND - 1)),
                            [t_w1b[c // 2], t_xg[sp], t_pm1_free[rot], t])
                    return t

                t_m1 = [None] * NF
                t_m1[0] = emit_mm1(0)
                if NF > 1:
                    t_m1[1] = emit_mm1(1)
            for fc in range(NF):
                if half == 0:
                    rot = fc % 2
                    t_h1_free[fc] = prog.add("scalar", lambda s, fc=fc, rot=rot:
                                             s.activation(
                        h1full[:, fc, 0:TW], PM1[rot][:, 0:TW], AF.Relu,
                        bias=b1_sb[:, fc:fc + 1]),
                        [t_m1[fc], t_b1, t_h1cons.get(fc)])
                    t_pm1_free[rot] = t_h1_free[fc]
                    if fc + 2 < NF:
                        t_m1[fc + 2] = emit_mm1(fc + 2)
                for jj in range(QSG):
                    t_mmq[jj] = prog.add("tensor", lambda t, fc=fc, jj=jj,
                                         half=half: t.matmul(
                        PACC[jj][:, 0:hw_],
                        h1full[:, fc, jj * 128:(jj + 1) * 128],
                        w2ring[:, fc % 8, 0:hw_],
                        start=False, stop=(fc == NF - 1)),
                        [t_h1_free[fc], t_w2tok[fc], t_mmq[jj]])
                if fc % 4 == 3:
                    gb = (sg * NH + half) * NB2 + fc // 4
                    w2_batch_done[gb] = [t_mmq[jj] for jj in range(QSG)]
                    if fc // 4 + 2 < NB2:
                        issue_w2(fc // 4 + 2)
                if half == NH - 1:
                    t_h1cons[fc] = [t_mmq[jj] for jj in range(QSG)]
                    if NH >= 2:
                        for _ in range(2):
                            if dsi < len(dsq):
                                dsq[dsi]()
                                dsi += 1
            for jj, j in enumerate(jlist):
                yp = yi % 2
                yi += 1
                t_xb = prog.dma("gpsimd", lambda g, j=j, half=half, yp=yp:
                                g.dma_start(
                    out=xbufB[:, yp, 0:hw_],
                    in_=xr[j][:, half * 512:half * 512 + hw_]), [t_xb_free[yp]])
                t_y = prog.add("vector", lambda v, jj=jj, yp=yp, half=half:
                               v.scalar_tensor_tensor(
                    y_sb[:, yp, 0:hw_], PACC[jj][:, 0:hw_], 1.0,
                    xbufB[:, yp, 0:hw_], ALU.mult, ALU.add),
                    [t_mmq[jj], t_xb, t_ysb_free[yp]])
                t_pacc_free[jj] = t_y
                t_xb_free[yp] = t_y
                td = prog.dma("gpsimd" if yp == 0 else "sync",
                              lambda g, j=j, half=half, yp=yp:
                              g.dma_start(
                    out=yr[j][:, half * 512:half * 512 + hw_],
                    in_=y_sb[:, yp, 0:hw_]), [t_y])
                t_ysb_free[yp] = td
                t_y_done[2 * j + half] = td
        while dsi < len(dsq):
            dsq[dsi]()
            dsi += 1
        t_xnTg_free[sp] = [t_m1[fc] for fc in range(NF)]

    prog.add("gpsimd", None, t_y_done)

    with nc.Block() as block:
        prog.emit(nc, block, stack)
    ctxB.close()
    stack.close()
    return nc



_CACHE = {}


def _get_prog(cfg_key):
    if cfg_key not in _CACHE:
        _CACHE[cfg_key] = build_program(FULL_CFG)
    return _CACHE[cfg_key]


def kernel(L0_attn_x, L0_attn_gamma, L0_attn_Wq, L0_attn_Wkv, L0_attn_q_scale,
           L0_attn_k_scale, L0_attn_null_kv, L0_attn_Wout, L0_ffn_W1, L0_ffn_b1,
           L0_ffn_W2, L0_ffn_b2):
    x = np.ascontiguousarray(np.asarray(L0_attn_x, dtype=np.float32))
    B = x.shape[0]
    nc = _get_prog("full")
    consts = make_consts(FULL_CFG)
    shared = {
        "gamma": np.ascontiguousarray(L0_attn_gamma, dtype=np.float32),
        "Wq": np.ascontiguousarray(L0_attn_Wq, dtype=np.float32),
        "Wkv": np.ascontiguousarray(L0_attn_Wkv, dtype=np.float32),
        "q_scale": np.ascontiguousarray(L0_attn_q_scale, dtype=np.float32),
        "k_scale": np.ascontiguousarray(L0_attn_k_scale, dtype=np.float32),
        "null_kv": np.ascontiguousarray(L0_attn_null_kv, dtype=np.float32),
        "Wout": np.ascontiguousarray(L0_attn_Wout, dtype=np.float32),
        "W1": np.ascontiguousarray(L0_ffn_W1, dtype=np.float32),
        "b1": np.ascontiguousarray(L0_ffn_b1, dtype=np.float32),
        "W2": np.ascontiguousarray(L0_ffn_W2, dtype=np.float32),
        "b2": np.ascontiguousarray(L0_ffn_b2, dtype=np.float32),
    }
    shared.update(consts)
    in_maps = [dict(shared, x=np.ascontiguousarray(x[b])) for b in range(B)]
    res = run_bass_kernel_spmd(nc, in_maps, core_ids=list(range(B)))
    return np.stack([r["y"] for r in res.results], axis=0)


# version 36
# speedup vs baseline: 1.2951x; 1.0314x over previous
import contextlib
import numpy as np
import ml_dtypes
import concourse.bass as bass
import concourse.mybir as mybir
from concourse.bass_utils import run_bass_kernel_spmd

F32 = mybir.dt.float32
BF16 = mybir.dt.bfloat16
AF = mybir.ActivationFunctionType
ALU = mybir.AluOpType
AX = mybir.AxisListType

MAGIC = 12582912.0
NEGBIG = -1.0e30
LN_EPS = 1e-5
Q_EPS = 1e-5
SCALE = 8.0

FULL_CFG = dict(T=2048, D=1024, H=16, F=4096, PRELEN=64)



class Op:
    __slots__ = ("eng", "fn", "deps", "dma", "sem_val", "needed", "idx")

    def __init__(self, eng, fn, deps, dma):
        self.eng = eng
        self.fn = fn
        self.deps = deps
        self.dma = dma
        self.sem_val = None
        self.needed = False
        self.idx = None


def _flat(deps):
    out = []

    def rec(d):
        if d is None:
            return
        if isinstance(d, (list, tuple)):
            for e in d:
                rec(e)
        else:
            out.append(d)
    rec(deps)
    return out


class Prog:
    CE = ("tensor", "vector", "scalar", "gpsimd")

    def __init__(self, n_g=8, n_y=8, n_s=16):
        self.ops = []
        self.n_dma = {"gpsimd": n_g, "sync": n_y, "scalar": n_s}
        self.extra = []

    def add(self, eng, fn, deps=()):
        op = Op(eng, fn, _flat(deps) + self.extra, dma=False)
        self.ops.append(op)
        return op

    def dma(self, eng, fn, deps=()):
        op = Op(eng, fn, _flat(deps) + self.extra, dma=True)
        self.ops.append(op)
        return op

    def fence(self):
        last = {}
        dix = {e: 0 for e in self.n_dma}
        for op in self.ops:
            if op.dma:
                last[(op.eng, dix[op.eng] % self.n_dma[op.eng])] = op
                dix[op.eng] += 1
            else:
                last[op.eng] = op
        self.extra = list(last.values())

    def emit(self, nc, block, stack):
        for op in self.ops:
            for d in op.deps:
                d.needed = True
        counters = {}
        dma_idx = {"gpsimd": 0, "sync": 0, "scalar": 0}
        for op in self.ops:
            if op.dma:
                i = dma_idx[op.eng]
                dma_idx[op.eng] = i + 1
                op.idx = i
                op.sem_val = 16 * (i // self.n_dma[op.eng] + 1)
            elif op.needed:
                c = counters.get(op.eng, 0) + 1
                counters[op.eng] = c
                op.sem_val = c

        sems = {e: stack.enter_context(nc.semaphore("sem_" + e)) for e in self.CE}
        dsems = {e: [stack.enter_context(nc.semaphore(f"sem_{e}{i}")) for i in range(n)]
                 for e, n in self.n_dma.items()}
        ops = self.ops
        n_dma = self.n_dma

        def runner(engname):
            def body(eng):
                hw = {}

                def wait(key, handle, val):
                    if hw.get(key, 0) < val:
                        eng.wait_ge(handle, val)
                        hw[key] = val

                for op in ops:
                    if op.eng != engname:
                        continue
                    for d in op.deps:
                        if d.dma:
                            slot = d.idx % n_dma[d.eng]
                            wait((d.eng, slot), dsems[d.eng][slot], d.sem_val)
                        else:
                            if d.eng == engname and engname == "tensor":
                                continue
                            if d.sem_val is None:
                                continue
                            wait((d.eng,), sems[d.eng], d.sem_val)
                    if op.dma:
                        slot = op.idx % n_dma[op.eng]
                        if op.sem_val > 16:
                            wait((op.eng, slot), dsems[op.eng][slot], op.sem_val - 16)
                        op.fn(eng).then_inc(dsems[op.eng][slot], 16)
                    elif op.fn is not None:
                        inst = op.fn(eng)
                        if op.needed:
                            inst.then_inc(sems[op.eng], 1)
            return body

        block.tensor(runner("tensor"))
        block.vector(runner("vector"))
        block.scalar(runner("scalar"))
        block.gpsimd(runner("gpsimd"))
        block.sync(runner("sync"))



def make_consts(cfg):
    P = 128
    pre = cfg["PRELEN"]
    t = np.arange(P)[:, None]
    kk = np.arange(P)[None, :]
    maskA = np.where(kk <= t, 0.0, NEGBIG)
    maskC = np.where((kk <= t) | ((t < pre) & (kk < pre)), 0.0, NEGBIG)
    ind = np.zeros((2, P), np.float32)
    ind[0, 0:64] = 1.0
    ind[1, 64:128] = 1.0
    dind = np.zeros((P, 4), np.float32)
    dind[:, 0] = 1.0
    dind[:, 3] = 1.0
    return {
        "c_maskA": maskA.astype(ml_dtypes.bfloat16),
        "c_maskC": maskC.astype(ml_dtypes.bfloat16),
        "c_maskAT": maskA.T.copy().astype(ml_dtypes.bfloat16),
        "c_maskCT": maskC.T.copy().astype(ml_dtypes.bfloat16),
        "c_dind": dind.astype(ml_dtypes.bfloat16),
        "c_nind": dind[0:2, :].copy().astype(ml_dtypes.bfloat16),
        "c_ident": np.eye(P).astype(ml_dtypes.bfloat16),
        "c_identf": np.eye(P).astype(np.float32),
        "c_ones_row_b": np.ones((1, P), ml_dtypes.bfloat16),
        "c_ones_row_f": np.ones((1, P), np.float32),
        "c_ones_col_b": np.ones((P, 1), ml_dtypes.bfloat16),
        "c_ones_col_f": np.ones((P, 1), np.float32),
        "c_indic": ind,
    }



def build_program(cfg):
    T, D, H, F = cfg["T"], cfg["D"], cfg["H"], cfg["F"]
    NT, ND, NF = T // 128, D // 128, F // 128
    assert H * 64 == D and NT % 2 == 0
    PW = min(F, 1024)
    NW2S = PW // 128
    NQ2 = F // PW
    W2ST = ND * NQ2
    NH = (D + 511) // 512
    BQ = 4 if NT % 4 == 0 else 2
    BW = BQ * 128
    AOFF = []
    _aw = 0
    for ib in range(BQ):
        AOFF.append(_aw)
        _aw += (NT - BQ + ib + 1) * 128 + 4
    AW = _aw

    nc = bass.Bass("TRN2", target_bir_lowering=False)

    dx = nc.dram_tensor("x", [T, D], F32, kind="ExternalInput").ap()
    dgamma = nc.dram_tensor("gamma", [D], F32, kind="ExternalInput").ap()
    dWq = nc.dram_tensor("Wq", [D, D], F32, kind="ExternalInput").ap()
    dWkv = nc.dram_tensor("Wkv", [D, 128], F32, kind="ExternalInput").ap()
    dqs = nc.dram_tensor("q_scale", [64], F32, kind="ExternalInput").ap()
    dks = nc.dram_tensor("k_scale", [64], F32, kind="ExternalInput").ap()
    dnkv = nc.dram_tensor("null_kv", [2, 2, 64], F32, kind="ExternalInput").ap()
    dWout = nc.dram_tensor("Wout", [D, D], F32, kind="ExternalInput").ap()
    dW1 = nc.dram_tensor("W1", [F, D], F32, kind="ExternalInput").ap()
    db1 = nc.dram_tensor("b1", [F], F32, kind="ExternalInput").ap()
    dW2 = nc.dram_tensor("W2", [D, F], F32, kind="ExternalInput").ap()
    db2 = nc.dram_tensor("b2", [D], F32, kind="ExternalInput").ap()
    dY = nc.dram_tensor("y", [T, D], F32, kind="ExternalOutput").ap()
    dw2qt = nc.dram_tensor("w2qt", [F, D], BF16).ap()
    dw1qt = nc.dram_tensor("w1qt", [D, F], BF16).ap()
    dxnt = nc.dram_tensor("xnt", [ND * 128, T], BF16).ap()
    dinv = nc.dram_tensor("dinvt", [2, ND, T], F32).ap()
    dcon = {}
    for n, a in make_consts(cfg).items():
        dt = BF16 if a.dtype == ml_dtypes.bfloat16 else F32
        dcon[n] = nc.dram_tensor(n, list(a.shape), dt, kind="ExternalInput").ap()

    xr = dx.rearrange("(i p) d -> i p d", p=128)
    yr = dY.rearrange("(i p) d -> i p d", p=128)
    Wqr = dWq.rearrange("(c p) e -> p c e", p=128)
    Wkvr = dWkv.rearrange("(c p) e -> p c e", p=128)
    Woutr = dWout.rearrange("(c p) e -> p c e", p=128)
    W1r = dW1.rearrange("(c p) d -> c p d", p=128)
    W2r = dW2.rearrange("(c p) f -> c p f", p=128)
    w2qt_rd = dw2qt.rearrange("(c p) d -> p c d", p=128)
    w2qt_wr = dw2qt.rearrange("(q s p) d -> p q s d", p=128, s=NW2S)
    w1qt_rw = dw1qt.rearrange("(c p) f -> p c f", p=128)
    xnt_rw = dxnt.rearrange("(c p) t -> p c t", p=128)
    gam_r = dgamma.rearrange("(c p) -> p c", p=128)
    b1_r = db1.rearrange("(c p) -> p c", p=128)

    prog = Prog()
    stack = contextlib.ExitStack()
    S = stack.enter_context

    def sb(name, shape, dtype):
        return S(nc.sbuf_tensor(name, shape, dtype))

    psim = S(nc.psum_tensor("psim", [128, 2048], F32))
    ptrb = S(nc.psum_tensor("ptrb", [128, 1024], BF16))
    pav = S(nc.psum_tensor("pav", [128, 512], F32))
    pwork = S(nc.psum_tensor("pwork", [128, 1024], F32))

    outunT = sb("outunT", [128, ND, T], BF16)
    qtmp = sb("qtmp", [128, max(D, PW)], F32)
    maskAT = sb("maskAT", [128, 128], BF16)
    maskCT = sb("maskCT", [128, 128], BF16)
    dind = sb("dind", [128, 4], BF16)
    nind = sb("nind", [2, 4], BF16)
    ident = sb("ident", [128, 128], BF16)
    identf = sb("identf", [128, 128], F32)
    ones_row_b = sb("ones_row_b", [1, 128], BF16)
    ones_row_f = sb("ones_row_f", [1, 128], F32)
    ones_col_b = sb("ones_col_b", [128, 1], BF16)
    ones_col_f = sb("ones_col_f", [128, 1], F32)
    indic = sb("indic", [2, 128], F32)
    b1_sb = sb("b1_sb", [128, NF], F32)
    b2_sb = sb("b2_sb", [1, D], BF16)
    gam_sb = sb("gam_sb", [128, ND], F32)
    s8col = sb("s8col", [128, 1], F32)
    qs_st = sb("qs_st", [128, 1], F32)
    ks_st = sb("ks_st", [128, 1], F32)
    mu2b = sb("mu2b", [128, 2], BF16)
    mu2row = sb("mu2row", [1, 2, 128], BF16)
    negc = sb("negc", [1, D], BF16)
    stat = sb("stat", [128, 2, 8], F32)
    pabs = sb("pabs", [128, NF + W2ST], F32)
    gsc = sb("gsc", [1, 8], F32)
    ginv = sb("ginv", [128, 2], F32)
    gcol = sb("gcol", [128, 2], F32)
    epsc = sb("epsc", [128, 1], F32)
    magc = sb("magc", [128, 1], F32)

    ctxA = contextlib.ExitStack()
    SA = ctxA.enter_context

    def sba(name, shape, dtype):
        return SA(nc.sbuf_tensor(name, shape, dtype))

    xbufA = sba("xbufA", [128, 2, D], F32)
    xnT = sba("xnT", [128, 2, ND, 128], BF16)
    xnbf = sba("xnbf", [128, D], BF16)
    qraw = sba("qraw", [128, D], BF16)
    qss = sba("qss", [128, H], F32)
    qlog = sba("qlog", [128, H], F32)
    invq = sba("invq", [128, H], F32)
    qhat = sba("qhat", [128, D], BF16)
    qhatT = sba("qhatT", [128, 3, ND, BW], BF16)
    ATS = 3
    attnT = sba("attnT", [128, ATS, NT, BW], BF16)
    attnTn = sba("attnTn", [2, ATS, BW], BF16)
    negC = sba("negC", [128, 1], F32)
    pairden = sba("pairden", [2, 2, BW], F32)
    Wq_sb = sba("Wq_sb", [128, ND, D], BF16)
    Wkv_sb = sba("Wkv_sb", [128, ND, 128], BF16)
    khsT = sba("khsT", [128, T], BF16)
    nkhsT = sba("nkhsT", [128, 2], BF16)
    v_sb = sba("v_sb", [128, NT, 64], BF16)
    vnull = sba("vnull", [2, 64], BF16)
    kvT_sb = sba("kvT_sb", [128, 128], BF16)
    ktok = sba("ktok", [128, 128], BF16)
    ktr2 = sba("ktr2", [128, 128], BF16)
    kss = sba("kss", [128, 2], F32)
    krs = sba("krs", [128, 1], F32)
    nst = sba("nst", [2, 64], F32)
    nss = sba("nss", [2, 2], F32)
    nkh = sba("nkh", [2, 64], BF16)
    wstage = sba("wstage", [128, 2, max(D, PW)], F32)
    qsc = sba("qsc", [128, max(D, PW)], F32)
    wqbf = sba("wqbf", [128, 1, max(D, PW)], BF16)
    wtout = sba("wtout", [128, 8, 128], BF16)

    for name, ap_sb in [("c_maskAT", maskAT), ("c_maskCT", maskCT),
                        ("c_dind", dind), ("c_nind", nind), ("c_ident", ident),
                        ("c_identf", identf), ("c_ones_row_b", ones_row_b),
                        ("c_ones_row_f", ones_row_f), ("c_ones_col_b", ones_col_b),
                        ("c_ones_col_f", ones_col_f), ("c_indic", indic)]:
        prog.dma("gpsimd", lambda g, a=ap_sb, d=dcon[name]: g.dma_start(out=a[:], in_=d))
    t_consts = prog.ops[-1]
    t_b1 = prog.dma("gpsimd", lambda g: g.dma_start(
        out=b1_sb[:], in_=b1_r, allow_slow_non_contiguous=True))
    t_b2 = prog.dma("gpsimd", lambda g: g.dma_start(
        out=b2_sb[:], in_=db2.rearrange("(o d) -> o d", o=1)))
    t_gam = prog.dma("gpsimd", lambda g: g.dma_start(
        out=gam_sb[:], in_=gam_r, allow_slow_non_contiguous=True))
    t_qs = [prog.dma("gpsimd", lambda g, o=o: g.dma_start(
        out=qs_st[o * 64:(o + 1) * 64, :], in_=dqs.rearrange("(k o) -> k o", o=1)))
        for o in range(2)]
    t_ks = [prog.dma("gpsimd", lambda g, o=o: g.dma_start(
        out=ks_st[o * 64:(o + 1) * 64, :], in_=dks.rearrange("(k o) -> k o", o=1)))
        for o in range(2)]
    t_s8 = prog.add("vector", lambda v: v.scalar_tensor_tensor(
        s8col[:], qs_st[:], SCALE, ks_st[:], ALU.mult, ALU.mult), [t_qs, t_ks])
    t_eps = prog.add("vector", lambda v: v.memset(epsc[:], LN_EPS))
    t_mag = prog.add("vector", lambda v: v.memset(magc[:], -MAGIC))

    t_wst = [None, None]
    for c in range(ND):
        tl = prog.dma("scalar", lambda g, c=c: g.dma_start(
            out=wstage[:, c % 2, 0:128], in_=Wkvr[:, c, :]), [t_wst[c % 2]])
        t_wst[c % 2] = prog.add("vector", lambda v, c=c: v.tensor_copy(
            Wkv_sb[:, c, :], wstage[:, c % 2, 0:128]), [tl])
    t_wkv = [t_wst[0], t_wst[1]]
    t_wq_chunks = []
    for c in range(ND):
        tl = prog.dma("scalar", lambda g, c=c: g.dma_start(
            out=wstage[:, c % 2, 0:D], in_=Wqr[:, c, :]), [t_wst[c % 2]])
        t_wst[c % 2] = prog.add("vector", lambda v, c=c: v.tensor_scalar(
            Wq_sb[:, c, :], wstage[:, c % 2, 0:D], gam_sb[:, c:c + 1], None,
            ALU.mult), [tl, t_gam])
        t_wq_chunks.append(t_wst[c % 2])
    t_wq = list(t_wq_chunks)

    t_ptr_free = t_consts
    t_pav_free = None
    t_pwork_free = None
    t_qtmp_free = None

    t_negc_parts = []
    for hh in range(NH):
        w = min(512, D - hh * 512)
        t_mm = None
        for c in range(ND):
            t_mm = prog.add("tensor", lambda t, c=c, hh=hh, w=w: t.matmul(
                pwork[0:1, hh * 512:hh * 512 + w], ones_col_b[:],
                Wq_sb[:, c, hh * 512:hh * 512 + w],
                start=(c == 0), stop=(c == ND - 1)),
                [t_wq, t_mm, t_consts])
        t_negc_parts.append(t_mm)
    t_negc = prog.add("vector", lambda v: v.tensor_scalar(
        negc[:], pwork[0:1, 0:D], -1.0 / D, None, ALU.mult), t_negc_parts)

    t_nk_l = prog.dma("gpsimd", lambda g: g.dma_start(out=nst[:], in_=dnkv[0]))
    t_nv_l = prog.dma("gpsimd", lambda g: g.dma_start(out=qtmp[0:2, 64:128], in_=dnkv[1]))
    t_vnull = prog.add("vector", lambda v: v.tensor_copy(vnull[:], qtmp[0:2, 64:128]),
                       [t_nv_l])
    t_qtmp_free = [t_qtmp_free, t_vnull]
    t_nss = prog.add("vector", lambda v: v.scalar_tensor_tensor(
        qtmp[0:2, 0:64], nst[:], 1.0, nst[:], ALU.mult, ALU.mult,
        accum_out=nss[:, 0:1]), [t_nk_l, t_qtmp_free])
    t_qtmp_free = t_nss
    t_nlog = prog.add("scalar", lambda s: s.activation(nss[:, 1:2], nss[:, 0:1],
                                                       AF.Ln), [t_nss])
    t_nrs = prog.add("scalar", lambda s: s.activation(nss[:, 0:1], nss[:, 1:2],
                                                      AF.Exp, scale=-0.5), [t_nlog])
    t_nkh = prog.add("vector", lambda v: v.tensor_scalar(
        nkh[:], nst[:], nss[:, 0:1], None, ALU.mult), [t_nrs])
    t_nkt = prog.add("tensor", lambda t: t.transpose(
        ptrb[0:64, 0:2], nkh[:], ident[0:2, 0:2]), [t_nkh, t_ptr_free])
    t_nkt2 = prog.add("tensor", lambda t: t.transpose(
        ptrb[64:128, 0:2], nkh[:], ident[0:2, 0:2], tile_position=(0, 64)),
        [t_nkh, t_ptr_free])
    t_ptr_free = prog.add("vector", lambda v: v.tensor_scalar(
        nkhsT[:], ptrb[:, 0:2], s8col[:], None, ALU.mult),
        [t_nkt, t_nkt2, t_s8])
    t_nkhsT = t_ptr_free

    t_s8t = prog.add("tensor", lambda t: t.transpose(
        pav[0:1, 384:512], s8col[:], identf[:]), [t_s8, t_pav_free])
    t_cb = prog.add("vector", lambda v: v.reduce_max(
        gsc[:, 6:7], pav[0:1, 384:512], axis=AX.X,
        apply_absolute_value=True), [t_s8t])
    t_cbb = prog.add("tensor", lambda t: t.matmul(
        pav[0:128, 440:441], ones_row_f[:], gsc[:, 6:7], start=True,
        stop=True), [t_cb])
    t_negC = prog.add("vector", lambda v: v.tensor_scalar(
        negC[:], pav[:, 440:441], -1.0, None, ALU.mult), [t_cbb])
    t_pav_free = t_negC

    x_tok = [None] * NT

    def xload_A(i, dep):
        x_tok[i] = prog.dma("gpsimd", lambda g, i=i: g.dma_start(
            out=xbufA[:, i % 2, :], in_=xr[i]), [dep])

    xload_A(0, None)
    xload_A(1, None)

    qstate = dict(wst=t_wst, ptr_free=None, pav_free=None, qtmp_free=None,
                  pabs_done=[], g1=None, wto=[None], wqbf_free=None,
                  qsc_free=[None, None], w1qt_done=[], w2qt_done=[])

    def quant_load(kind, idx, st):
        def go():
            slot = quant_load.slot
            quant_load.slot ^= 1
            st["slot"] = slot
            if kind == "w1":
                src_, wdt = W1r[idx], D
            else:
                dd, q = divmod(idx, NQ2)
                src_, wdt = W2r[dd][:, q * PW:(q + 1) * PW], PW
            st["ld"] = prog.dma("scalar", lambda g: g.dma_start(
                out=wstage[:, slot, 0:wdt], in_=src_), [qstate["wst"][slot]])
        return go
    quant_load.slot = 0

    def quant_abs(kind, idx, st):
        def go():
            slot = st["slot"]
            if kind == "w1":
                wdt, col = D, idx
            else:
                wdt, col = PW, NF + idx
            tr = prog.add("vector", lambda v: v.reduce_sum(
                pabs[:, col:col + 1], wstage[:, slot, 0:wdt], axis=AX.X,
                apply_absolute_value=True), [st["ld"]])
            qstate["wst"][slot] = tr
            qstate["pabs_done"].append(tr)
        return go

    def quant_gamma():
        def go():
            nonlocal ring_ctr
            deps = list(qstate["pabs_done"])
            t_c1 = prog.add("vector", lambda v: v.reduce_sum(
                gcol[:, 0:1], pabs[:, 0:NF], axis=AX.X), deps)
            t_c2 = prog.add("vector", lambda v: v.reduce_sum(
                gcol[:, 1:2], pabs[:, NF:NF + W2ST], axis=AX.X), deps)
            s1 = ring_ctr % NRS
            ring_ctr += 1
            t_tm = prog.add("tensor", lambda t, s1=s1: t.matmul(
                ring_ap2(s1, 0, 1, 0, 2), ones_col_f[:], gcol[:], start=True,
                stop=True), [t_c1, t_c2, ring_tok[s1]])
            t_g = prog.add("vector", lambda v, s1=s1: v.tensor_scalar(
                gsc[:, 0:2], ring_ap2(s1, 0, 1, 0, 2), 1.0 / (F * D), Q_EPS,
                ALU.mult, ALU.add), [t_tm])
            ring_tok[s1] = t_g
            t_r = prog.add("vector", lambda v: v.reciprocal(gsc[:, 2:4],
                                                            gsc[:, 0:2]), [t_g])
            s2 = ring_ctr % NRS
            ring_ctr += 1
            t_bc = prog.add("tensor", lambda t, s2=s2: t.matmul(
                ring_ap2(s2, 0, 128, 0, 2), ones_row_f[:], gsc[:, 2:4],
                start=True, stop=True), [t_r, t_g, ring_tok[s2]])
            t_gv = prog.add("vector", lambda v, s2=s2: v.tensor_copy(
                ginv[:], ring_ap2(s2, 0, 128, 0, 2)), [t_bc])
            ring_tok[s2] = t_gv
            qstate["g1"] = t_gv
        return go

    def quant_pass2c(kind, idx, st):
        if kind == "w1":
            wdt, gc = D, 0
        else:
            wdt, gc = PW, 1
        hw2 = wdt // 2

        def m_t1(hf):
            def go():
                slot = st["slot"]
                a = hf * hw2
                st["t1%d" % hf] = prog.add("vector", lambda v, a=a, slot=slot:
                                           v.tensor_scalar(
                    qsc[:, a:a + hw2], wstage[:, slot, a:a + hw2],
                    ginv[:, gc:gc + 1],
                    MAGIC, ALU.mult, ALU.add), [st["ld"], qstate["g1"],
                                                qstate["qsc_free"][hf]])
            return go

        def m_t2(hf):
            def go():
                slot = st["slot"]
                a = hf * hw2
                t2 = prog.add("scalar", lambda s, a=a, slot=slot: s.add(
                    wstage[:, slot, a:a + hw2], qsc[:, a:a + hw2],
                    magc[:, 0:1]),
                    [st["t1%d" % hf], t_mag])
                qstate["qsc_free"][hf] = t2
                st["t2%d" % hf] = t2
            return go

        def m_t3(hf):
            def go():
                slot = st["slot"]
                a = hf * hw2
                st["t3%d" % hf] = prog.add("vector", lambda v, a=a, slot=slot:
                                           v.tensor_scalar(
                    wqbf[:, 0, a:a + hw2], wstage[:, slot, a:a + hw2],
                    1.0, -1.0,
                    ALU.min, ALU.max), [st["t2%d" % hf], qstate["wqbf_free"]])
            return go

        def m_out():
            def go():
                slot = st["slot"]
                nsub = wdt // 128
                tdt = prog.dma("sync", lambda y, nsub=nsub, wdt=wdt:
                               y.dma_start_transpose(
                    wtout[:, 0:nsub, :], wqbf[:, 0, 0:wdt]),
                    [st["t30"], st["t31"], qstate["wto"][0]])
                qstate["wqbf_free"] = tdt
                if kind == "w1":
                    dst = w1qt_rw[:, :, idx * 128:(idx + 1) * 128]
                else:
                    dd, q = divmod(idx, NQ2)
                    dst = w2qt_wr[:, q, :, dd * 128:(dd + 1) * 128]
                tst = prog.dma("gpsimd", lambda g, dst=dst, nsub=nsub:
                               g.dma_start(out=dst, in_=wtout[:, 0:nsub, :]),
                               [tdt])
                qstate["wto"][0] = tst
                qstate["wst"][st["slot"]] = [st["t30"], st["t31"]]
                (qstate["w1qt_done"] if kind == "w1" else
                 qstate["w2qt_done"]).append(tst)
            return go

        return [m_t1(0), m_t2(0), m_t3(0), m_t1(1), m_t2(1), m_t3(1), m_out()]

    _stages = []
    for kind, n in (("w1", NF), ("w2", W2ST)):
        for i in range(n):
            st = {}
            _stages.append((quant_load(kind, i, st), [quant_abs(kind, i, st)]))
    _stages.append((None, [quant_gamma()]))
    for kind, n in (("w1", NF), ("w2", W2ST)):
        for i in range(n):
            st = {}
            _stages.append((quant_load(kind, i, st), quant_pass2c(kind, i, st)))

    NQS = len(_stages)
    quant_ops = []
    for k in range(NQS + 2):
        if k - 2 >= 0:
            quant_ops.extend(_stages[k - 2][1])
        if k < NQS and _stages[k][0] is not None:
            quant_ops.append(_stages[k][0])
    qi = 0

    def run_quant(n):
        nonlocal qi, t_ptr_free, t_pav_free, t_qtmp_free
        for _ in range(n):
            if qi < len(quant_ops):
                qstate["ptr_free"] = t_ptr_free
                qstate["pav_free"] = t_pav_free
                qstate["qtmp_free"] = t_qtmp_free
                quant_ops[qi]()
                t_ptr_free = qstate["ptr_free"]
                t_pav_free = qstate["pav_free"]
                t_qtmp_free = qstate["qtmp_free"]
                qi += 1

    t_xnT = [None] * NT
    t_xnst = [None] * NT
    t_kv_done = [None] * NT
    t_qT = [None] * NT
    t_outun = [[None] * ND for _ in range(NT)]

    stat_free = [None, None]
    xnT_free = [None, None]
    mu2_free = [None, None]
    mu2row_tok = [None, None]
    xnbf_free = None
    qchain_free = None
    kchain_free = None
    blk_tail = {}

    def stage0_steps(i):
        par = i % 2
        xb = xbufA[:, par, :]
        sl = stat[:, par, :]
        st_ = {}

        def s_stats():
            nonlocal t_qtmp_free
            st_["mu"] = prog.add("vector", lambda v: v.reduce_sum(
                sl[:, 0:1], xb, axis=AX.X), [x_tok[i], stat_free[par]])
            t_sq = prog.add("vector", lambda v: v.scalar_tensor_tensor(
                qtmp[:, 0:D], xb, 1.0, xb, ALU.mult, ALU.mult,
                accum_out=sl[:, 1:2]), [x_tok[i], stat_free[par], t_qtmp_free])
            t_qtmp_free = t_sq
            t_var = prog.add("vector", lambda v: v.scalar_tensor_tensor(
                sl[:, 2:3], sl[:, 0:1], 1.0 / D, sl[:, 0:1], ALU.mult, ALU.mult),
                [st_["mu"]])
            t_var2 = prog.add("vector", lambda v: v.scalar_tensor_tensor(
                sl[:, 3:4], sl[:, 1:2], 1.0, sl[:, 2:3], ALU.bypass,
                ALU.subtract), [t_sq, t_var])
            st_["var2"] = t_var2

        def s_ln():
            t_ln = prog.add("scalar", lambda s: s.activation(
                sl[:, 4:5], sl[:, 3:4], AF.Ln, scale=1.0 / D,
                bias=epsc[:, 0:1]), [st_["var2"], t_eps])
            t_rsig = prog.add("scalar", lambda s: s.activation(
                sl[:, 3:4], sl[:, 4:5], AF.Exp, scale=-0.5), [t_ln])
            st_["nmr"] = prog.add("vector", lambda v: v.scalar_tensor_tensor(
                sl[:, 5:6], sl[:, 0:1], -1.0 / D, sl[:, 3:4], ALU.mult,
                ALU.mult), [t_rsig, st_["mu"]])

        def s_xnf():
            nonlocal xnbf_free
            st_["xnf"] = prog.add("vector", lambda v: v.tensor_scalar(
                xnbf[:], xb, sl[:, 3:4], sl[:, 5:6], ALU.mult, ALU.add),
                [st_["nmr"], xnbf_free])
            st_["mu2"] = prog.add("vector", lambda v: v.reduce_sum(
                sl[:, 6:7], xnbf[:], axis=AX.X), [st_["xnf"]])
            st_["xnbf"] = st_["xnf"]
            stat_free[par] = [st_["xnf"], st_["mu2"]]

        def s_mu2xn():
            nonlocal t_ptr_free, xnbf_free
            t_mu2b = prog.add("vector", lambda v: v.tensor_copy(
                mu2b[:, par:par + 1], sl[:, 6:7]), [st_["mu2"], mu2_free[par]])
            t_mu2t = prog.add("tensor", lambda t: t.transpose(
                ptrb[0:1, 768:896], mu2b[:, par:par + 1], ident[:]),
                [t_mu2b, t_ptr_free])
            st_["mu2r"] = prog.add("vector", lambda v: v.tensor_copy(
                mu2row[:, par, :], ptrb[0:1, 768:896]), [t_mu2t])
            mu2row_tok[par] = st_["mu2r"]
            t_ptr_free = st_["mu2r"]
            tdt = prog.dma("sync", lambda y: y.dma_start_transpose(
                xnT[:, par, :, :], xnbf[:, 0:D]),
                [st_["xnbf"], xnT_free[par]])
            t_xnT[i] = tdt
            tst = prog.dma("gpsimd", lambda g: g.dma_start(
                out=xnt_rw[:, :, i * 128:(i + 1) * 128], in_=xnT[:, par, :, :]),
                [tdt])
            st_["xnst"] = tst
            t_xnst[i] = tst
            xnbf_free = [tdt, st_["mu2"]]
            if i + 2 < NT:
                xload_A(i + 2, [st_["xnf"], t_qtmp_free])

        def s_kv():
            nonlocal ring_ctr, t_ptr_free, kchain_free, t_qtmp_free
            slot = ring_ctr % NRS
            ring_ctr += 1
            t_kvmm = None
            for c in range(ND):
                t_kvmm = prog.add("tensor", lambda t, c=c, slot=slot, par=par:
                                  t.matmul(
                    ring_ap2(slot, 0, 128, 0, 128), Wkv_sb[:, c, :],
                    xnT[:, par, c, :],
                    start=(c == 0), stop=(c == ND - 1)),
                    [t_wkv, t_xnT[i], ring_tok[slot], t_kvmm])
            t_kvev = prog.add("vector", lambda v, slot=slot: v.tensor_copy(
                kvT_sb[:], ring_ap2(slot, 0, 128, 0, 128)), [t_kvmm, kchain_free])
            ring_tok[slot] = t_kvev
            t_ktok = prog.dma("sync", lambda y: y.dma_start_transpose(
                ktok[:], kvT_sb[:]), [t_kvev, kchain_free])
            t_kss = prog.add("vector", lambda v: v.scalar_tensor_tensor(
                qtmp[:, 0:64], ktok[:, 0:64], 1.0, ktok[:, 0:64], ALU.mult,
                ALU.mult, accum_out=kss[:, 0:1]), [t_ktok, t_vnull, t_qtmp_free])
            t_qtmp_free = t_kss
            st_["kss"] = t_kss
            st_["ktok"] = t_ktok

        def s_kv2():
            nonlocal kchain_free
            t_kss = st_["kss"]
            t_ktok = st_["ktok"]
            t_klog = prog.add("scalar", lambda s: s.activation(
                kss[:, 1:2], kss[:, 0:1], AF.Ln), [t_kss])
            t_krs = prog.add("scalar", lambda s: s.activation(
                krs[:, 0:1], kss[:, 1:2], AF.Exp, scale=-0.5), [t_klog])
            t_khat = prog.add("vector", lambda v: v.tensor_scalar(
                ktok[:, 0:64], ktok[:, 0:64], krs[:, 0:1], None, ALU.mult),
                [t_krs])
            t_vcp = prog.add("vector", lambda v: v.tensor_copy(
                v_sb[:, i, :], ktok[:, 64:128]), [t_ktok])
            t_kdup = prog.add("vector", lambda v: v.tensor_copy(
                ktok[:, 64:128], ktok[:, 0:64]), [t_khat, t_vcp])
            t_kt2 = prog.dma("sync", lambda y: y.dma_start_transpose(
                ktr2[:], ktok[:]), [t_kdup, kchain_free])
            t_khsT = prog.add("vector", lambda v: v.tensor_scalar(
                khsT[:, i * 128:(i + 1) * 128], ktr2[:], s8col[:], None,
                ALU.mult), [t_kt2, t_s8])
            kchain_free = [t_khat, t_vcp, t_khsT, t_kss]
            t_kv_done[i] = [t_khsT, t_vcp, t_nkhsT]

        def s_qproj():
            nonlocal ring_ctr
            _ = qchain_free
            st_["qslots"] = []
            st_["qmm"] = []
            for hh in range(NH):
                w = min(512, D - hh * 512)
                slot = ring_ctr % NRS
                ring_ctr += 1
                st_["qslots"].append((slot, hh, w))
                t_qmm = None
                for c in range(ND):
                    t_qmm = prog.add("tensor", lambda t, c=c, hh=hh, w=w, slot=slot,
                                     par=par: t.matmul(
                        ring_ap2(slot, 0, 128, 0, w),
                        xnT[:, par, c, :],
                        Wq_sb[:, c, hh * 512:hh * 512 + w],
                        start=(c == 0), stop=False),
                        [t_wq, t_xnT[i], ring_tok[slot], t_qmm])
                t_qmm = prog.add("tensor", lambda t, hh=hh, w=w, slot=slot: t.matmul(
                    ring_ap2(slot, 0, 128, 0, w), mu2row[:, par, :],
                    negc[:, hh * 512:hh * 512 + w], start=False, stop=True),
                    [t_negc, st_["mu2r"], t_qmm])
                st_["qmm"].append(t_qmm)
                tq = prog.add("vector", lambda v, slot=slot, hh=hh, w=w: v.tensor_copy(
                    qraw[:, hh * 512:hh * 512 + w], ring_ap2(slot, 0, 128, 0, w)),
                    [t_qmm, qchain_free])
                ring_tok[slot] = tq
                st_.setdefault("qraw", []).append(tq)
            mu2_free[par] = list(st_["qmm"])
            xnT_free[par] = [st_["qmm"], st_["xnst"]]

        def s_qss():
            nonlocal t_qtmp_free
            t_qraw = st_["qraw"]
            t_qssl = []
            for h in range(H):
                tq = prog.add("vector", lambda v, h=h: v.scalar_tensor_tensor(
                    qtmp[:, h * 64:(h + 1) * 64], qraw[:, h * 64:(h + 1) * 64],
                    1.0, qraw[:, h * 64:(h + 1) * 64], ALU.mult, ALU.mult,
                    accum_out=qss[:, h:h + 1]), [t_qraw, t_qtmp_free])
                t_qssl.append(tq)
            t_qtmp_free = t_qssl
            st_["qssl"] = t_qssl

        def s_qinv():
            t_qlog = prog.add("scalar", lambda s: s.activation(
                qlog[:], qss[:], AF.Ln), st_["qssl"])
            t_invq = prog.add("scalar", lambda s: s.activation(
                invq[:], qlog[:], AF.Exp, scale=-0.5), [t_qlog])
            st_["qhat"] = []
            for h in range(H):
                st_["qhat"].append(prog.add("vector", lambda v, h=h: v.tensor_scalar(
                    qhat[:, h * 64:(h + 1) * 64], qraw[:, h * 64:(h + 1) * 64],
                    invq[:, h:h + 1], None, ALU.mult), [t_invq]))

        def s_qT():
            nonlocal qchain_free
            bp = (i // BQ) % 3
            ic = i % BQ
            tdt = prog.dma("sync", lambda y, bp=bp, ic=ic: y.dma_start_transpose(
                qhatT[:, bp, :, ic * 128:(ic + 1) * 128], qhat[:, 0:D]),
                st_["qhat"] + [qchain_free, blk_tail.get(i // BQ - 3)])
            t_qT[i] = tdt
            qchain_free = [tdt] + st_["qhat"]

        return [s_stats, s_ln, s_xnf, s_mu2xn, s_kv, s_kv2, s_qproj,
                s_qss, s_qinv, s_qT]

    NRS = 5
    ring_tok = [None] * NRS
    ring_tok[4] = t_negc
    ring_ctr = 0

    def ring_ap(s, w=512):
        if s < 4:
            return psim[:, s * 512:s * 512 + w]
        return pwork[:, (s - 4) * 512:(s - 4) * 512 + w]

    def ring_ap2(s, p0, p1, c0, w):
        if s < 4:
            return psim[p0:p1, s * 512 + c0:s * 512 + c0 + w]
        return pwork[p0:p1, (s - 4) * 512 + c0:(s - 4) * 512 + c0 + w]

    attnT_free = [None] * ATS
    attnTn_free = [None] * ATS
    den_chain = [t_negc]
    pairden_free = None
    t_dinvst = {}

    def head_meta(B4, h):
        return dict(B4=B4, h=h, par=h % 2, ts=(B4 * H + h) % ATS,
                    po=(h % 2) * 64,
                    cc=h // 2, bp=B4 % 3, base=B4 * BQ, NC=B4 * BQ + BQ,
                    t_exps=[], t_nes=[], t_av=None, t_dn=None)

    def emit_score_chunk(hd, c):
        nonlocal ring_ctr
        B4, ts_, cc, po, bp, base = (hd["B4"], hd["ts"], hd["cc"], hd["po"],
                                     hd["bp"], hd["base"])
        j0 = max(0, c - base)
        w = BW - j0 * 128
        slot = ring_ctr % NRS
        ring_ctr += 1
        diag = c >= base
        t = prog.add("tensor", lambda tt, c=c, j0=j0, w=w, slot=slot, bp=bp,
                     cc=cc, po=po: tt.matmul(
            ring_ap2(slot, 0, 128, 0, w),
            khsT[po:po + 64, c * 128:(c + 1) * 128],
            qhatT[po:po + 64, bp, cc, j0 * 128:j0 * 128 + w],
            start=True, stop=not diag),
            [[t_qT[base + ib] for ib in range(j0, BQ)], t_kv_done[c],
             ring_tok[slot]])
        if diag:
            msk = maskCT if (c == 0 and B4 == 0) else maskAT
            t = prog.add("tensor", lambda tt, slot=slot, msk=msk: tt.matmul(
                ring_ap2(slot, 0, 128, 0, 128), ident[:], msk[:],
                start=False, stop=True), [t])
        te = prog.add("scalar", lambda s, c=c, j0=j0, w=w, slot=slot,
                      ts_=ts_: s.activation(
            attnT[:, ts_, c, j0 * 128:j0 * 128 + w],
            ring_ap2(slot, 0, 128, 0, w),
            AF.Exp, bias=negC[:, 0:1]),
            [t, t_negC, attnT_free[ts_]])
        ring_tok[slot] = te
        hd["t_exps"].append(te)

    def emit_nulls(hd):
        nonlocal ring_ctr
        ts_, cc, po, bp, base = (hd["ts"], hd["cc"], hd["po"], hd["bp"],
                                 hd["base"])
        slot = ring_ctr % NRS
        ring_ctr += 1
        t_ns = ring_tok[slot]
        for ib in range(BQ):
            lqn = qhatT[po:po + 64, bp, cc, ib * 128:(ib + 1) * 128]
            t_ns = prog.add("tensor", lambda tt, slot=slot, lqn=lqn, po=po,
                            ib=ib: tt.matmul(
                ring_ap2(slot, 0, 2, ib * 128, 128), nkhsT[po:po + 64, :],
                lqn, start=True, stop=True),
                [t_qT[base + ib], t_kv_done[base + ib], t_ns])
        t_ne = prog.add("scalar", lambda s, slot=slot, ts_=ts_:
                        s.activation(
            attnTn[:, ts_, :], ring_ap2(slot, 0, 2, 0, BW), AF.Exp,
            bias=negC[0:2, 0:1]), [t_ns, t_negC, attnTn_free[ts_]])
        ring_tok[slot] = t_ne
        hd["t_nes"].append(t_ne)

    def emit_den_chunk(hd, c):
        p, ts_, base = hd["par"], hd["ts"], hd["base"]
        j0 = max(0, c - base)
        w = BW - j0 * 128
        hd["t_dn"] = prog.add("tensor", lambda tt, c=c, j0=j0, w=w, p=p,
                              ts_=ts_: tt.matmul(
            pwork[0:2, 512 + j0 * 128:512 + j0 * 128 + w],
            dind[:, 2 * p:2 * p + 2],
            attnT[:, ts_, c, j0 * 128:j0 * 128 + w],
            start=(c == 0 and p == 0), stop=False),
            [hd["t_exps"][c], den_chain, hd["t_dn"]])

    def emit_den_null(hd):
        nonlocal den_chain, pairden_free
        B4, ts_, cc, p = hd["B4"], hd["ts"], hd["cc"], hd["par"]
        t = prog.add("tensor", lambda tt, p=p, ts_=ts_: tt.matmul(
            pwork[0:2, 512:512 + BW], nind[0:2, 2 * p:2 * p + 2],
            attnTn[0:2, ts_, :], start=False, stop=(p == 1)),
            [hd["t_nes"], den_chain, hd["t_dn"]])
        if p == 1:
            t_dln = prog.add("scalar", lambda s: s.activation(
                pairden[:, 1, 0:BW], pwork[0:2, 512:512 + BW], AF.Ln),
                [t, pairden_free])
            den_chain = [t_dln]
            t_rec = prog.add("scalar", lambda s: s.activation(
                pairden[:, 0, 0:BW], pairden[:, 1, 0:BW], AF.Exp,
                scale=-1.0), [t_dln])
            t_st = prog.dma("gpsimd", lambda g, cc=cc, B4=B4: g.dma_start(
                out=dinv[:, cc, B4 * BW:(B4 + 1) * BW],
                in_=pairden[:, 0, 0:BW]),
                [t_rec])
            pairden_free = [t_st, t_rec]
            t_dinvst[(cc, B4)] = t_st

    def emit_av_chunk(hd, c):
        B4, po, ts_ = hd["B4"], hd["po"], hd["ts"]
        j0 = max(0, c - BQ * B4)
        w = (BQ - j0) * 128
        hd["t_av"] = prog.add("tensor", lambda t, c=c, po=po, ts_=ts_, j0=j0,
                              w=w: t.matmul(
            pav[po:po + 64, j0 * 128:j0 * 128 + w],
            v_sb[:, c, :],
            attnT[:, ts_, c, j0 * 128:j0 * 128 + w],
            start=(c == 0), stop=False,
            tile_position=(0, po)),
            [hd["t_exps"][c], t_pav_free, hd["t_av"]])

    def emit_av_null(hd):
        nonlocal t_pav_free
        h, po, ts_, B4 = hd["h"], hd["po"], hd["ts"], hd["B4"]
        t_av = prog.add("tensor", lambda t, po=po, ts_=ts_: t.matmul(
            pav[po:po + 64, 0:BW], vnull[:], attnTn[:, ts_, :], start=False,
            stop=True, tile_position=(0, po)), [hd["t_nes"], hd["t_av"],
                                                t_vnull])
        attnT_free[ts_] = t_av
        attnTn_free[ts_] = t_av
        if h % 2 == 1:
            t_pav_free = prog.add("vector", lambda v, B4=B4, cc=h // 2: v.tensor_copy(
                outunT[:, cc, B4 * BW:(B4 + 1) * BW], pav[:, 0:BW]),
                [t_av])
            for ib in range(BQ):
                t_outun[B4 * BQ + ib][h // 2] = t_pav_free
        return t_av

    for i0 in range(min(2 * BQ, NT)):
        for s in stage0_steps(i0):
            s()
    NBK = NT // BQ
    qper = max(1, (len(quant_ops) + NBK * H - 1) // (NBK * H))
    prev_hd = None
    for B4 in range(NBK):
        steps_next = []
        if (B4 + 2) * BQ < NT:
            for i2 in range((B4 + 2) * BQ, (B4 + 3) * BQ):
                steps_next += stage0_steps(i2)
        si = 0
        for h in range(H):
            hd = head_meta(B4, h)
            pNC = prev_hd["NC"] if prev_hd is not None else 0
            for c in range(hd["NC"]):
                emit_score_chunk(hd, c)
                if c < pNC:
                    emit_av_chunk(prev_hd, c)
                run_quant(1)
            emit_nulls(hd)
            if prev_hd is not None:
                for c in range(pNC):
                    emit_den_chunk(prev_hd, c)
                emit_den_null(prev_hd)
                tavp = emit_av_null(prev_hd)
                if prev_hd["h"] == H - 1:
                    blk_tail[prev_hd["B4"]] = tavp
            prev_hd = hd
            for _ in range(3):
                if si < len(steps_next):
                    steps_next[si]()
                    si += 1

        while si < len(steps_next):
            steps_next[si]()
            si += 1
    for c in range(prev_hd["NC"]):
        emit_av_chunk(prev_hd, c)
    for c in range(prev_hd["NC"]):
        emit_den_chunk(prev_hd, c)
    emit_den_null(prev_hd)
    blk_tail[prev_hd["B4"]] = emit_av_null(prev_hd)
    run_quant(len(quant_ops))

    t_w1qt = qstate["w1qt_done"]
    t_w2qt = qstate["w2qt_done"]

    ctxA.close()
    prog.fence()

    ctxB = contextlib.ExitStack()
    SB_ = ctxB.enter_context

    def sbb(name, shape, dtype):
        return SB_(nc.sbuf_tensor(name, shape, dtype))

    NB2 = max(1, (NF + 3) // 4)
    NSG = NT // 4 if NT % 4 == 0 else NT // 2
    QSG = NT // NSG
    TW = QSG * 128

    Wout_sb = sbb("Wout_sb", [128, ND, D], BF16)
    W1B = sbb("W1B", [128, ND, F], BF16)
    w2ring = sbb("w2ring", [128, 8, 512], BF16)
    xnTg = sbb("xnTg", [128, 2, ND, TW], BF16)
    h1full = sbb("h1full", [128, NF, TW], BF16)
    y_sb = sbb("y_sb", [128, 2, 512], F32)
    xbufB = sbb("xbufB", [128, 2, 512], F32)
    invdenT = sbb("invdenT", [2, 2, ND, 128], F32)

    t_wout = prog.dma("gpsimd", lambda g: g.dma_start(out=Wout_sb[:], in_=Woutr))
    t_w1b = []
    for c4 in range(0, ND, 2):
        nn = min(2, ND - c4)
        t_w1b.append(prog.dma("scalar" if (c4 // 2) % 2 == 0 else "sync",
                              lambda g, c4=c4, nn=nn: g.dma_start(
            out=W1B[:, c4:c4 + nn, :],
            in_=w1qt_rw[:, c4:c4 + nn, :]), [t_w1qt]))

    t_y_done = [None] * NT * 2
    w2_batch_done = {}
    t_xb_free = [None, None]
    t_ysb_free = [None, None]
    t_xnTg_free = [None, None]
    t_ivT_readers = {0: [], 1: []}
    t_psimA = [ring_tok[0], ring_tok[1], ring_tok[2], ring_tok[3]]
    PM1 = [psim[:, 0:512], psim[:, 512:1024]]
    t_pm1_free = [[t_psimA], [t_psimA]]
    PACC = [psim[:, 1024:1536], psim[:, 1536:2048], pwork[:, 0:512],
            pwork[:, 512:1024]]
    t_pacc_free = [[t_psimA], [t_psimA], [ring_tok[4]], [den_chain]]
    t_bc_free = {}
    bank_chain = {}
    t_h1cons = {}
    t_h1_free = [None] * NF
    t_xg = [None, None]
    CCX_ORDER = [c for pr in zip(range(ND // 2), range(ND // 2, ND))
                 for c in pr] or list(range(ND))

    def den_steps(sg):
        spp = sg % 2
        jlist_ = list(range(QSG * sg, QSG * sg + QSG))
        ds = {}
        steps = []
        for jj, j in enumerate(jlist_):
            def s_load(j=j, spp=spp):
                ds["ivl"] = prog.dma("gpsimd", lambda g, j=j, spp=spp:
                                     g.dma_start(
                    out=invdenT[:, spp, :, :],
                    in_=dinv[:, :, j * 128:(j + 1) * 128]),
                    [[t_dinvst[(cc2, j // BQ)] for cc2 in range(ND)],
                     t_ivT_readers[spp]])
                t_ivT_readers[spp] = []
            steps.append(s_load)
            for ccx in CCX_ORDER:
                def s_bc(ccx=ccx, j=j, spp=spp):
                    pb = psim[:, ccx * 128:(ccx + 1) * 128]
                    bank = ccx // 4
                    t_bc = prog.add("tensor", lambda t, pb=pb, ccx=ccx, spp=spp:
                                    t.matmul(
                        pb, indic[:], invdenT[:, spp, ccx, :],
                        start=True, stop=True),
                        [ds["ivl"], bank_chain.get(bank), t_pm1_free[bank]])
                    t_ivT_readers[spp].append(t_bc)
                    t_oh = prog.add("vector", lambda v, pb=pb, ccx=ccx, j=j:
                                    v.tensor_tensor(
                        outunT[:, ccx, j * 128:(j + 1) * 128],
                        outunT[:, ccx, j * 128:(j + 1) * 128], pb, ALU.mult),
                        [t_bc, t_outun[j][ccx]])
                    t_bc_free[(spp, ccx)] = t_oh
                    bank_chain[bank] = t_oh
                steps.append(s_bc)

        def s_fin(spp=spp):
            t_pm1_free[0] = [t_pm1_free[0]] + [t_bc_free[(spp, c)]
                                               for c in range(0, min(4, ND))]
            t_pm1_free[1] = [t_pm1_free[1]] + [t_bc_free[(spp, c)]
                                               for c in range(4, ND)]
        steps.append(s_fin)
        return steps

    yi = 0
    for sg in range(NSG):
        sp = sg % 2
        jlist = list(range(QSG * sg, QSG * sg + QSG))
        if sg == 0:
            t_xg[0] = prog.dma("scalar", lambda g: g.dma_start(
                out=xnTg[:, 0, :, :], in_=xnt_rw[:, :, 0:TW]),
                [[t_xnst[j] for j in jlist]])
            for s_ in den_steps(0):
                s_()
        if sg + 1 < NSG:
            jn = list(range(QSG * (sg + 1), QSG * (sg + 1) + QSG))
            t_xg[1 - sp] = prog.dma("scalar", lambda g, sg=sg, sp=sp:
                                    g.dma_start(
                out=xnTg[:, 1 - sp, :, :],
                in_=xnt_rw[:, :, (sg + 1) * TW:(sg + 2) * TW]),
                [[t_xnst[j] for j in jn], t_xnTg_free[1 - sp]])
            dsq = den_steps(sg + 1)
        else:
            dsq = []
        dsi = 0
        for half in range(NH):
            hw_ = min(512, D - half * 512)
            t_w2tok = [None] * NF

            def issue_w2(b, half=half, sg=sg, t_w2tok=t_w2tok):
                nbc = min(4, NF - 4 * b)
                s0 = (4 * b) % 8
                gb = (sg * NH + half) * NB2 + b
                war = w2_batch_done.get(gb - 2)
                tk = prog.dma("scalar" if b % 2 == 0 else "sync",
                              lambda g, b=b, s0=s0, nbc=nbc, half=half:
                              g.dma_start(
                    out=w2ring[:, s0:s0 + nbc, 0:hw_],
                    in_=w2qt_rd[:, 4 * b:4 * b + nbc,
                                half * 512:half * 512 + hw_]),
                    [t_w2qt, war])
                for k in range(nbc):
                    t_w2tok[4 * b + k] = tk

            for b in range(min(2, NB2)):
                issue_w2(b)
            t_mmq = [None] * QSG
            for jj, j in enumerate(jlist):
                for ccx in range(ND):
                    t_mmq[jj] = prog.add("tensor", lambda t, ccx=ccx, j=j,
                                         jj=jj, half=half: t.matmul(
                        PACC[jj][:, 0:hw_],
                        outunT[:, ccx, j * 128:(j + 1) * 128],
                        Wout_sb[:, ccx, half * 512:half * 512 + hw_],
                        start=(ccx == 0), stop=False),
                        [t_bc_free[(sp, ccx)], t_wout, t_pacc_free[jj],
                         t_mmq[jj]])
                t_mmq[jj] = prog.add("tensor", lambda t, jj=jj, half=half: t.matmul(
                    PACC[jj][:, 0:hw_], ones_row_b[:],
                    b2_sb[:, half * 512:half * 512 + hw_], start=False,
                    stop=False), [t_b2, t_mmq[jj]])
            if half == 0:
                def emit_mm1(fc, sg=sg, sp=sp):
                    rot = fc % 2
                    t = None
                    for c in range(ND):
                        t = prog.add("tensor", lambda tt, c=c, fc=fc, rot=rot,
                                     sp=sp: tt.matmul(
                            PM1[rot][:, 0:TW], W1B[:, c, fc * 128:(fc + 1) * 128],
                            xnTg[:, sp, c, :],
                            start=(c == 0), stop=(c == ND - 1)),
                            [t_w1b[c // 2], t_xg[sp], t_pm1_free[rot], t])
                    return t

                t_m1 = [None] * NF
                t_m1[0] = emit_mm1(0)
                if NF > 1:
                    t_m1[1] = emit_mm1(1)
            for fc in range(NF):
                if half == 0:
                    rot = fc % 2
                    t_h1_free[fc] = prog.add("scalar", lambda s, fc=fc, rot=rot:
                                             s.activation(
                        h1full[:, fc, 0:TW], PM1[rot][:, 0:TW], AF.Relu,
                        bias=b1_sb[:, fc:fc + 1]),
                        [t_m1[fc], t_b1, t_h1cons.get(fc)])
                    t_pm1_free[rot] = t_h1_free[fc]
                    if fc + 2 < NF:
                        t_m1[fc + 2] = emit_mm1(fc + 2)
                for jj in range(QSG):
                    t_mmq[jj] = prog.add("tensor", lambda t, fc=fc, jj=jj,
                                         half=half: t.matmul(
                        PACC[jj][:, 0:hw_],
                        h1full[:, fc, jj * 128:(jj + 1) * 128],
                        w2ring[:, fc % 8, 0:hw_],
                        start=False, stop=(fc == NF - 1)),
                        [t_h1_free[fc], t_w2tok[fc], t_mmq[jj]])
                if fc % 4 == 3:
                    gb = (sg * NH + half) * NB2 + fc // 4
                    w2_batch_done[gb] = [t_mmq[jj] for jj in range(QSG)]
                    if fc // 4 + 2 < NB2:
                        issue_w2(fc // 4 + 2)
                if half == NH - 1:
                    t_h1cons[fc] = [t_mmq[jj] for jj in range(QSG)]
                    if NH >= 2:
                        for _ in range(2):
                            if dsi < len(dsq):
                                dsq[dsi]()
                                dsi += 1
            for jj, j in enumerate(jlist):
                yp = yi % 2
                yi += 1
                t_xb = prog.dma("gpsimd", lambda g, j=j, half=half, yp=yp:
                                g.dma_start(
                    out=xbufB[:, yp, 0:hw_],
                    in_=xr[j][:, half * 512:half * 512 + hw_]), [t_xb_free[yp]])
                t_y = prog.add("vector", lambda v, jj=jj, yp=yp, half=half:
                               v.scalar_tensor_tensor(
                    y_sb[:, yp, 0:hw_], PACC[jj][:, 0:hw_], 1.0,
                    xbufB[:, yp, 0:hw_], ALU.mult, ALU.add),
                    [t_mmq[jj], t_xb, t_ysb_free[yp]])
                t_pacc_free[jj] = t_y
                t_xb_free[yp] = t_y
                td = prog.dma("gpsimd" if yp == 0 else "sync",
                              lambda g, j=j, half=half, yp=yp:
                              g.dma_start(
                    out=yr[j][:, half * 512:half * 512 + hw_],
                    in_=y_sb[:, yp, 0:hw_]), [t_y])
                t_ysb_free[yp] = td
                t_y_done[2 * j + half] = td
        while dsi < len(dsq):
            dsq[dsi]()
            dsi += 1
        t_xnTg_free[sp] = [t_m1[fc] for fc in range(NF)]

    prog.add("gpsimd", None, t_y_done)

    with nc.Block() as block:
        prog.emit(nc, block, stack)
    ctxB.close()
    stack.close()
    return nc



_CACHE = {}


def _get_prog(cfg_key):
    if cfg_key not in _CACHE:
        _CACHE[cfg_key] = build_program(FULL_CFG)
    return _CACHE[cfg_key]


def kernel(L0_attn_x, L0_attn_gamma, L0_attn_Wq, L0_attn_Wkv, L0_attn_q_scale,
           L0_attn_k_scale, L0_attn_null_kv, L0_attn_Wout, L0_ffn_W1, L0_ffn_b1,
           L0_ffn_W2, L0_ffn_b2):
    x = np.ascontiguousarray(np.asarray(L0_attn_x, dtype=np.float32))
    B = x.shape[0]
    nc = _get_prog("full")
    consts = make_consts(FULL_CFG)
    shared = {
        "gamma": np.ascontiguousarray(L0_attn_gamma, dtype=np.float32),
        "Wq": np.ascontiguousarray(L0_attn_Wq, dtype=np.float32),
        "Wkv": np.ascontiguousarray(L0_attn_Wkv, dtype=np.float32),
        "q_scale": np.ascontiguousarray(L0_attn_q_scale, dtype=np.float32),
        "k_scale": np.ascontiguousarray(L0_attn_k_scale, dtype=np.float32),
        "null_kv": np.ascontiguousarray(L0_attn_null_kv, dtype=np.float32),
        "Wout": np.ascontiguousarray(L0_attn_Wout, dtype=np.float32),
        "W1": np.ascontiguousarray(L0_ffn_W1, dtype=np.float32),
        "b1": np.ascontiguousarray(L0_ffn_b1, dtype=np.float32),
        "W2": np.ascontiguousarray(L0_ffn_W2, dtype=np.float32),
        "b2": np.ascontiguousarray(L0_ffn_b2, dtype=np.float32),
    }
    shared.update(consts)
    in_maps = [dict(shared, x=np.ascontiguousarray(x[b])) for b in range(B)]
    res = run_bass_kernel_spmd(nc, in_maps, core_ids=list(range(B)))
    return np.stack([r["y"] for r in res.results], axis=0)
